# version 13
# baseline (speedup 1.0000x reference)
import contextlib
import numpy as np
import ml_dtypes
import concourse.bass as bass
import concourse.mybir as mybir
from concourse.bass_utils import run_bass_kernel_spmd

F32 = mybir.dt.float32
BF16 = mybir.dt.bfloat16
AF = mybir.ActivationFunctionType
ALU = mybir.AluOpType
AX = mybir.AxisListType

ENGINES = ("tensor", "vector", "scalar", "gpsimd", "sync")
_DBG = False

D = 1024
H = 16
HD = 64
NF = 64
DFF = 4096
NCORE = 8


class Prog:
    def __init__(self, nc: bass.Bass):
        self.nc = nc
        self.sem = {}
        self.cnt = {}
        for e in ENGINES:
            self.sem[e] = nc.alloc_semaphore(name=f"es_{e}")
            self.cnt[e] = 0
        self._cc_toks = []
        self._ccidx = 0
        self._dma_sems = {}
        self._bar = None
        self._bar_seen = set()

    def eng(self, name):
        return getattr(self.nc, name)

    def barrier(self):
        toks = [(self.sem[e], self.cnt[e]) for e in ENGINES]
        toks += [(rec[0], rec[1]) for rec in self._dma_sems.values()]
        toks += self._cc_toks
        self._bar = [t for t in toks if t[1] > 0]
        self._bar_seen = set()

    def _waits(self, engine, waits):
        if self._bar is not None and engine not in self._bar_seen:
            self._bar_seen.add(engine)
            waits = list(waits) + self._bar
        best = {}
        for t in waits:
            if t is None:
                continue
            s, v = t
            k = id(s)
            if k not in best or best[k][1] < v:
                best[k] = (s, v)
        e = self.eng(engine)
        for (s, v) in best.values():
            e.wait_ge(s, v)

    def op(self, engine, method, *args, waits=(), **kwargs):
        self._waits(engine, waits)
        inst = getattr(self.eng(engine), method)(*args, **kwargs)
        self.cnt[engine] += 1
        inst.then_inc(self.sem[engine], 1)
        return (self.sem[engine], self.cnt[engine])

    def op_nosem(self, engine, method, *args, waits=(), **kwargs):
        self._waits(engine, waits)
        getattr(self.eng(engine), method)(*args, **kwargs)
        return None

    def dma_sem(self, name):
        if name not in self._dma_sems:
            self._dma_sems[name] = [self.nc.alloc_semaphore(name=f"dma_{name}"), 0]
        return name

    def dma(self, engine, out, in_, sem, waits=(), **kwargs):
        self._waits(engine, waits)
        inst = self.eng(engine).dma_start(out=out, in_=in_, **kwargs)
        rec = self._dma_sems[sem]
        rec[1] += 16
        inst.then_inc(rec[0], 16)
        return (rec[0], rec[1])

    def dma_all(self, sem):
        rec = self._dma_sems[sem]
        return (rec[0], rec[1])

    def collective(self, kind, alu, groups, ins, outs, waits=()):
        self._waits("gpsimd", waits)
        inst = self.nc.gpsimd.collective_compute(kind, alu, replica_groups=groups,
                                                 ins=ins, outs=outs)
        sem = self.nc.alloc_semaphore(name=f"cc{self._ccidx}")
        self._ccidx += 1
        inst.then_inc(sem)
        self._cc_toks.append((sem, 1))
        return (sem, 1)

    def tok(self, engine):
        return (self.sem[engine], self.cnt[engine])


class StageCtx:

    def __init__(self, nc, tag):
        self.nc, self.tag = nc, tag
        self.stack = contextlib.ExitStack()

    def sb(self, name, shape, dtype=F32):
        return self.stack.enter_context(
            self.nc.sbuf_tensor(self.tag + name, shape, dtype))

    def ps(self, name, shape, dtype=F32):
        return self.stack.enter_context(
            self.nc.psum_tensor(self.tag + name, shape, dtype))

    def close(self):
        self.stack.close()


class TernBG:

    def __init__(self, p, Csh, g):
        self.p = p
        nc = p.nc
        self.wld = [Csh.sb("t_wldA", [128, 2048], F32),
                    Csh.sb("t_wldB", [128, 2048], F32)]
        self.sgn = Csh.sb("t_sgn", [128, 2048], F32)
        self.wt = [Csh.sb(f"t_wt{i}", [128, 2048], BF16) for i in range(2)]
        self.rsum = Csh.sb("t_rsum", [128, 64], F32)
        self.thr = Csh.sb("t_thr", [128, 2], F32)
        self.onesf = Csh.sb("t_onesf", [128, 1], F32)
        self.zc = Csh.sb("t_zc", [128, 1], F32)
        self.thr_d = nc.dram_tensor("thr_d", [1, 2], F32, kind="Internal")
        self.t_zc = p.op("vector", "memset", self.zc[:], 0.0)
        self.t_of = p.op("vector", "memset", self.onesf[:], 1.0)
        self.specs = []
        for c in range(8):
            for su in range(2):
                self.specs.append((g["wupT"], c, su, 2048, c * 2 + su, 0, g["wternU"]))
        for c in range(32):
            self.specs.append((g["wdnT"], c, 0, 1024, 20 + c, 1, g["wternD"]))
        self.s_l1 = [p.dma_sem("tbg1a"), p.dma_sem("tbg1b")]
        self.s_l2 = [p.dma_sem("tbg2a"), p.dma_sem("tbg2b")]
        self.s_o = [p.dma_sem("tbgoa"), p.dma_sem("tbgob")]
        self.i1 = 0
        self.i2 = 0
        self.p1_free = [None, None]
        self.p2_free = [None, None]
        self.wt_free = [None, None]
        self.acc_last = None
        self.sel_last = None
        self.t_thr = [None, None]

    def p1(self, k):
        p = self.p
        for _ in range(k):
            if self.i1 >= len(self.specs):
                return
            i = self.i1
            self.i1 += 1
            wT, c, su, SW, rcol, tc, dst = self.specs[i]
            b = i % 2
            buf = self.wld[b]
            w = [self.p1_free[b]] if self.p1_free[b] else []
            t_l = p.dma("sync" if b == 0 else "gpsimd", buf[:, 0:SW],
                        wT.ap()[c * 128:(c + 1) * 128, su * SW:(su + 1) * SW],
                        sem=self.s_l1[b], waits=w)
            t_a = p.op("scalar", "activation", out=buf[:, 0:SW], in_=buf[:, 0:SW],
                       func=AF.Abs, bias=self.zc[:, 0:1], scale=1.0,
                       waits=[t_l, self.t_zc])
            t_r = p.op("vector", "reduce_sum", out=self.rsum[:, rcol:rcol + 1],
                       in_=buf[:, 0:SW], axis=AX.X, waits=[t_a])
            self.p1_free[b] = t_r
            self.acc_last = t_r

    def emit_threshold(self, C):
        p = self.p
        pb_a = C.ps("t_pb_a", [1, 2], F32)
        s_t = p.dma_sem("tbgt")
        for base, nsums, wd, nch, tc in ((0, 16, DFF, 8, 0), (20, 32, 1024, 32, 1)):
            t_r = p.op("vector", "reduce_sum",
                       out=self.rsum[:, base + nsums:base + nsums + 1],
                       in_=self.rsum[:, base:base + nsums], axis=AX.X, waits=[])
            t_m = p.op("tensor", "matmul", out=pb_a[0:1, tc:tc + 1],
                       lhsT=self.onesf[:],
                       rhs=self.rsum[:, base + nsums:base + nsums + 1],
                       start=True, stop=True, waits=[t_r, self.t_of])
            t_t = p.op("vector", "tensor_scalar", out=self.thr[0:1, tc:tc + 1],
                       in0=pb_a[0:1, tc:tc + 1], scalar1=0.5 / (nch * 128 * wd),
                       scalar2=None, op0=ALU.mult, waits=[t_m])
            t_d1 = p.dma("sync", self.thr_d.ap()[0:1, tc:tc + 1],
                         self.thr[0:1, tc:tc + 1], sem=s_t, waits=[t_t])
            t_d2 = p.dma("sync", self.thr[:, tc:tc + 1],
                         bass.AP(tensor=self.thr_d, offset=tc, ap=[[0, 128], [1, 1]]),
                         sem=s_t, waits=[t_d1])
            self.t_thr[tc] = t_d2

    def p2(self, k):
        p = self.p
        for _ in range(k):
            if self.i2 >= len(self.specs):
                return
            i = self.i2
            self.i2 += 1
            wT, c, su, SW, rcol, tc, dst = self.specs[i]
            b = i % 2
            buf = self.wld[b]
            w = [self.p2_free[b]] if self.p2_free[b] else [self.acc_last]
            t_l = p.dma("sync" if b == 0 else "gpsimd", buf[:, 0:SW],
                        wT.ap()[c * 128:(c + 1) * 128, su * SW:(su + 1) * SW],
                        sem=self.s_l2[b], waits=w)
            t_s = p.op("scalar", "sign", out=self.sgn[:, 0:SW], in_=buf[:, 0:SW],
                       waits=[t_l, self.t_thr[tc]] +
                             ([self.sel_last] if self.sel_last else []))
            t_a = p.op("scalar", "activation", out=buf[:, 0:SW], in_=buf[:, 0:SW],
                       func=AF.Abs, bias=self.zc[:, 0:1], scale=1.0, waits=[t_s])
            wv = [t_a] + ([self.wt_free[b]] if self.wt_free[b] else [])
            t_w = p.op("vector", "scalar_tensor_tensor", out=self.wt[b][:, 0:SW],
                       in0=buf[:, 0:SW], scalar=self.thr[:, tc:tc + 1],
                       in1=self.sgn[:, 0:SW], op0=ALU.is_gt, op1=ALU.mult,
                       waits=wv)
            t_o = p.dma("sync" if b == 0 else "gpsimd",
                        dst.ap()[c * 128:(c + 1) * 128, su * SW:(su + 1) * SW],
                        self.wt[b][:, 0:SW], sem=self.s_o[b], waits=[t_w])
            self.p2_free[b] = t_w
            self.sel_last = t_w
            self.wt_free[b] = t_o


def bcast_ap(dram_handle, parts, free, offset=0, row_len=None):
    return bass.AP(tensor=dram_handle, offset=offset, ap=[[0, parts], [1, free]])


def ln_stats(p, xs, sb, pre, waits):
    t1 = p.op("vector", "bn_stats", out=sb["bnr"][:, 0:6], in_=xs[:, 0:512], waits=waits)
    t2 = p.op("vector", "bn_stats", out=sb["bnr"][:, 6:12], in_=xs[:, 512:1024], waits=[t1])
    t3 = p.op("vector", "bn_aggr", out=sb["agg"][:, 0:2], in_=sb["bnr"][:, 0:12], waits=[t2])
    t4 = p.op("scalar", "activation", out=sb["rs"][:], in_=sb["agg"][:, 1:2], func=AF.Ln,
              bias=sb["cst"][:, 0:1], scale=1.0, waits=[t3] + list(pre))
    t5 = p.op("scalar", "activation", out=sb["rs"][:], in_=sb["rs"][:], func=AF.Exp,
              bias=sb["cst"][:, 1:2], scale=-0.5, waits=[t4])
    t6 = p.op("vector", "scalar_tensor_tensor", out=sb["bi"][:], in0=sb["agg"][:, 0:1],
              scalar=-1.0, in1=sb["rs"][:], op0=ALU.mult, op1=ALU.mult, waits=[t3, t5])
    return t5, t6, t3



def st1_body(p, g, C, T, bg=None):
    nc = p.nc
    NT = T // 128
    GS = min(512, T)
    NG = T // GS
    BT = min(8, NT)

    x = g["xin"].ap()
    wqkvT = g["wqkvT"]
    bq_row = g["bq_row"].ap()
    bq_cols = g["bq_cols"].ap()
    pmT = g["pmT"]
    eye = g["eye"].ap()

    qef = g["qef"].ap()
    kvpo = g["kvpo"].ap()

    xs = [C.sb(f"xs{i}", [128, D], F32) for i in range(2)]
    lnb = C.sb("lnb", [128, D], BF16)
    ln0f = C.sb("ln0f", [128, 8 * T], BF16)
    wqb = C.sb("wqb", [128, 8 * 3 * D], BF16)
    bqr = C.sb("bqr", [1, 3 * D], BF16)
    bqc = C.sb("bqc", [128, 24], F32)
    pm2 = C.sb("pm2", [128, 64], BF16)
    pmf = C.sb("pmf", [128, 64], F32)
    eyes = C.sb("eyes", [128, 128], BF16)
    ones_r = C.sb("ones_r", [1, 128], BF16)
    vtok = C.sb("vtok", [128, NT * H * 65], BF16)
    qfp = C.sb("qfp", [128, T], BF16)
    kfp = C.sb("kfp", [128, T], BF16)
    qep = C.sb("qep", [128, T], BF16)
    kep = C.sb("kep", [128, T], BF16)
    qefp = C.sb("qefp", [128, T], BF16)
    kv_all = C.sb("kv_all", [128, 8 * 65], F32)
    sb = {
        "bnr": C.sb("bnr", [128, 12], F32),
        "agg": C.sb("agg", [128, 2], F32),
        "rs": C.sb("rs", [128, 1], F32),
        "bi": C.sb("bi", [128, 1], F32),
        "cst": C.sb("cst", [128, 4], F32),
    }
    mcol = C.sb("mcol", [128, 4], F32)

    pb_tp = C.ps("pb_tp", [128, 1024], BF16)
    pb_v = [C.ps(f"pb_v{i}", [128, 512], F32) for i in range(2)]
    pb_g = [C.ps(f"pb_g{i}", [128, 512], F32) for i in range(2)]
    pb_f = [C.ps(f"pb_f{i}", [128, 64], F32) for i in range(2)]
    pb_kv = C.ps("pb_kv", [128, 65], F32)

    s_wq = p.dma_sem("wq")
    for kc in range(8):
        p.dma("gpsimd", wqb[:, kc * 3072:(kc + 1) * 3072],
              wqkvT[kc * 128:(kc + 1) * 128, :], sem=s_wq)
    p.dma("gpsimd", bqr[:], bq_row, sem=s_wq)
    t_wq = p.dma_all(s_wq)

    s_ms = p.dma_sem("misc")
    p.dma("sync", bqc[:], bq_cols, sem=s_ms)
    p.dma("sync", eyes[:], eye, sem=s_ms)
    p.dma("sync", pmf[0:64, :], pmT.ap(), sem=s_ms)
    p.dma("sync", pmf[64:128, :], pmT.ap(), sem=s_ms)
    t_ms = p.dma_all(s_ms)

    t_c = p.op("vector", "memset", sb["cst"][:, 0:1], 1e-5)
    t_c2 = p.op("vector", "memset", sb["cst"][:, 1:2], 0.0)
    t_on = p.op("vector", "memset", ones_r[:], 1.0)
    ones_view = bass.AP(tensor=vtok, offset=64,
                        ap=[[NT * H * 65, 128], [65, NT * H], [1, 1]])
    t_ov = p.op("vector", "memset", ones_view, 1.0, waits=[])
    t_pm = p.op("scalar", "activation", out=pm2[:], in_=pmf[:], func=AF.Identity,
                bias=sb["cst"][:, 1:2], scale=float(HD ** -0.25), waits=[t_ms, t_c2])

    x_t = x.rearrange("(n p) d -> n p d", p=128)

    s_x = [p.dma_sem("x0"), p.dma_sem("x1")]
    x_free = [None, None]
    ln_done = [None] * NT
    v_done = [None] * NT
    tp_evac_prev = None
    vband_prev = [None, None]
    for t in range(NT):
        b = t % 2
        w = [x_free[b]] if x_free[b] else []
        t_x = p.dma("sync", xs[b][:], x_t[t], sem=s_x[b], waits=w)
        t_rs, t_bi, t_agg = ln_stats(p, xs[b], sb, [t_c, t_c2], [t_x, t_c, t_c2])
        w_apply = [t_rs, t_bi, t_x]
        if tp_evac_prev is not None:
            w_apply.append(tp_last)
        t_ap = p.op("scalar", "activation", out=lnb[:], in_=xs[b][:], func=AF.Identity,
                    bias=sb["bi"][:, 0:1], scale=sb["rs"][:, 0:1], waits=w_apply)
        x_free[b] = t_ap
        w_tp = [t_ap, t_ms]
        if tp_evac_prev is not None:
            w_tp.append(tp_evac_prev)
        for c in range(8):
            tt = p.op("tensor", "transpose", out=pb_tp[:, c * 128:(c + 1) * 128],
                      in_=lnb[:, c * 128:(c + 1) * 128], identity=eyes[:],
                      waits=(w_tp if c == 0 else []))
        tp_last = tt
        dst = bass.AP(tensor=ln0f, offset=t * 128,
                      ap=[[8 * T, 128], [T, 8], [1, 128]])
        src = bass.AP(tensor=pb_tp, offset=0, ap=[[1024, 128], [128, 8], [1, 128]])
        t_ev = p.op("vector", "tensor_copy", out=dst, in_=src, waits=[tt])
        tp_evac_prev = t_ev
        ln_done[t] = t_ev
        for dc in range(2):
            bank = pb_v[dc]
            wv = [t_ev, t_wq]
            if vband_prev[dc] is not None:
                wv.append(vband_prev[dc])
            for kc in range(8):
                p.op_nosem("tensor", "matmul", out=bank[:],
                           lhsT=ln0f[:, kc * T + t * 128: kc * T + (t + 1) * 128],
                           rhs=wqb[:, kc * 3072 + 2048 + dc * 512: kc * 3072 + 2048 + (dc + 1) * 512],
                           start=(kc == 0), stop=False, waits=(wv if kc == 0 else []))
            t_mm = p.op("tensor", "matmul", out=bank[:], lhsT=ones_r[:],
                        rhs=bqr[0:1, 2048 + dc * 512: 2048 + (dc + 1) * 512],
                        start=False, stop=True, waits=[t_on])
            vdst = bass.AP(tensor=vtok, offset=t * H * 65 + dc * 8 * 65,
                           ap=[[NT * H * 65, 128], [65, 8], [1, 64]])
            vsrc = bass.AP(tensor=pb_v[dc], offset=0,
                           ap=[[512, 128], [64, 8], [1, 64]])
            t_vev = p.op("vector", "tensor_copy", out=vdst, in_=vsrc, waits=[t_mm, t_ov])
            vband_prev[dc] = t_vev
        v_done[t] = vband_prev[1]

    t_ln_all = p.tok("tensor")
    t_v_all = p.tok("vector")

    s_out = p.dma_sem("out")
    g_prev = [None, None]
    f_prev = [None, None]
    kv_prev = None
    qef_dma_prev = None

    def pair_gemm(hp, which, dst_sb, bias_col, g_prev, t_pairstart):
        base = which * 1024 + hp * 128
        toks = []
        for g in range(NG):
            bank = pb_g[g % 2]
            wv = [t_wq, t_v_all, t_pairstart]
            if g_prev[g % 2] is not None:
                wv.append(g_prev[g % 2])
            for kc in range(8):
                tok = (p.op if kc == 7 else p.op_nosem)(
                    "tensor", "matmul", out=bank[:, 0:GS],
                    lhsT=wqb[:, kc * 3072 + base: kc * 3072 + base + 128],
                    rhs=ln0f[:, kc * T + g * GS: kc * T + (g + 1) * GS],
                    start=(kc == 0), stop=(kc == 7), waits=(wv if kc == 0 else []))
            t_e = p.op("scalar", "activation", out=dst_sb[:, g * GS:(g + 1) * GS],
                       in_=bank[:, 0:GS], func=AF.Identity,
                       bias=bqc[:, bias_col:bias_col + 1], scale=1.0,
                       waits=[tok, t_ms, t_pairstart])
            g_prev[g % 2] = t_e
            toks.append(t_e)
        return toks

    for hp in range(8):
        t_pairstart = p.tok("tensor")
        tq = pair_gemm(hp, 0, qfp, hp, g_prev, t_pairstart)
        tk = pair_gemm(hp, 1, kfp, 8 + hp, g_prev, t_pairstart)
        for which, src_sb, dst_sb, toks in ((0, qfp, qep, tq), (1, kfp, kep, tk)):
            for t in range(NT):
                t_dep = toks[(t * 128) // GS]
                for half in range(2):
                    bank = pb_f[half]
                    wv = [t_dep, t_pm]
                    if f_prev[half] is not None:
                        wv.append(f_prev[half])
                    tm = p.op("tensor", "matmul", out=bank[:],
                              lhsT=src_sb[half * 64:(half + 1) * 64, t * 128:(t + 1) * 128],
                              rhs=pm2[half * 64:(half + 1) * 64, :],
                              start=True, stop=True,
                              tile_position=(half * 64, 0), waits=wv)
                    tr = p.op("vector", "reduce_max", out=mcol[:, 2 * half:2 * half + 1],
                              in_=bank[:], axis=AX.X, waits=[tm])
                    tb = p.op("vector", "tensor_scalar", out=mcol[:, 2 * half + 1:2 * half + 2],
                              in0=mcol[:, 2 * half:2 * half + 1], scalar1=-1.0, scalar2=1e-4,
                              op0=ALU.mult, op1=ALU.add, waits=[tr])
                    te = p.op("scalar", "activation",
                              out=dst_sb[:, t * 128 + half * 64: t * 128 + (half + 1) * 64],
                              in_=bank[:], func=AF.Exp, bias=mcol[:, 2 * half + 1:2 * half + 2],
                              scale=1.0, waits=[tb, t_pairstart])
                    f_prev[half] = te
        k_last = p.tok("scalar")
        for half in range(2):
            for t in range(NT):
                wv = [k_last, t_v_all]
                if kv_prev is not None:
                    wv.append(kv_prev)
                tok = (p.op if (t == NT - 1) else p.op_nosem)(
                    "tensor", "matmul",
                    out=pb_kv[half * 64:(half + 1) * 64, :],
                    lhsT=kep[:, t * 128 + half * 64: t * 128 + (half + 1) * 64],
                    rhs=vtok[:, t * H * 65 + (2 * hp + half) * 65: t * H * 65 + (2 * hp + half + 1) * 65],
                    start=(t == 0), stop=(t == NT - 1),
                    tile_position=(0, half * 64),
                    waits=(wv if t == 0 else []))
        t_kve = p.op("vector", "tensor_copy", out=kv_all[:, hp * 65:(hp + 1) * 65],
                     in_=pb_kv[:], waits=[tok])
        kv_prev = t_kve
        for bt in range(NT // BT):
            w_tp = [p.tok("scalar"), tp_evac_prev]
            if qef_dma_prev is not None:
                w_tp.append(qef_dma_prev)
            for i in range(BT):
                t0 = bt * BT + i
                tt = p.op("tensor", "transpose", out=pb_tp[:, i * 128:(i + 1) * 128],
                          in_=qep[:, t0 * 128:(t0 + 1) * 128], identity=eyes[:],
                          waits=(w_tp if i == 0 else []))
            t_ev = p.op("vector", "tensor_copy",
                        out=qefp[:, bt * BT * 128:(bt + 1) * BT * 128],
                        in_=bass.AP(tensor=pb_tp, offset=0,
                                    ap=[[1024, 128], [1, BT * 128]]),
                        waits=[tt])
            tp_evac_prev = t_ev
        qef_dma_prev = p.dma("sync", qef[hp * 128:(hp + 1) * 128, :], qefp[:], sem=s_out,
                             waits=[tp_evac_prev])
        if bg:
            bg.p1(6)

    if bg:
        bg.p1(99)
    p.dma("sync", kvpo, kv_all[:], sem=s_out, waits=[kv_prev])
    p._waits("sync", [p.dma_all(s_out)])


def st2_body(p, g, C, T, bg=None):
    nc = p.nc
    NT = T // 128

    x = g["xin"].ap()
    qefd = g["qef"]
    kvp0 = g["kvsum"].ap()
    wprojT = g["wprojT"]
    bp_row = g["bp_row"].ap()
    eye = g["eye"].ap()
    x1o = g["x1"].ap()

    qefs = C.sb("qefs", [128, 8 * T], BF16)
    kva = C.sb("kva", [128, 8 * 65], F32)
    kvbf = C.sb("kvbf", [128, 8 * 65], BF16)
    wpb = C.sb("wpb", [128, 8 * D], BF16)
    bpr = C.sb("bpr", [1, D], BF16)
    eyes = C.sb("eyes", [128, 128], BF16)
    ones_r = C.sb("ones_r", [1, 128], BF16)
    att = C.sb("att", [128, NT * D], BF16)
    attf = C.sb("attf", [128, 8 * T], BF16)
    xs = [C.sb(f"xs{i}", [128, D], F32) for i in range(2)]
    x1s = C.sb("x1s", [128, D], F32)
    ctmp = C.sb("ctmp", [128, 6], F32)

    pb_o = [C.ps(f"pb_o{i}", [128, 65], F32) for i in range(2)]
    pb_tp = C.ps("pb_tp", [128, 1024], BF16)
    pb_g = [C.ps(f"pb_g{i}", [128, 512], F32) for i in range(2)]

    s_ld = p.dma_sem("ld")
    for hp in range(8):
        p.dma("sync", qefs[:, hp * T:(hp + 1) * T], qefd.ap()[hp * 128:(hp + 1) * 128, :], sem=s_ld)
    p.dma("sync", kva[:], kvp0, sem=s_ld)
    p.dma("sync", eyes[:], eye, sem=s_ld)
    t_ld = p.dma_all(s_ld)
    s_w = p.dma_sem("w")
    for kc in range(8):
        p.dma("gpsimd", wpb[:, kc * D:(kc + 1) * D],
              wprojT.ap()[kc * 128:(kc + 1) * 128, :], sem=s_w)
    p.dma("gpsimd", bpr[:], bp_row, sem=s_w)
    t_w = p.dma_all(s_w)
    t_on = p.op("vector", "memset", ones_r[:], 1.0)
    t_cz = p.op("vector", "memset", ctmp[:, 3:4], 0.0)

    if bg:
        bg.emit_threshold(C)
    t_kvb = p.op("vector", "tensor_copy", out=kvbf[:], in_=kva[:], waits=[t_ld])

    o_prev = [None, None]
    for hp in range(8):
        for t in range(NT):
            toks = []
            for half in range(2):
                wv = [t_ld, t_kvb]
                if o_prev[half] is not None:
                    wv.append(o_prev[half])
                tm = p.op("tensor", "matmul", out=pb_o[half][:],
                          lhsT=qefs[half * 64:(half + 1) * 64, hp * T + t * 128: hp * T + (t + 1) * 128],
                          rhs=kvbf[half * 64:(half + 1) * 64, hp * 65:(hp + 1) * 65],
                          start=True, stop=True, tile_position=(half * 64, 0), waits=wv)
                toks.append(tm)
            t1 = p.op("vector", "tensor_scalar", out=ctmp[:, 0:1], in0=pb_o[0][:, 64:65],
                      scalar1=1e-6, scalar2=None, op0=ALU.add, waits=[toks[0]])
            t2 = p.op("vector", "tensor_scalar", out=ctmp[:, 1:2], in0=pb_o[1][:, 64:65],
                      scalar1=1e-6, scalar2=None, op0=ALU.add, waits=[toks[1]])
            t3 = p.op("vector", "reciprocal", out=ctmp[:, 2:3], in_=ctmp[:, 0:1], waits=[t1, t2])
            t3b = p.op("vector", "reciprocal", out=ctmp[:, 5:6], in_=ctmp[:, 1:2], waits=[t3])
            rc = [t3, t3b]
            for half in range(2):
                h = 2 * hp + half
                te = p.op("scalar", "activation",
                          out=att[:, t * D + h * 64: t * D + (h + 1) * 64],
                          in_=pb_o[half][:, 0:64], func=AF.Identity,
                          bias=ctmp[:, 3:4],
                          scale=ctmp[:, (2 if half == 0 else 5):(3 if half == 0 else 6)],
                          waits=[rc[half], t_cz])
                o_prev[half] = te
        if bg:
            bg.p2(2)
    t_att = p.tok("scalar")

    tp_prev = None
    for t in range(NT):
        for c in range(8):
            wv = [t_att]
            if tp_prev is not None:
                wv.append(tp_prev)
            tt = p.op("tensor", "transpose", out=pb_tp[:, c * 128:(c + 1) * 128],
                      in_=att[:, t * D + c * 128: t * D + (c + 1) * 128],
                      identity=eyes[:], waits=(wv if c == 0 else []))
        dst = bass.AP(tensor=attf, offset=t * 128,
                      ap=[[8 * T, 128], [T, 8], [1, 128]])
        src = bass.AP(tensor=pb_tp, offset=0, ap=[[1024, 128], [128, 8], [1, 128]])
        tp_prev = p.op("vector", "tensor_copy", out=dst, in_=src, waits=[tt])
    t_attf = p.tok("vector")

    s_x = [p.dma_sem("x0"), p.dma_sem("x1b")]
    s_out = p.dma_sem("out")
    x_free = [None, None]
    g_prev = [None, None]
    x1_dma_prev = None
    for t in range(NT):
        b = t % 2
        w = [x_free[b]] if x_free[b] else []
        t_x = p.dma("sync", xs[b][:], x.rearrange("(n p) d -> n p d", p=128)[t], sem=s_x[b], waits=w)
        etoks = []
        for dc in range(2):
            bank = pb_g[dc]
            wv = [t_attf, t_w]
            if g_prev[dc] is not None:
                wv.append(g_prev[dc])
            for kc in range(8):
                p.op_nosem("tensor", "matmul", out=bank[:],
                           lhsT=attf[:, kc * T + t * 128: kc * T + (t + 1) * 128],
                           rhs=wpb[:, kc * D + dc * 512: kc * D + (dc + 1) * 512],
                           start=(kc == 0), stop=False, waits=(wv if kc == 0 else []))
            tm = p.op("tensor", "matmul", out=bank[:], lhsT=ones_r[:],
                      rhs=bpr[0:1, dc * 512:(dc + 1) * 512], start=False, stop=True,
                      waits=[t_on])
            te = p.op("vector", "tensor_add", out=x1s[:, dc * 512:(dc + 1) * 512],
                      in0=pb_g[dc][:], in1=xs[b][:, dc * 512:(dc + 1) * 512],
                      waits=[tm, t_x, x1_dma_prev] if x1_dma_prev else [tm, t_x])
            g_prev[dc] = te
            etoks.append(te)
        x_free[b] = etoks[-1]
        x1_dma_prev = p.dma("sync", x1o.rearrange("(n p) d -> n p d", p=128)[t],
                            x1s[:], sem=s_out, waits=etoks)
        if bg:
            bg.p2(3)
    if bg:
        bg.p2(99)
    p._waits("sync", [p.dma_all(s_out)])


def st3_body(p, g, C, T):
    nc = p.nc
    NT = T // 128

    x = g["x1"].ap()
    wternU = g["wternU"]
    wternD = g["wternD"]
    eye = g["eye"].ap()
    x2o = g["x2"].ap()

    wub = C.sb("wub", [128, 8 * DFF], BF16)
    wdb = C.sb("wdb", [128, 32 * D], BF16)
    eyes = C.sb("eyes", [128, 128], BF16)
    xs = [C.sb(f"xs{i}", [128, D], F32) for i in range(2)]
    lnb = C.sb("lnb", [128, D], BF16)
    ffn_f = C.sb("ffn_f", [128, 8 * 128], BF16)
    hgel = C.sb("hgel", [128, DFF], F32)
    hb = C.sb("hb", [128, DFF], BF16)
    hfb = C.sb("hfb", [128, 32 * 128], BF16)
    x2s = C.sb("x2s", [128, D], F32)
    sb = {
        "bnr": C.sb("bnr", [128, 48], F32),
        "agg": C.sb("agg", [128, 2], F32),
        "rs": C.sb("rs", [128, 1], F32),
        "bi": C.sb("bi", [128, 1], F32),
        "cst": C.sb("cst", [128, 4], F32),
    }
    sc2 = C.sb("sc2", [128, 6], F32)

    pb_tp = C.ps("pb_tp", [128, 1024], BF16)
    pb_u = [C.ps(f"pb_u{i}", [128, 512], F32) for i in range(2)]
    pb_d = [C.ps(f"pb_d{i}", [128, 512], F32) for i in range(2)]

    t_c = p.op("vector", "memset", sb["cst"][:, 0:1], 1e-5)
    t_c2 = p.op("vector", "memset", sb["cst"][:, 1:2], 0.0)
    s_ms = p.dma_sem("misc")
    p.dma("sync", eyes[:], eye, sem=s_ms)
    t_ms = p.dma_all(s_ms)

    s_wl = [p.dma_sem("wla"), p.dma_sem("wlb")]
    t_wub = []
    for dc in range(8):
        for kc in range(8):
            p.dma("sync" if kc % 2 == 0 else "gpsimd",
                  wub[:, kc * DFF + dc * 512: kc * DFF + (dc + 1) * 512],
                  wternU.ap()[kc * 128:(kc + 1) * 128, dc * 512:(dc + 1) * 512],
                  sem=s_wl[kc % 2])
        t_wub.append((p.dma_all(s_wl[0]), p.dma_all(s_wl[1])))
    t_wdb = []
    for dc in range(2):
        for kc in range(32):
            p.dma("sync" if kc % 2 == 0 else "gpsimd",
                  wdb[:, kc * D + dc * 512: kc * D + (dc + 1) * 512],
                  wternD.ap()[kc * 128:(kc + 1) * 128, dc * 512:(dc + 1) * 512],
                  sem=s_wl[kc % 2])
        t_wdb.append((p.dma_all(s_wl[0]), p.dma_all(s_wl[1])))

    s_x = [p.dma_sem("x0"), p.dma_sem("x1b")]
    s_out = p.dma_sem("out")
    x_free = [None, None]
    tp_prev = None
    u_prev = [None, None]
    d_prev = [None, None]
    hg_prev = None
    out_prev = None
    for t in range(NT):
        b = t % 2
        w = [x_free[b]] if x_free[b] else []
        t_x = p.dma("sync", xs[b][:], x.rearrange("(n p) d -> n p d", p=128)[t],
                    sem=s_x[b], waits=w)
        t_rs, t_bi, t_agg = ln_stats(p, xs[b], sb, [t_c, t_c2], [t_x, t_c, t_c2])
        t_q = p.op("scalar", "activation", out=sc2[:, 0:1], in_=sb["agg"][:, 1:2],
                   func=AF.Sqrt, bias=sb["cst"][:, 1:2], scale=1.0, waits=[t_agg, t_rs])
        t_rm = p.op("vector", "tensor_mul", out=sc2[:, 1:2], in0=sc2[:, 0:1],
                    in1=sb["rs"][:], waits=[t_q])
        t_rp = p.op("vector", "tensor_scalar", out=sc2[:, 1:2], in0=sc2[:, 1:2],
                    scalar1=1e-8, scalar2=None, op0=ALU.add, waits=[t_rm])
        t_rc = p.op("vector", "reciprocal", out=sc2[:, 2:3], in_=sc2[:, 1:2], waits=[t_rp])
        t_sc = p.op("vector", "tensor_mul", out=sc2[:, 3:4], in0=sc2[:, 2:3],
                    in1=sb["rs"][:], waits=[t_rc])
        t_sb = p.op("vector", "scalar_tensor_tensor", out=sc2[:, 4:5],
                    in0=sb["agg"][:, 0:1], scalar=-1.0, in1=sc2[:, 3:4],
                    op0=ALU.mult, op1=ALU.mult, waits=[t_sc, t_agg])
        w_ap = [t_sc, t_sb, t_x] + ([tp_prev] if tp_prev else [])
        t_ap = p.op("scalar", "activation", out=lnb[:], in_=xs[b][:], func=AF.Identity,
                    bias=sc2[:, 4:5], scale=sc2[:, 3:4], waits=w_ap)
        x_free[b] = t_ap
        for c in range(8):
            wv = [t_ap, t_ms] + ([tp_prev] if tp_prev else [])
            tt = p.op("tensor", "transpose", out=pb_tp[:, c * 128:(c + 1) * 128],
                      in_=lnb[:, c * 128:(c + 1) * 128], identity=eyes[:],
                      waits=(wv if c == 0 else []))
        t_ev = p.op("vector", "tensor_copy", out=ffn_f[:],
                    in_=bass.AP(tensor=pb_tp, offset=0, ap=[[1024, 128], [1, 1024]]),
                    waits=[tt])
        ge = []
        for dc in range(8):
            bank = pb_u[dc % 2]
            wv = [t_ev, t_wub[dc][0], t_wub[dc][1]] + ([u_prev[dc % 2]] if u_prev[dc % 2] else [])
            for kc in range(8):
                tok = (p.op if kc == 7 else p.op_nosem)(
                    "tensor", "matmul", out=bank[:],
                    lhsT=ffn_f[:, kc * 128:(kc + 1) * 128],
                    rhs=wub[:, kc * DFF + dc * 512: kc * DFF + (dc + 1) * 512],
                    start=(kc == 0), stop=(kc == 7), waits=(wv if kc == 0 else []))
            t_g = p.op("scalar", "activation", out=hgel[:, dc * 512:(dc + 1) * 512],
                       in_=bank[:], func=AF.Gelu, bias=sb["cst"][:, 1:2], scale=1.0,
                       waits=[tok, t_c2] + ([hg_prev] if hg_prev else []))
            u_prev[dc % 2] = t_g
            ge.append(t_g)
        st = []
        for i in range(8):
            st.append(p.op("vector", "bn_stats", out=sb["bnr"][:, 6 * i:6 * i + 6],
                           in_=hgel[:, i * 512:(i + 1) * 512], waits=[ge[i]]))
        t_ag = p.op("vector", "bn_aggr", out=sb["agg"][:, 0:2], in_=sb["bnr"][:, 0:48],
                    waits=st)
        t_m2 = p.op("vector", "scalar_tensor_tensor", out=sc2[:, 5:6],
                    in0=sb["agg"][:, 0:1], scalar=1.0, in1=sb["agg"][:, 0:1],
                    op0=ALU.mult, op1=ALU.mult, waits=[t_ag])
        t_v2 = p.op("vector", "tensor_add", out=sc2[:, 5:6], in0=sc2[:, 5:6],
                    in1=sb["agg"][:, 1:2], waits=[t_m2])
        t_q2 = p.op("scalar", "activation", out=sc2[:, 5:6], in_=sc2[:, 5:6],
                    func=AF.Sqrt, bias=sb["cst"][:, 1:2], scale=1.0, waits=[t_v2])
        t_p2 = p.op("vector", "tensor_scalar", out=sc2[:, 5:6], in0=sc2[:, 5:6],
                    scalar1=1e-8, scalar2=None, op0=ALU.add, waits=[t_q2])
        t_r2 = p.op("vector", "reciprocal", out=sc2[:, 5:6], in_=sc2[:, 5:6], waits=[t_p2])
        t_hb = p.op("vector", "tensor_copy", out=hb[:], in_=hgel[:], waits=ge + [t_ag])
        hg_prev = t_hb
        t_hf = []
        for bt in range(4):
            wv = [t_hb] + ([t_ev2] if bt > 0 else [])
            for i in range(8):
                c = bt * 8 + i
                tt = p.op("tensor", "transpose", out=pb_tp[:, i * 128:(i + 1) * 128],
                          in_=hb[:, c * 128:(c + 1) * 128], identity=eyes[:],
                          waits=(wv if i == 0 else []))
            t_ev2 = p.op("vector", "tensor_copy",
                         out=hfb[:, bt * 1024:(bt + 1) * 1024],
                         in_=bass.AP(tensor=pb_tp, offset=0, ap=[[1024, 128], [1, 1024]]),
                         waits=[tt])
            t_hf.append(t_ev2)
        etoks = []
        for dc in range(2):
            bank = pb_d[dc]
            wv = t_hf + [t_wdb[dc][0], t_wdb[dc][1]] + ([d_prev[dc]] if d_prev[dc] else [])
            for kc in range(32):
                tok = (p.op if kc == 31 else p.op_nosem)(
                    "tensor", "matmul", out=bank[:],
                    lhsT=hfb[:, kc * 128:(kc + 1) * 128],
                    rhs=wdb[:, kc * D + dc * 512: kc * D + (dc + 1) * 512],
                    start=(kc == 0), stop=(kc == 31), waits=(wv if kc == 0 else []))
            te = p.op("vector", "scalar_tensor_tensor",
                      out=x2s[:, dc * 512:(dc + 1) * 512], in0=bank[:],
                      scalar=sc2[:, 5:6], in1=xs[b][:, dc * 512:(dc + 1) * 512],
                      op0=ALU.mult, op1=ALU.add,
                      waits=[tok, t_r2, t_x] + ([out_prev] if out_prev else []))
            d_prev[dc] = te
            etoks.append(te)
        x_free[b] = etoks[-1]
        out_prev = p.dma("sync", x2o.rearrange("(n p) d -> n p d", p=128)[t],
                         x2s[:], sem=s_out, waits=etoks)
    p._waits("sync", [p.dma_all(s_out)])


def st4a_body(p, g, C, T):
    nc = p.nc
    NT = T // 128
    NCH = T // 64
    NP = NCH // 2
    QB = min(4, NP)
    NBB = NP // QB
    TE = T + 1

    x = g["x2"].ap()
    xh = g["xhg"].ap()[0:1, :]
    wmT = g["wmT"]
    alog = g["alog"]
    dtb = g["dtb"]
    eye = g["eye"].ap()
    maskT = g["maskT"].ap()

    gro = g["gro"].ap()
    edfo = g["edf"].ap()
    ln2fo = g["ln2f"].ap()
    lnfx_o = g["lnfx"].ap()
    vbet_o = g["vbet"].ap()
    wkdw_o = g["wkdw"].ap()
    dcd = g["dcd"]
    gd = nc.dram_tensor("gd", [H, NCH], F32, kind="Internal")

    xs = [C.sb(f"xs{i}", [128, D], F32) for i in range(2)]
    lnb = C.sb("lnb", [128, D], BF16)
    ln2fx = C.sb("ln2fx", [128, 8 * TE], BF16)
    wmb = C.sb("wmb", [128, 8 * 1072], BF16)
    vbet = C.sb("vbet_s", [128, NT * D], BF16)
    wkdw = C.sb("wkdw_s", [128, NT * D], BF16)
    gr = C.sb("gr", [128, NT * 48], F32)
    eyes = C.sb("eyes", [128, 128], BF16)
    eyesF = C.sb("eyesF", [128, 128], F32)
    maskS = C.sb("maskS", [128, 64], F32)
    dtbS = C.sb("dtbS", [128, H], F32)
    negea = C.sb("negea", [128, H], F32)
    dect = C.sb("dect", [128, NT * H], F32)
    decF = C.sb("decF", [16, T], F32)
    dcum = C.sb("dcum", [16, T], F32)
    ones16 = C.sb("ones16", [16, 64], F32)
    gF = C.sb("gF", [16, NCH], F32)
    sendF = C.sb("sendF", [16, NCH + 1], F32)
    fmscr = C.sb("fmscr", [16, T], F32)
    dcumt = g["sh_dcumt"]
    dext = g["sh_dext"]
    edft = C.sb("edft", [128, NT * H], F32)
    dwt = C.sb("dwt", [128, NT * H], F32)
    gcols = g["sh_gcols"]
    hstat = C.sb("hstat", [1, 16], F32)
    sb = {
        "bnr": C.sb("bnr", [128, 12], F32),
        "agg": C.sb("agg", [128, 2], F32),
        "rs": C.sb("rs", [128, 1], F32),
        "bi": C.sb("bi", [128, 1], F32),
        "cst": C.sb("cst", [128, 4], F32),
    }
    scr = C.sb("scr", [128, 48], F32)

    pb_tp = C.ps("pb_tp", [128, 1024], BF16)
    pbF = [C.ps(f"pbF{i}", [128, 512], F32) for i in range(7)]
    pb_v = [pbF[0], pbF[1]]
    pb_gt = pbF[5][:, 0:48]
    pb_dt = pbF[5][:, 256:384]

    s_w = p.dma_sem("w")
    for kc in range(8):
        p.dma("gpsimd", wmb[:, kc * 1072:(kc + 1) * 1072],
              wmT.ap()[kc * 128:(kc + 1) * 128, :], sem=s_w)
    t_w = p.dma_all(s_w)
    s_ms = p.dma_sem("misc")
    p.dma("sync", eyes[:], eye, sem=s_ms)
    p.dma("sync", maskS[0:64, :], maskT, sem=s_ms)
    p.dma("sync", maskS[64:128, :], maskT, sem=s_ms)
    p.dma("sync", dtbS[:], bass.AP(tensor=dtb, offset=0, ap=[[0, 128], [1, H]]), sem=s_ms)
    p.dma("sync", negea[:], bass.AP(tensor=alog, offset=0, ap=[[0, 128], [1, H]]), sem=s_ms)
    s_hx = p.dma_sem("hx")
    t_ms = p.dma_all(s_ms)
    t_c = p.op("vector", "memset", sb["cst"][:, 0:1], 1e-5)
    t_c2 = p.op("vector", "memset", sb["cst"][:, 1:2], 0.0)
    t_c3 = p.op("vector", "memset", sb["cst"][:, 2:3], 1.0)
    t_o16 = p.op("vector", "memset", ones16[:], 1.0)
    t_eyf = p.op("vector", "tensor_copy", out=eyesF[:], in_=eyes[:], waits=[t_ms])
    t_snd = p.op("vector", "memset", sendF[:, 0:1], 0.0)
    t_na = p.op("scalar", "activation", out=negea[:], in_=negea[:], func=AF.Exp,
                bias=sb["cst"][:, 1:2], scale=1.0, waits=[t_ms, t_c2])
    t_na = p.op("vector", "tensor_scalar", out=negea[:], in0=negea[:],
                scalar1=-1.0, scalar2=None, op0=ALU.mult, waits=[t_na])

    xhs = C.sb("xhs", [1, D], F32)
    lnh = C.sb("lnh", [1, D], BF16)
    hms = C.sb("hms", [1, 1], F32)
    t_xh0 = p.dma("sync", xhs[:], xh, sem=s_hx)
    t_hm = p.dma("sync", hms[:], g["hmask"].ap(), sem=s_hx)
    t_xh = p.op("scalar", "activation", out=xhs[:], in_=xhs[:], func=AF.Identity,
                bias=sb["cst"][0:1, 1:2], scale=hms[0:1, 0:1],
                waits=[t_xh0, t_hm, t_c2])
    h1 = p.op("vector", "bn_stats", out=hstat[0:1, 0:6], in_=xhs[0:1, 0:512], waits=[t_xh])
    h2 = p.op("vector", "bn_stats", out=hstat[0:1, 6:12], in_=xhs[0:1, 512:1024], waits=[h1])
    h3 = p.op("vector", "bn_aggr", out=hstat[0:1, 12:14], in_=hstat[0:1, 0:12], waits=[h2])
    h4 = p.op("scalar", "activation", out=hstat[0:1, 14:15], in_=hstat[0:1, 13:14],
              func=AF.Ln, bias=sb["cst"][0:1, 0:1], scale=1.0, waits=[h3, t_c])
    h5 = p.op("scalar", "activation", out=hstat[0:1, 14:15], in_=hstat[0:1, 14:15],
              func=AF.Exp, bias=sb["cst"][0:1, 1:2], scale=-0.5, waits=[h4, t_c2])
    h6 = p.op("vector", "scalar_tensor_tensor", out=hstat[0:1, 15:16],
              in0=hstat[0:1, 12:13], scalar=-1.0, in1=hstat[0:1, 14:15],
              op0=ALU.mult, op1=ALU.mult, waits=[h3, h5])
    h7 = p.op("scalar", "activation", out=lnh[:], in_=xhs[:], func=AF.Identity,
              bias=hstat[0:1, 15:16], scale=hstat[0:1, 14:15], waits=[h5, h6])
    for c in range(8):
        tt = p.op("tensor", "transpose", out=pb_tp[:, 2 * c:2 * c + 1],
                  in_=lnh[0:1, c * 128:(c + 1) * 128], identity=eyes[0:1, 0:1],
                  waits=([h7, t_ms] if c == 0 else []))
    hdst = bass.AP(tensor=ln2fx, offset=0, ap=[[8 * TE, 128], [TE, 8], [1, 1]])
    t_hev = p.op("vector", "tensor_copy", out=hdst,
                 in_=bass.AP(tensor=pb_tp, offset=0, ap=[[1024, 128], [2, 8]]),
                 waits=[tt])

    s_x = [p.dma_sem("x0"), p.dma_sem("x1b")]
    x_free = [None, None]
    tp_prev = t_hev
    v_prev = [None, None]
    gt_prev = None
    ln_done = []
    for t in range(NT):
        b = t % 2
        w = [x_free[b]] if x_free[b] else []
        t_x = p.dma("sync", xs[b][:], x.rearrange("(n p) d -> n p d", p=128)[t],
                    sem=s_x[b], waits=w)
        t_rs, t_bi, _ = ln_stats(p, xs[b], sb, [t_c, t_c2], [t_x, t_c, t_c2])
        w_ap = [t_rs, t_bi, t_x, tp_prev]
        t_ap = p.op("scalar", "activation", out=lnb[:], in_=xs[b][:], func=AF.Identity,
                    bias=sb["bi"][:, 0:1], scale=sb["rs"][:, 0:1], waits=w_ap)
        x_free[b] = t_ap
        for c in range(8):
            tt = p.op("tensor", "transpose", out=pb_tp[:, c * 128:(c + 1) * 128],
                      in_=lnb[:, c * 128:(c + 1) * 128], identity=eyes[:],
                      waits=([t_ap, t_ms, tp_prev] if c == 0 else []))
        dst = bass.AP(tensor=ln2fx, offset=t * 128 + 1,
                      ap=[[8 * TE, 128], [TE, 8], [1, 128]])
        src = bass.AP(tensor=pb_tp, offset=0, ap=[[1024, 128], [128, 8], [1, 128]])
        t_ev = p.op("vector", "tensor_copy", out=dst, in_=src, waits=[tt])
        tp_prev = t_ev
        ln_done.append(t_ev)
        wvg = [t_ev, t_w] + ([gt_prev] if gt_prev else [])
        for kc in range(8):
            tokg = (p.op if kc == 7 else p.op_nosem)(
                "tensor", "matmul", out=pb_gt,
                lhsT=ln2fx[:, kc * TE + t * 128 + 1: kc * TE + (t + 1) * 128 + 1],
                rhs=wmb[:, kc * 1072 + 1024: kc * 1072 + 1072],
                start=(kc == 0), stop=(kc == 7), waits=(wvg if kc == 0 else []))
        t_gev = p.op("vector", "tensor_copy", out=gr[:, t * 48:(t + 1) * 48],
                     in_=pb_gt, waits=[tokg])
        gt_prev = t_gev
        t_be = p.op("scalar", "activation", out=scr[:, 0:16], in_=gr[:, t * 48 + 16: t * 48 + 32],
                    func=AF.Sigmoid, bias=sb["cst"][:, 1:2], scale=1.0, waits=[t_gev, t_c2])
        t_z = p.op("vector", "tensor_add", out=scr[:, 16:32],
                   in0=gr[:, t * 48 + 32: t * 48 + 48], in1=dtbS[:], waits=[t_gev, t_ms])
        t_az = p.op("scalar", "activation", out=scr[:, 32:48], in_=scr[:, 16:32],
                    func=AF.Abs, bias=sb["cst"][:, 1:2], scale=1.0, waits=[t_z])
        t_ex = p.op("scalar", "activation", out=scr[:, 32:48], in_=scr[:, 32:48],
                    func=AF.Exp, bias=sb["cst"][:, 1:2], scale=-1.0, waits=[t_az])
        t_ln = p.op("scalar", "activation", out=scr[:, 32:48], in_=scr[:, 32:48],
                    func=AF.Ln, bias=sb["cst"][:, 2:3], scale=1.0, waits=[t_ex, t_c3])
        t_rl = p.op("vector", "tensor_scalar_max", out=scr[:, 16:32], in0=scr[:, 16:32],
                    scalar1=0.0, waits=[t_ln])
        t_sp = p.op("vector", "tensor_add", out=scr[:, 16:32], in0=scr[:, 16:32],
                    in1=scr[:, 32:48], waits=[t_rl])
        t_dc = p.op("vector", "tensor_mul", out=dect[:, t * H:(t + 1) * H],
                    in0=scr[:, 16:32], in1=negea[:], waits=[t_sp, t_na])
        for dc in range(2):
            bank = pb_v[dc]
            wv = [t_ev, t_w] + ([v_prev[dc]] if v_prev[dc] else [])
            for kc in range(8):
                tokv = (p.op if kc == 7 else p.op_nosem)(
                    "tensor", "matmul", out=bank[:],
                    lhsT=ln2fx[:, kc * TE + t * 128 + 1: kc * TE + (t + 1) * 128 + 1],
                    rhs=wmb[:, kc * 1072 + dc * 512: kc * 1072 + (dc + 1) * 512],
                    start=(kc == 0), stop=(kc == 7), waits=(wv if kc == 0 else []))
            bview = bass.AP(tensor=scr, offset=dc * 8,
                            ap=[[48, 128], [1, 8], [0, 64]])
            t_vb = p.op("vector", "tensor_tensor",
                        out=vbet[:, t * D + dc * 512: t * D + (dc + 1) * 512],
                        in0=bank[:], in1=bview, op=ALU.mult, waits=[tokv, t_be])
            v_prev[dc] = t_vb
    t_lnall = p.tok("vector")

    dt_prev = None
    for t in range(NT):
        wv = [p.tok("vector"), t_ms] + ([dt_prev] if dt_prev else [])
        tt = p.op("tensor", "transpose", out=pbF[5][0:16, 256:384],
                  in_=dect[:, t * H:(t + 1) * H], identity=eyesF[:],
                  waits=wv + [t_eyf])
        dt_prev = p.op("vector", "tensor_copy", out=decF[:, t * 128:(t + 1) * 128],
                       in_=pbF[5][0:16, 256:384], waits=[tt])
    sc_prev = []
    for ch in range(NCH):
        tsc = p.op("vector", "tensor_tensor_scan", out=dcum[:, ch * 64:(ch + 1) * 64],
                   data0=ones16[:], data1=decF[:, ch * 64:(ch + 1) * 64],
                   initial=0.0, op0=ALU.mult, op1=ALU.add,
                   waits=[dt_prev, t_o16] if ch == 0 else [])
        sc_prev.append(tsc)
    ends = bass.AP(tensor=dcum, offset=63, ap=[[T, 16], [64, NCH]])
    t_g = p.op("scalar", "activation", out=gF[:], in_=ends, func=AF.Exp,
               bias=sb["cst"][0:16, 1:2], scale=1.0, waits=sc_prev[-1:] + sc_prev[:1])
    t_se = p.op("vector", "tensor_tensor_scan", out=sendF[:, 1:NCH + 1],
                data0=ones16[:, 0:NCH], data1=ends, initial=0.0,
                op0=ALU.mult, op1=ALU.add, waits=[sc_prev[-1], t_snd])
    oview = bass.AP(tensor=sendF, offset=0, ap=[[NCH + 1, 16], [1, NCH], [0, 64]])
    endsb = bass.AP(tensor=dcum, offset=63, ap=[[T, 16], [64, NCH], [0, 64]])

    def fm_to_tok(srcF, dstT, wtok, tt_prev):
        for t in range(NT):
            wv = [wtok, t_ms, t_eyf] + ([tt_prev] if tt_prev else [])
            tt = p.op("tensor", "transpose", out=pbF[5][0:128, 256:272],
                      in_=srcF[:, t * 128:(t + 1) * 128], identity=eyesF[0:16, 0:16],
                      waits=wv)
            tt_prev = p.op("vector", "tensor_copy", out=dstT[:, t * H:(t + 1) * H],
                           in_=pbF[5][0:128, 256:272], waits=[tt])
        return tt_prev, tt

    tt_prev, tlast = fm_to_tok(dcum, dcumt, sc_prev[-1], None)
    t_1 = p.op("vector", "tensor_add", out=fmscr[:], in0=dcum[:], in1=oview,
               waits=[t_se] + sc_prev[-1:])
    t_2 = p.op("scalar", "activation", out=fmscr[:], in_=fmscr[:], func=AF.Exp,
               bias=sb["cst"][0:16, 1:2], scale=1.0, waits=[t_1])
    tt_prev, tlast = fm_to_tok(fmscr, edft, t_2, tt_prev)
    t_3 = p.op("scalar", "activation", out=fmscr[:], in_=dcum[:], func=AF.Exp,
               bias=sb["cst"][0:16, 1:2], scale=1.0, waits=[t_2, (tlast)])
    tt_prev, tlast = fm_to_tok(fmscr, dext, t_3, tt_prev)
    t_4 = p.op("vector", "tensor_sub", out=fmscr[:], in0=dcum[:], in1=endsb,
               waits=[t_1, (tlast)])
    t_5 = p.op("scalar", "activation", out=fmscr[:], in_=fmscr[:], func=AF.Exp,
               bias=sb["cst"][0:16, 1:2], scale=-1.0, waits=[t_4])
    tt_prev, tlast = fm_to_tok(fmscr, dwt, t_5, tt_prev)
    t_decall = tt_prev

    wk_prev = None
    for t in range(NT):
        for c in range(8):
            tt = p.op("tensor", "transpose", out=pb_tp[:, c * 128:(c + 1) * 128],
                      in_=ln2fx[:, c * TE + t * 128: c * TE + (t + 1) * 128],
                      identity=eyes[:],
                      waits=([t_lnall, t_hev, t_ms] + ([wk_prev] if wk_prev else [])
                             if c == 0 else []))
        dwview = bass.AP(tensor=dwt, offset=t * H, ap=[[NT * H, 128], [1, H], [0, 64]])
        wk_prev = p.op("vector", "tensor_tensor", out=wkdw[:, t * D:(t + 1) * D],
                       in0=bass.AP(tensor=pb_tp, offset=0, ap=[[1024, 128], [1, 1024]]),
                       in1=dwview, op=ALU.mult, waits=[tt, t_decall])
    t_wkall = wk_prev

    s_d1 = p.dma_sem("dcumd")
    s_d2 = p.dma_sem("gdd")
    t_d1 = p.dma("sync", dcd.ap(), dcum[:], sem=s_d1, waits=[p.tok("scalar"), p.tok("vector")])
    t_d2 = p.dma("sync", gd.ap(), gF[:], sem=s_d2, waits=[t_g])
    t_gc0 = p.dma("sync", gcols[0:64, :],
                  bass.AP(tensor=gd, offset=0, ap=[[0, 64], [2 * NCH, 8], [1, NCH]]),
                  sem=s_d2, waits=[t_d2])
    t_gc1 = p.dma("sync", gcols[64:128, :],
                  bass.AP(tensor=gd, offset=NCH, ap=[[0, 64], [2 * NCH, 8], [1, NCH]]),
                  sem=s_d2, waits=[t_gc0])
    t_gc = p.dma_all(s_d2)

    s_out = p.dma_sem("out")
    for c in range(8):
        p.dma("sync", lnfx_o[:, c * TE:(c + 1) * TE], ln2fx[:, c * TE:(c + 1) * TE],
              sem=s_out, waits=[t_lnall, t_hev])
        p.dma("sync", ln2fo[c * 128:(c + 1) * 128, :],
              ln2fx[:, c * TE + 1:(c + 1) * TE], sem=s_out, waits=[t_lnall, t_hev])
    for t in range(0, NT, 2):
        p.dma("sync", vbet_o[:, t * D:(t + 2) * D], vbet[:, t * D:(t + 2) * D],
              sem=s_out, waits=[t_lnall])
        p.dma("gpsimd", wkdw_o[:, t * D:(t + 2) * D], wkdw[:, t * D:(t + 2) * D],
              sem=s_out, waits=[t_wkall])
    p.dma("sync", edfo, edft[:], sem=s_out, waits=[t_decall])
    p.dma("sync", gro, gr[:], sem=s_out, waits=[t_lnall])
    p._waits("sync", [p.dma_all(s_out), p.dma_all(s_d1), p.dma_all(s_d2)])
    p._waits("gpsimd", [p.dma_all(s_out)])


def st4b_body(p, g, C, T):
    nc = p.nc
    NT = T // 128
    NCH = T // 64
    NP = NCH // 2
    QB = min(4, NP)
    NBB = NP // QB
    TE = T + 1

    dcd = g["dcd"]
    maskT = g["maskT"].ap()

    oo = g["o_local"].ap()
    sfo = g["sfin"].ap()

    ln2fx = C.sb("ln2fx", [128, 8 * TE], BF16)
    vbet = C.sb("vbet_s", [128, NT * D], BF16)
    wkdw = C.sb("wkdw_s", [128, NT * D], BF16)
    dcumt = g["sh_dcumt"]
    dext = g["sh_dext"]
    gcols = g["sh_gcols"]
    maskS = C.sb("maskS", [128, 64], F32)
    spair = C.sb("spair", [128, 8 * 64], F32)
    spbf = C.sb("spbf", [128, 8 * 64], BF16)
    dcb = [C.sb(f"dcb{i}", [128, T], F32) for i in range(2)]
    e2m = [C.sb(f"e2m{i}", [128, T // 2], F32) for i in range(2)]
    itg = C.sb("itg", [128, NBB * QB * 128], BF16)
    ostg = [C.sb(f"ostg{i}", [128, 512], F32) for i in range(2)]
    ostg2 = [C.sb(f"ostg2{i}", [128, 512], F32) for i in range(2)]
    cst = C.sb("cst", [128, 2], F32)

    pbF = [C.ps(f"pbF{i}", [128, 512], F32) for i in range(7)]
    pb_os = [pbF[0], pbF[1]]
    pb_io = [pbF[2], pbF[3]]
    pb_sc = pbF[4]
    pb_su = [pbF[5][:, 64:128], pbF[6][:, 0:64]]

    lnfx_d = g["lnfx"]
    vbet_d = g["vbet"]
    wkdw_d = g["wkdw"]
    s_w = p.dma_sem("w")
    for c in range(8):
        p.dma("gpsimd", ln2fx[:, c * TE:(c + 1) * TE],
              lnfx_d.ap()[:, c * TE:(c + 1) * TE], sem=s_w)
    for t in range(0, NT, 2):
        p.dma("gpsimd", vbet[:, t * D:(t + 2) * D], vbet_d.ap()[:, t * D:(t + 2) * D], sem=s_w)
        p.dma("gpsimd", wkdw[:, t * D:(t + 2) * D], wkdw_d.ap()[:, t * D:(t + 2) * D], sem=s_w)
    t_w = p.dma_all(s_w)
    s_ms = p.dma_sem("misc")
    p.dma("sync", maskS[0:64, :], maskT, sem=s_ms)
    p.dma("sync", maskS[64:128, :], maskT, sem=s_ms)
    t_ms = p.dma_all(s_ms)
    t_s0 = p.op("vector", "memset", spair[:], 0.0)
    t_s0b = p.op("vector", "memset", spbf[:], 0.0)
    t_c2 = p.op("vector", "memset", cst[:, 1:2], 0.0)

    t_lnall = t_w
    t_decall = None
    t_wkall = t_w
    t_gc = None
    t_hev = t_w

    s_dc = [p.dma_sem("dcb0"), p.dma_sem("dcb1")]
    s_oout = p.dma_sem("oout")
    dcb_free = [None, None]
    e2_free = [None, None]
    itg_free = None
    os_free = [None, None]
    io_free = [None, None]
    su_free = [None, None]
    ost_free = [None, None]
    sc_free = None
    o_dma = []
    for j in range(8):
        tdc = []
        for half in range(2):
            h = 2 * j + half
            w = [dcb_free[half]]
            tdc.append(p.dma("sync", dcb[half][:],
                             bass.AP(tensor=dcd, offset=h * T, ap=[[0, 128], [1, T]]),
                             sem=s_dc[half], waits=[t for t in w if t]))
        e2tok = []
        dcb_last = [None, None]
        for half in range(2):
            h = 2 * j + half
            toks = []
            for par in range(2):
                dccview = bass.AP(tensor=dcb[half], offset=par * 64 * T + par * 64,
                                  ap=[[T, 64], [128, NP], [1, 64]])
                dckview = bass.AP(tensor=dcumt, offset=par * 64 * (NT * H) + h,
                                  ap=[[NT * H, 64], [H, NT], [0, 64]])
                wv = [tdc[half], t_decall] + ([e2_free[half]] if e2_free[half] else [])
                t1 = p.op("vector", "tensor_tensor",
                          out=e2m[half][par * 64:(par + 1) * 64, :],
                          in0=dccview, in1=dckview,
                          op=ALU.subtract, waits=wv)
                dcb_last[half] = t1
                mview = bass.AP(tensor=maskS, offset=par * 64 * 64,
                                ap=[[64, 64], [0, NP], [1, 64]])
                t1b = p.op("gpsimd", "tensor_tensor",
                           out=e2m[half][par * 64:(par + 1) * 64, :],
                           in0=e2m[half][par * 64:(par + 1) * 64, :], in1=mview,
                           op=ALU.add, waits=[t1, t_ms])
                t2 = p.op("scalar", "activation", out=e2m[half][par * 64:(par + 1) * 64, :],
                          in_=e2m[half][par * 64:(par + 1) * 64, :], func=AF.Exp,
                          bias=cst[par * 64:(par + 1) * 64, 1:2], scale=1.0,
                          waits=[t1b, t_c2])
                toks.append(t2)
            e2tok.append(toks)
        dcb_free = dcb_last
        it_toks = []
        sc_tok = sc_free
        for bb in range(NBB):
            for q in range(QB):
                m = bb * QB + q
                for par in range(2):
                    i = 2 * m + par
                    for half in range(2):
                        h = 2 * j + half
                        wv = [t_lnall, t_hev] + ([sc_tok] if sc_tok else [])
                        tm = p.op("tensor", "matmul",
                                  out=pb_sc[par * 64:(par + 1) * 64,
                                            q * 128 + half * 64: q * 128 + (half + 1) * 64],
                                  lhsT=ln2fx[half * 64:(half + 1) * 64,
                                             j * TE + i * 64: j * TE + (i + 1) * 64],
                                  rhs=ln2fx[half * 64:(half + 1) * 64,
                                            j * TE + i * 64 + 1: j * TE + (i + 1) * 64 + 1],
                                  start=True, stop=True,
                                  tile_position=(half * 64, par * 64), waits=wv)
            for half in range(2):
                h = 2 * j + half
                e2view = bass.AP(tensor=e2m[half], offset=bb * QB * 64,
                                 ap=[[T // 2, 128], [64, QB], [1, 64]])
                dst = bass.AP(tensor=itg, offset=bb * QB * 128 + half * 64,
                              ap=[[NBB * QB * 128, 128], [128, QB], [1, 64]])
                srcv = bass.AP(tensor=pb_sc, offset=half * 64,
                               ap=[[512, 128], [128, QB], [1, 64]])
                wv = [tm] + e2tok[half] + ([itg_free] if itg_free else [])
                t_it = p.op("vector", "tensor_tensor", out=dst, in0=srcv, in1=e2view,
                            op=ALU.mult, waits=wv)
                it_toks.append(t_it)
            sc_tok = it_toks[-1]
        sc_free = sc_tok
        e2_free = [it_toks[-2], it_toks[-1]]
        st_prev = [t_s0, t_s0b, t_s0b]
        for i in range(NCH):
            m, par = i // 2, i % 2
            bb, q = m // QB, m % QB
            ba = bb % 2
            for half in range(2):
                wv = [st_prev[-1], t_lnall] + ([os_free[ba]] if (i % (2 * QB) == 0 and os_free[ba]) else [])
                tos = p.op("tensor", "matmul",
                           out=pb_os[ba][par * 64:(par + 1) * 64,
                                         q * 128 + half * 64: q * 128 + (half + 1) * 64],
                           lhsT=ln2fx[half * 64:(half + 1) * 64,
                                      j * TE + i * 64 + 1: j * TE + (i + 1) * 64 + 1],
                           rhs=spbf[half * 64:(half + 1) * 64, j * 64:(j + 1) * 64],
                           start=True, stop=True,
                           tile_position=(half * 64, par * 64), waits=wv)
            for half in range(2):
                wv = it_toks + [t_lnall] + ([io_free[ba]] if (i % (2 * QB) == 0 and io_free[ba]) else [])
                tio = p.op("tensor", "matmul",
                           out=pb_io[ba][par * 64:(par + 1) * 64,
                                         q * 128 + half * 64: q * 128 + (half + 1) * 64],
                           lhsT=bass.AP(tensor=itg,
                                        offset=par * 64 * (NBB * QB * 128) + bb * QB * 128 + q * 128 + half * 64,
                                        ap=[[NBB * QB * 128, 64], [1, 64]]),
                           rhs=vbet[par * 64:(par + 1) * 64,
                                    m * 1024 + (2 * j + half) * 64: m * 1024 + (2 * j + half + 1) * 64],
                           start=True, stop=True,
                           tile_position=(par * 64, par * 64), waits=wv)
            for half in range(2):
                wv = [t_wkall, st_prev[-1]] + ([su_free[par]] if su_free[par] else [])
                tu = p.op("tensor", "matmul",
                          out=pb_su[par][half * 64:(half + 1) * 64, 0:64],
                          lhsT=wkdw[par * 64:(par + 1) * 64,
                                    m * 1024 + (2 * j + half) * 64: m * 1024 + (2 * j + half + 1) * 64],
                          rhs=vbet[par * 64:(par + 1) * 64,
                                   m * 1024 + (2 * j + half) * 64: m * 1024 + (2 * j + half + 1) * 64],
                          start=True, stop=True,
                          tile_position=(par * 64, half * 64), waits=wv)
            tst = p.op("vector", "scalar_tensor_tensor", out=spair[:, j * 64:(j + 1) * 64],
                       in0=spair[:, j * 64:(j + 1) * 64],
                       scalar=gcols[:, j * NCH + i: j * NCH + i + 1],
                       in1=pb_su[par], op0=ALU.mult, op1=ALU.add,
                       waits=[tu, tos, t_gc, st_prev[-1]])
            su_free[par] = tst
            tsh = p.op("vector", "tensor_copy", out=spbf[:, j * 64:(j + 1) * 64],
                       in_=spair[:, j * 64:(j + 1) * 64], waits=[tst])
            st_prev.append(tsh)
            if i % (2 * QB) == 2 * QB - 1:
                stoks = []
                for q2 in range(QB):
                    for half in range(2):
                        hcol = (2 * j + half)
                        t_sc1 = p.op("scalar", "activation",
                                     out=ostg[ba][:, q2 * 128 + half * 64: q2 * 128 + (half + 1) * 64],
                                     in_=pb_os[ba][:, q2 * 128 + half * 64: q2 * 128 + (half + 1) * 64],
                                     func=AF.Identity, bias=cst[:, 1:2],
                                     scale=dext[:, (bb * QB + q2) * H + hcol: (bb * QB + q2) * H + hcol + 1],
                                     waits=[tos, t_decall, t_c2] + ([ost_free[ba]] if ost_free[ba] else []))
                        stoks.append(t_sc1)
                t_add = p.op("vector", "tensor_add", out=ostg2[ba][:, 0:QB * 128],
                             in0=ostg[ba][:, 0:QB * 128],
                             in1=pb_io[ba][:, 0:QB * 128], waits=stoks + [tio])
                os_free[ba] = t_add
                io_free[ba] = t_add
                odst = bass.AP(tensor=oo.tensor, offset=(bb * QB) * 128 * D + (2 * j) * 64,
                               ap=[[D, 128], [128 * D, QB], [64, 2], [1, 64]])
                t_od = p.dma("scalar", odst, ostg2[ba][:, 0:QB * 128], sem=s_oout, waits=[t_add])
                ost_free[ba] = t_od
                o_dma.append(t_od)
        itg_free = p.tok("tensor")

    s_out = p.dma_sem("out")
    p.dma("sync", sfo, spair[:], sem=s_out, waits=[p.tok("vector")])
    p._waits("sync", [p.dma_all(s_out), p.dma_all(s_oout)])
    p._waits("scalar", [p.dma_all(s_oout)])



def st5_body(p, g, C, T):
    nc = p.nc
    NT = T // 128
    NCH = T // 64
    NP = NCH // 2
    QB = min(4, NP)
    NBB = NP // QB

    x = g["x2"].ap()
    od = g["o_local"]
    lnfd = g["ln2f"]
    grd = g["gro"].ap()
    edfd = g["edf"].ap()
    sind = g["sfg"].ap()[0:128, :]
    woutT = g["woutT"]
    eye = g["eye"].ap()
    x3o = g["x3"].ap()

    lnf = C.sb("lnf", [128, 8 * T], BF16)
    olo = C.sb("olo", [128, NT * D], F32)
    gr = C.sb("grs", [128, NT * 48], F32)
    gsig = C.sb("gsig", [128, NT * H], F32)
    edf = C.sb("edfs", [128, NT * H], F32)
    sinf = C.sb("sinf", [128, 8 * 64], F32)
    sinb = C.sb("sinb", [128, 8 * 64], BF16)
    wob = C.sb("wob", [128, 8 * D], BF16)
    eyes = C.sb("eyes", [128, 128], BF16)
    att = C.sb("att", [128, NT * D], BF16)
    attf = C.sb("attf", [128, 8 * T], BF16)
    ostg = [C.sb(f"ostg{i}", [128, 512], F32) for i in range(2)]
    xs = [C.sb(f"xs{i}", [128, D], F32) for i in range(2)]
    ln2s = C.sb("ln2s", [128, D], F32)
    x3s = C.sb("x3s", [128, D], F32)
    sb = {
        "bnr": C.sb("bnr", [128, 12], F32),
        "agg": C.sb("agg", [128, 2], F32),
        "rs": C.sb("rs", [128, 1], F32),
        "bi": C.sb("bi", [128, 1], F32),
        "cst": C.sb("cst", [128, 4], F32),
    }

    pb_c = [C.ps(f"pb_c{i}", [128, 512], F32) for i in range(2)]
    pb_tp = C.ps("pb_tp", [128, 1024], BF16)
    pb_g = [C.ps(f"pb_g{i}", [128, 512], F32) for i in range(2)]

    s_ld = p.dma_sem("ld")
    for c in range(8):
        p.dma("sync", lnf[:, c * T:(c + 1) * T], lnfd.ap()[c * 128:(c + 1) * 128, :], sem=s_ld)
    for t in range(NT):
        p.dma("sync", olo[:, t * D:(t + 1) * D],
              od.ap().rearrange("(n p) d -> n p d", p=128)[t], sem=s_ld)
    hmb = C.sb("hmb", [128, 1], F32)
    p.dma("sync", gr[:], grd, sem=s_ld)
    p.dma("sync", edf[:], edfd, sem=s_ld)
    p.dma("sync", sinf[:], sind, sem=s_ld)
    p.dma("sync", hmb[:], bass.AP(tensor=g["hmask"], offset=0, ap=[[0, 128], [1, 1]]),
          sem=s_ld)
    p.dma("sync", eyes[:], eye, sem=s_ld)
    t_ld = p.dma_all(s_ld)
    s_w = p.dma_sem("w")
    for kc in range(8):
        p.dma("gpsimd", wob[:, kc * D:(kc + 1) * D],
              woutT.ap()[kc * 128:(kc + 1) * 128, :], sem=s_w)
    t_w = p.dma_all(s_w)
    t_c = p.op("vector", "memset", sb["cst"][:, 0:1], 1e-5)
    t_c2 = p.op("vector", "memset", sb["cst"][:, 1:2], 0.0)
    t_gm = p.op("scalar", "activation", out=sinf[:], in_=sinf[:], func=AF.Identity,
                bias=sb["cst"][:, 1:2], scale=hmb[:, 0:1], waits=[t_ld, t_c2])
    t_sb = p.op("vector", "tensor_copy", out=sinb[:], in_=sinf[:], waits=[t_gm])
    gview_in = bass.AP(tensor=gr, offset=0, ap=[[NT * 48, 128], [48, NT], [1, 16]])
    gview_out = bass.AP(tensor=gsig, offset=0, ap=[[NT * H, 128], [16, NT], [1, 16]])
    t_gs = p.op("scalar", "activation", out=gview_out, in_=gview_in, func=AF.Sigmoid,
                bias=sb["cst"][:, 1:2], scale=1.0, waits=[t_ld, t_c2])

    c_free = [None, None]
    st_free = [None, None]
    att_prev = None
    for j in range(8):
        for bb in range(NBB):
            ba = bb % 2
            for q in range(QB):
                m = bb * QB + q
                for par in range(2):
                    i = 2 * m + par
                    for half in range(2):
                        wv = [t_ld, t_sb] + ([c_free[ba]] if (q == 0 and par == 0 and half == 0 and c_free[ba]) else [])
                        tm = p.op("tensor", "matmul",
                                  out=pb_c[ba][par * 64:(par + 1) * 64,
                                               q * 128 + half * 64: q * 128 + (half + 1) * 64],
                                  lhsT=lnf[half * 64:(half + 1) * 64,
                                           j * T + i * 64: j * T + (i + 1) * 64],
                                  rhs=sinb[half * 64:(half + 1) * 64, j * 64:(j + 1) * 64],
                                  start=True, stop=True,
                                  tile_position=(half * 64, par * 64), waits=wv)
            stoks = []
            for q2 in range(QB):
                for half in range(2):
                    hcol = 2 * j + half
                    t_s1 = p.op("scalar", "activation",
                                out=ostg[ba][:, q2 * 128 + half * 64: q2 * 128 + (half + 1) * 64],
                                in_=pb_c[ba][:, q2 * 128 + half * 64: q2 * 128 + (half + 1) * 64],
                                func=AF.Identity, bias=sb["cst"][:, 1:2],
                                scale=edf[:, (bb * QB + q2) * H + hcol: (bb * QB + q2) * H + hcol + 1],
                                waits=[tm, t_ld, t_c2] + ([st_free[ba]] if st_free[ba] else []))
                    stoks.append(t_s1)
            oview = bass.AP(tensor=olo, offset=(bb * QB) * D + (2 * j) * 64,
                            ap=[[NT * D, 128], [D, QB], [64, 2], [1, 64]])
            t_a1 = p.op("vector", "tensor_add", out=ostg[ba][:, 0:QB * 128],
                        in0=ostg[ba][:, 0:QB * 128], in1=oview, waits=stoks + [t_ld])
            gview = bass.AP(tensor=gsig, offset=(bb * QB) * H + 2 * j,
                            ap=[[NT * H, 128], [16, QB], [1, 2], [0, 64]])
            adst = bass.AP(tensor=att, offset=(bb * QB) * D + (2 * j) * 64,
                           ap=[[NT * D, 128], [D, QB], [64, 2], [1, 64]])
            t_a2 = p.op("vector", "tensor_tensor", out=adst, in0=ostg[ba][:, 0:QB * 128],
                        in1=gview, op=ALU.mult, waits=[t_a1, t_gs])
            c_free[ba] = t_a2
            st_free[ba] = t_a2
    t_att = p.tok("vector")

    tp_prev = None
    for t in range(NT):
        for c in range(8):
            wv = [t_att, t_ld] + ([tp_prev] if tp_prev else [])
            tt = p.op("tensor", "transpose", out=pb_tp[:, c * 128:(c + 1) * 128],
                      in_=att[:, t * D + c * 128: t * D + (c + 1) * 128],
                      identity=eyes[:], waits=(wv if c == 0 else []))
        dst = bass.AP(tensor=attf, offset=t * 128, ap=[[8 * T, 128], [T, 8], [1, 128]])
        srcv = bass.AP(tensor=pb_tp, offset=0, ap=[[1024, 128], [128, 8], [1, 128]])
        tp_prev = p.op("vector", "tensor_copy", out=dst, in_=srcv, waits=[tt])
    t_attf = p.tok("vector")

    s_x = [p.dma_sem("x0"), p.dma_sem("x1b")]
    s_out = p.dma_sem("out")
    x_free = [None, None]
    g_prev = [None, None]
    out_prev = None
    for t in range(NT):
        b = t % 2
        w = [x_free[b]] if x_free[b] else []
        t_x = p.dma("sync", xs[b][:], x.rearrange("(n p) d -> n p d", p=128)[t],
                    sem=s_x[b], waits=w)
        t_rs, t_bi, _ = ln_stats(p, xs[b], sb, [t_c, t_c2], [t_x, t_c, t_c2])
        t_ap = p.op("scalar", "activation", out=ln2s[:], in_=xs[b][:], func=AF.Identity,
                    bias=sb["bi"][:, 0:1], scale=sb["rs"][:, 0:1],
                    waits=[t_rs, t_bi, t_x] + ([out_prev] if out_prev else []))
        etoks = []
        for dc in range(2):
            bank = pb_g[dc]
            wv = [t_attf, t_w] + ([g_prev[dc]] if g_prev[dc] else [])
            for kc in range(8):
                tok = (p.op if kc == 7 else p.op_nosem)(
                    "tensor", "matmul", out=bank[:],
                    lhsT=attf[:, kc * T + t * 128: kc * T + (t + 1) * 128],
                    rhs=wob[:, kc * D + dc * 512: kc * D + (dc + 1) * 512],
                    start=(kc == 0), stop=(kc == 7), waits=(wv if kc == 0 else []))
            t_e1 = p.op("vector", "tensor_add", out=x3s[:, dc * 512:(dc + 1) * 512],
                        in0=bank[:], in1=xs[b][:, dc * 512:(dc + 1) * 512],
                        waits=[tok, t_x] + ([out_prev] if out_prev else []))
            t_e2 = p.op("vector", "tensor_add", out=x3s[:, dc * 512:(dc + 1) * 512],
                        in0=x3s[:, dc * 512:(dc + 1) * 512],
                        in1=ln2s[:, dc * 512:(dc + 1) * 512], waits=[t_e1, t_ap])
            g_prev[dc] = t_e2
            etoks.append(t_e2)
        x_free[b] = etoks[-1]
        out_prev = p.dma("sync", x3o.rearrange("(n p) d -> n p d", p=128)[t],
                         x3s[:], sem=s_out, waits=etoks)
    p._waits("sync", [p.dma_all(s_out)])


def st6_body(p, g, C, T):
    nc = p.nc
    NT = T // 128
    TT6 = min(512, T)
    NTT = T // TT6
    NTG = TT6 // 128
    NZC = 128

    x = g["x3"].ap()
    rmJ = g["rmJ"]
    rupJ = g["rupJ"]
    wcT = g["wcT"]
    eye = g["eye"].ap()
    yo = g["y"].ap()

    lnf = C.sb("lnf", [128, 8 * TT6], BF16)
    lnstg = C.sb("lnstg", [128, 1024], BF16)
    zb = C.sb("zb", [128, NZC * TT6], BF16)
    rms = C.sb("rms", [16, 256 * 16], BF16)
    rupq = C.sb("rupq", [16, 64 * 64], BF16)
    m1s = C.sb("m1s", [16, 2048], BF16)
    slab = C.sb("slab", [128, 32 * 128], BF16)
    ln3fd = nc.dram_tensor("ln3fd", [D, T], BF16, kind="Internal")
    m1d = nc.dram_tensor("m1d", [16, 256 * 64], BF16, kind="Internal")
    wcb = [C.sb(f"wcb{i}", [128, 4 * D], BF16) for i in range(2)]
    eyes = C.sb("eyes", [128, 128], BF16)
    xs = [C.sb(f"xs{i}", [128, D], F32) for i in range(2)]
    lnb = C.sb("lnb", [128, D], BF16)
    ys = [C.sb("ys0", [128, D], F32)] * 2
    sb = {
        "bnr": C.sb("bnr", [128, 12], F32),
        "agg": C.sb("agg", [128, 2], F32),
        "rs": C.sb("rs", [128, 1], F32),
        "bi": C.sb("bi", [128, 1], F32),
        "cst": C.sb("cst", [128, 4], F32),
    }

    pb_z = C.ps("pb_z", [128, 2048], F32)
    tp_guard = nc.psum_tensor("g_pb_tp", [128, 1024], BF16)
    pb_tp = tp_guard.__enter__()

    t_c = p.op("vector", "memset", sb["cst"][:, 0:1], 1e-5)
    t_c2 = p.op("vector", "memset", sb["cst"][:, 1:2], 0.0)
    s_ms = p.dma_sem("misc")
    p.dma("sync", eyes[:], eye, sem=s_ms)
    t_ms = p.dma_all(s_ms)
    s_r = p.dma_sem("rails")
    p.dma("gpsimd", rms[:],
          bass.AP(tensor=rmJ, offset=0, ap=[[16, 16], [256, 256], [1, 16]]), sem=s_r)
    t_r = p.dma_all(s_r)
    s_rq = p.dma_sem("railq")
    s_m1 = p.dma_sem("m1out")

    m1_prev = None
    t_md = None
    for q4 in range(4):
        wvq = [m1_prev] if m1_prev else []
        t_rq = p.dma("gpsimd", rupq[:],
                     bass.AP(tensor=rupJ, offset=q4 * 64 * 16 * 64,
                             ap=[[64, 16], [1024, 64], [1, 64]]), sem=s_rq, waits=wvq)
        for grp in range(8):
            for u in range(8):
                hrl = grp * 8 + u
                hr = q4 * 64 + hrl
                wv = [t_r, t_rq] + ([m1_prev] if m1_prev else [])
                tm = p.op("tensor", "matmul", out=pb_z[0:16, u * 64:(u + 1) * 64],
                          lhsT=rms[0:16, hr * 16:(hr + 1) * 16],
                          rhs=rupq[0:16, hrl * 64:(hrl + 1) * 64],
                          start=True, stop=True, waits=(wv if u == 0 else []))
            wv2 = [tm] + ([t_md] if (grp % 4 == 0 and t_md) else [])
            m1_prev = p.op("vector", "tensor_copy",
                           out=m1s[0:16, (grp % 4) * 512:(grp % 4 + 1) * 512],
                           in_=pb_z[0:16, 0:512], waits=wv2)
            if grp % 4 == 3:
                t_md = p.dma("sync", m1d.ap()[0:16, (q4 * 2 + grp // 4) * 2048:
                                              (q4 * 2 + grp // 4 + 1) * 2048],
                             m1s[0:16, :], sem=s_m1, waits=[m1_prev])
    t_z0 = p.op("vector", "memset", slab[:], 0.0)
    s_sl = p.dma_sem("slab")
    for g in range(4):
        for a in range(2):
            dst = bass.AP(tensor=slab, offset=(32 * g + 16 * a) * (32 * 128) + a * 64,
                          ap=[[32 * 128, 16], [128, 32], [1, 64]])
            src = bass.AP(tensor=m1d, offset=(2 * g + a) * 64,
                          ap=[[16384, 16], [4096, 4], [512, 8], [1, 64]])
            p.dma("sync", dst, src, sem=s_sl, waits=[t_md, t_z0])
    t_sl = p.dma_all(s_sl)

    s_x = [p.dma_sem("x0"), p.dma_sem("x1b")]
    x_free = [None, None]
    tp_prev = None
    for t in range(NT):
        b = t % 2
        w = [x_free[b]] if x_free[b] else []
        t_x = p.dma("sync", xs[b][:], x.rearrange("(n p) d -> n p d", p=128)[t],
                    sem=s_x[b], waits=w)
        t_rs, t_bi, _ = ln_stats(p, xs[b], sb, [t_c, t_c2], [t_x, t_c, t_c2])
        t_ap = p.op("scalar", "activation", out=lnb[:], in_=xs[b][:], func=AF.Identity,
                    bias=sb["bi"][:, 0:1], scale=sb["rs"][:, 0:1],
                    waits=[t_rs, t_bi, t_x] + ([tp_prev] if tp_prev else []))
        x_free[b] = t_ap
        for c in range(8):
            tt = p.op("tensor", "transpose", out=pb_tp[:, c * 128:(c + 1) * 128],
                      in_=lnb[:, c * 128:(c + 1) * 128], identity=eyes[:],
                      waits=([t_ap, t_ms] + ([tp_prev] if tp_prev else []) if c == 0 else []))
        t_ev = p.op("vector", "tensor_copy", out=lnstg[:],
                    in_=bass.AP(tensor=pb_tp, offset=0, ap=[[1024, 128], [1, 1024]]),
                    waits=[tt] + ([tp_prev] if tp_prev else []))
        tp_prev = p.dma("sync", bass.AP(tensor=ln3fd, offset=t * 128,
                                        ap=[[T, 128], [128 * T, 8], [1, 128]]),
                        bass.AP(tensor=lnstg, offset=0,
                                ap=[[1024, 128], [128, 8], [1, 128]]),
                        sem=p.dma_sem("lnout"), waits=[t_ev])
    t_lnall = tp_prev
    tp_guard.__exit__(None, None, None)
    pb_c = [C.ps(f"pb_c{i}", [128, 512], F32) for i in range(4)]

    s_wc = [p.dma_sem("wc0"), p.dma_sem("wc1")]
    wc_free = [None, None]
    ge_prev = [None] * 4
    out_prev = None
    y_free = [x_free[0], x_free[1]]
    s_lnin = p.dma_sem("lnin")
    lnf_free = None
    for tt6 in range(NTT):
        wv = [t_lnall] + ([lnf_free] if lnf_free else [])
        t_li = p.dma("sync", lnf[:],
                     bass.AP(tensor=ln3fd, offset=tt6 * TT6,
                             ap=[[T, 128], [128 * T, 8], [1, TT6]]),
                     sem=s_lnin, waits=wv)
        for zg in range(32):
            ztoks = []
            for u in range(4):
                zc = zg * 4 + u
                h, rp = zc // 32, zc % 32
                g = rp % 4
                ch = rp // 4
                wv = [t_sl, t_li] + ([ge_prev[u]] if ge_prev[u] else [])
                tm = p.op("tensor", "matmul",
                          out=pb_z[:, u * 512: u * 512 + TT6],
                          lhsT=bass.AP(tensor=slab,
                                       offset=32 * g * (32 * 128) + (h * 8 + rp // 4) * 128,
                                       ap=[[32 * 128, 32], [1, 128]]),
                          rhs=lnf[32 * g:32 * (g + 1), ch * TT6:(ch + 1) * TT6],
                          start=True, stop=True, tile_position=(32 * g, 0), waits=wv)
                ztoks.append(tm)
            t_ge = p.op("scalar", "activation",
                        out=bass.AP(tensor=zb, offset=(zg * 4) * TT6,
                                    ap=[[NZC * TT6, 128], [TT6, 4], [1, TT6]]),
                        in_=bass.AP(tensor=pb_z, offset=0,
                                    ap=[[2048, 128], [512, 4], [1, TT6]]),
                        func=AF.Gelu,
                        bias=sb["cst"][:, 1:2], scale=1.0, waits=ztoks + [t_c2])
            for u in range(4):
                ge_prev[u] = t_ge
        t_zall = p.tok("scalar")
        lnf_free = t_zall
        banks = [pb_z[:, i * 512:(i + 1) * 512] for i in range(4)] + [b[:] for b in pb_c]
        mm_last = [None] * 8
        for zq in range(32):
            bw = zq % 2
            wv = [wc_free[bw]] if wc_free[bw] else []
            t_wc = p.dma("gpsimd", wcb[bw][:],
                         bass.AP(tensor=wcT, offset=zq * 4 * 128 * D,
                                 ap=[[D, 128], [128 * D, 4], [1, D]]),
                         sem=s_wc[bw], waits=wv)
            for u in range(4):
                zc = zq * 4 + u
                for tg in range(NTG):
                    for dc in range(2):
                        bi_ = tg * 2 + dc
                        wv2 = [t_wc, t_zall]
                        if zc == 0 and out_prev is not None:
                            wv2.append(out_prev)
                        is_grp_last = (u == 3 and tg == NTG - 1 and dc == 1)
                        tok = (p.op if (zc >= NZC - 4 or is_grp_last) else p.op_nosem)(
                            "tensor", "matmul", out=banks[bi_],
                            lhsT=zb[:, zc * TT6 + tg * 128: zc * TT6 + (tg + 1) * 128],
                            rhs=wcb[bw][:, u * D + dc * 512: u * D + (dc + 1) * 512],
                            start=(zc == 0), stop=(zc == NZC - 1),
                            waits=(wv2 if (u == 0 and tg == 0 and dc == 0) else []))
                        if zc == NZC - 1:
                            mm_last[bi_] = tok
                        if is_grp_last:
                            wc_free[bw] = tok
        for tg in range(NTG):
            gt = tt6 * NTG + tg
            b = gt % 2
            w = [y_free[b]] if y_free[b] else []
            t_x = p.dma("sync", xs[b][:], x.rearrange("(n p) d -> n p d", p=128)[gt],
                        sem=s_x[b], waits=w)
            etoks = []
            for dc in range(2):
                te = p.op("vector", "tensor_add", out=ys[b][:, dc * 512:(dc + 1) * 512],
                          in0=banks[tg * 2 + dc], in1=xs[b][:, dc * 512:(dc + 1) * 512],
                          waits=[mm_last[tg * 2 + dc], t_x] + ([out_prev] if out_prev else []))
                etoks.append(te)
            y_free[b] = etoks[-1]
            out_prev = p.dma("sync", yo.rearrange("(n p) d -> n p d", p=128)[gt],
                             ys[b][:], sem=p.dma_sem("out"), waits=etoks)
    p._waits("sync", [p.dma_all(p.dma_sem("out"))])


def build_fused(T):
    nc = bass.Bass("TRN2", target_bir_lowering=False)
    p = Prog(nc)
    NT = T // 128
    NCH = T // 64

    def EI(n, s, d=F32):
        return nc.dram_tensor(n, s, d, kind="ExternalInput")

    def IN(n, s, d=F32):
        return nc.dram_tensor(n, s, d, kind="Internal")

    g = {
        "xin": EI("x", [T, D]),
        "hmask": EI("hmask", [1, 1]),
        "wqkvT": EI("wqkvT", [D, 3 * D]),
        "bq_row": EI("bq_row", [1, 3 * D]),
        "bq_cols": EI("bq_cols", [128, 24]),
        "pmT": EI("pmT", [64, 64]),
        "eye": EI("eye", [128, 128], BF16),
        "wprojT": EI("wprojT", [D, D]),
        "bp_row": EI("bp_row", [1, D]),
        "wupT": EI("wupT", [D, DFF]),
        "wdnT": EI("wdnT", [DFF, D]),
        "wmT": EI("wmT", [D, 1072]),
        "alog": EI("alog", [1, H]),
        "dtb": EI("dtb", [1, H]),
        "maskT": EI("maskT", [64, 64]),
        "woutT": EI("woutT", [D, D]),
        "rmJ": EI("rmJ", [256 * 16, 16]),
        "rupJ": EI("rupJ", [256 * 16, 64]),
        "wcT": EI("wcT", [16 * D, D]),
        "qef": IN("qef", [D, T], BF16),
        "wternU": nc.dram_tensor("wternU", [D, DFF], BF16,
                                 kind="ExternalOutput" if _DBG else "Internal"),
        "wternD": nc.dram_tensor("wternD", [DFF, D], BF16,
                                 kind="ExternalOutput" if _DBG else "Internal"),
        "lnfx": IN("lnfx", [128, 8 * (T + 1)], BF16),
        "vbet": IN("vbet", [128, NT * D], BF16),
        "wkdw": IN("wkdw", [128, NT * D], BF16),
        "kvpo": IN("kvpo", [128, 8 * 65]),
        "kvsum": IN("kvsum", [128, 8 * 65]),
        "x1": IN("x1", [T, D]),
        "x2": IN("x2", [T, D]),
        "xhg": IN("xhg", [2, D]),
        "gro": IN("gro", [128, NT * 48]),
        "edf": IN("edf", [128, NT * H]),
        "ln2f": IN("ln2f", [D, T], BF16),
        "dcd": IN("dcd", [H, T]),
        "o_local": IN("o_local", [T, D]),
        "sfin": IN("sfin", [128, 8 * 64]),
        "sfg": IN("sfg", [256, 8 * 64]),
        "x3": IN("x3", [T, D]),
        "y": nc.dram_tensor("y", [T, D], F32, kind="ExternalOutput"),
    }

    GRP = [[0, 1], [2, 3], [4, 5], [6, 7]]

    C13 = StageCtx(nc, "s13_")
    tern = TernBG(p, C13, g)
    C = StageCtx(nc, "a_"); st1_body(p, g, C, T, bg=tern); C.close()
    p.barrier()
    p.collective("AllReduce", ALU.add, GRP,
                 ins=[g["kvpo"].ap().opt()], outs=[g["kvsum"].ap().opt()])
    p.barrier()
    C = StageCtx(nc, "b_"); st2_body(p, g, C, T, bg=tern); C.close()
    C13.close()
    p.barrier()
    C = StageCtx(nc, "c_"); st3_body(p, g, C, T); C.close()
    p.barrier()
    p.collective("AllGather", ALU.bypass, GRP,
                 ins=[g["x2"].ap()[T - 1:T, :].opt()], outs=[g["xhg"].ap().opt()])
    p.barrier()
    C4 = StageCtx(nc, "s4_")
    g["sh_dcumt"] = C4.sb("dcumt", [128, NT * H])
    g["sh_dext"] = C4.sb("dext", [128, NT * H])
    g["sh_gcols"] = C4.sb("gcols", [128, 8 * NCH])
    C = StageCtx(nc, "d_"); st4a_body(p, g, C, T); C.close()
    p.barrier()
    C = StageCtx(nc, "e_"); st4b_body(p, g, C, T); C.close()
    C4.close()
    p.barrier()
    p.collective("AllGather", ALU.bypass, GRP,
                 ins=[g["sfin"].ap().opt()], outs=[g["sfg"].ap().opt()])
    p.barrier()
    C = StageCtx(nc, "f_"); st5_body(p, g, C, T); C.close()
    p.barrier()
    C = StageCtx(nc, "g_"); st6_body(p, g, C, T); C.close()
    return nc


def _run(nc, in_maps):
    return run_bass_kernel_spmd(nc, in_maps, core_ids=list(range(NCORE))).results


def kernel(L0_attn_x, L0_attn_W_qkv, L0_attn_b_qkv, L0_attn_W_proj, L0_attn_b_proj,
           L0_attn_proj_mat, L0_ffn_w_up, L0_ffn_w_down, L1_attn_W_write,
           L1_attn_W_gate, L1_attn_W_beta, L1_attn_W_alpha, L1_attn_W_out,
           L1_attn_A_log, L1_attn_dt_bias, L1_ffn_rail_maker, L1_ffn_rail_up_proj,
           L1_ffn_w_cproj):
    T = 2048
    xs = [np.ascontiguousarray(s) for s in
          np.asarray(L0_attn_x, dtype=np.float32).reshape(NCORE, T, D)]

    nc = build_fused(T)
    b_qkv = np.asarray(L0_attn_b_qkv)
    wm = np.concatenate([np.asarray(L1_attn_W_write).T, np.asarray(L1_attn_W_gate).T,
                         np.asarray(L1_attn_W_beta).T, np.asarray(L1_attn_W_alpha).T],
                        axis=1)
    base = {
        "wqkvT": np.ascontiguousarray(np.asarray(L0_attn_W_qkv).T),
        "bq_row": np.ascontiguousarray(b_qkv.reshape(1, -1)),
        "bq_cols": np.ascontiguousarray(b_qkv.reshape(24, 128).T),
        "pmT": np.ascontiguousarray(np.asarray(L0_attn_proj_mat).T),
        "eye": np.eye(128, dtype=np.float32).astype(ml_dtypes.bfloat16),
        "wprojT": np.ascontiguousarray(np.asarray(L0_attn_W_proj).T),
        "bp_row": np.ascontiguousarray(np.asarray(L0_attn_b_proj).reshape(1, -1)),
        "wupT": np.ascontiguousarray(np.asarray(L0_ffn_w_up).T),
        "wdnT": np.ascontiguousarray(np.asarray(L0_ffn_w_down).T),
        "wmT": np.ascontiguousarray(wm),
        "alog": np.ascontiguousarray(np.asarray(L1_attn_A_log).reshape(1, -1)),
        "dtb": np.ascontiguousarray(np.asarray(L1_attn_dt_bias).reshape(1, -1)),
        "maskT": np.ascontiguousarray(
            np.where(np.triu(np.ones((64, 64), bool), 1), 0.0, -1e30)
            .astype(np.float32)),
        "woutT": np.ascontiguousarray(np.asarray(L1_attn_W_out).T),
        "rmJ": np.ascontiguousarray(
            np.asarray(L1_ffn_rail_maker).transpose(0, 1, 3, 2).reshape(256 * 16, 16)),
        "rupJ": np.ascontiguousarray(
            np.asarray(L1_ffn_rail_up_proj).transpose(0, 1, 3, 2)
            .reshape(256, 16, 64).reshape(256 * 16, 64)),
        "wcT": np.ascontiguousarray(np.asarray(L1_ffn_w_cproj).T),
    }
    maps = []
    for c in range(NCORE):
        m = dict(base)
        m["x"] = xs[c]
        m["hmask"] = np.array([[1.0 if c % 2 else 0.0]], np.float32)
        maps.append(m)
    r = _run(nc, maps)
    globals()["_last_r"] = r
    y = np.concatenate([np.asarray(r[c]["y"]) for c in range(NCORE)], axis=0)
    return y.reshape(4, 4096, D)


# version 14
# speedup vs baseline: 1.1542x; 1.1542x over previous
import contextlib
import numpy as np
import ml_dtypes
import concourse.bass as bass
import concourse.mybir as mybir
from concourse.bass_utils import run_bass_kernel_spmd

F32 = mybir.dt.float32
BF16 = mybir.dt.bfloat16
AF = mybir.ActivationFunctionType
ALU = mybir.AluOpType
AX = mybir.AxisListType

ENGINES = ("tensor", "vector", "scalar", "gpsimd", "sync")
_DBG = False

D = 1024
H = 16
HD = 64
NF = 64
DFF = 4096
NCORE = 8


class Prog:
    def __init__(self, nc: bass.Bass):
        self.nc = nc
        self.sem = {}
        self.cnt = {}
        for e in ENGINES:
            self.sem[e] = nc.alloc_semaphore(name=f"es_{e}")
            self.cnt[e] = 0
        self._cc_toks = []
        self._ccidx = 0
        self._dma_sems = {}
        self._bar = None
        self._bar_seen = set()

    def eng(self, name):
        return getattr(self.nc, name)

    def barrier(self):
        toks = [(self.sem[e], self.cnt[e]) for e in ENGINES]
        toks += [(rec[0], rec[1]) for rec in self._dma_sems.values()]
        toks += self._cc_toks
        self._bar = [t for t in toks if t[1] > 0]
        self._bar_seen = set()

    def _waits(self, engine, waits):
        if self._bar is not None and engine not in self._bar_seen:
            self._bar_seen.add(engine)
            waits = list(waits) + self._bar
        best = {}
        for t in waits:
            if t is None:
                continue
            s, v = t
            k = id(s)
            if k not in best or best[k][1] < v:
                best[k] = (s, v)
        e = self.eng(engine)
        for (s, v) in best.values():
            e.wait_ge(s, v)

    def op(self, engine, method, *args, waits=(), **kwargs):
        self._waits(engine, waits)
        inst = getattr(self.eng(engine), method)(*args, **kwargs)
        self.cnt[engine] += 1
        inst.then_inc(self.sem[engine], 1)
        return (self.sem[engine], self.cnt[engine])

    def op_nosem(self, engine, method, *args, waits=(), **kwargs):
        self._waits(engine, waits)
        getattr(self.eng(engine), method)(*args, **kwargs)
        return None

    def dma_sem(self, name):
        if name not in self._dma_sems:
            self._dma_sems[name] = [self.nc.alloc_semaphore(name=f"dma_{name}"), 0]
        return name

    def dma(self, engine, out, in_, sem, waits=(), **kwargs):
        self._waits(engine, waits)
        inst = self.eng(engine).dma_start(out=out, in_=in_, **kwargs)
        rec = self._dma_sems[sem]
        rec[1] += 16
        inst.then_inc(rec[0], 16)
        return (rec[0], rec[1])

    def dma_all(self, sem):
        rec = self._dma_sems[sem]
        return (rec[0], rec[1])

    def collective(self, kind, alu, groups, ins, outs, waits=()):
        self._waits("gpsimd", waits)
        inst = self.nc.gpsimd.collective_compute(kind, alu, replica_groups=groups,
                                                 ins=ins, outs=outs)
        sem = self.nc.alloc_semaphore(name=f"cc{self._ccidx}")
        self._ccidx += 1
        inst.then_inc(sem)
        self._cc_toks.append((sem, 1))
        return (sem, 1)

    def tok(self, engine):
        return (self.sem[engine], self.cnt[engine])


class StageCtx:

    def __init__(self, nc, tag):
        self.nc, self.tag = nc, tag
        self.stack = contextlib.ExitStack()

    def sb(self, name, shape, dtype=F32):
        return self.stack.enter_context(
            self.nc.sbuf_tensor(self.tag + name, shape, dtype))

    def ps(self, name, shape, dtype=F32):
        return self.stack.enter_context(
            self.nc.psum_tensor(self.tag + name, shape, dtype))

    def close(self):
        self.stack.close()


class TernBG:

    def __init__(self, p, Csh, g):
        self.p = p
        nc = p.nc
        self.wld = [Csh.sb("t_wldA", [128, 2048], F32),
                    Csh.sb("t_wldB", [128, 2048], F32)]
        self.sgn = Csh.sb("t_sgn", [128, 2048], F32)
        self.wt = [Csh.sb(f"t_wt{i}", [128, 2048], BF16) for i in range(2)]
        self.rsum = Csh.sb("t_rsum", [128, 64], F32)
        self.thr = Csh.sb("t_thr", [128, 2], F32)
        self.onesf = Csh.sb("t_onesf", [128, 1], F32)
        self.zc = Csh.sb("t_zc", [128, 1], F32)
        self.thr_d = nc.dram_tensor("thr_d", [1, 2], F32, kind="Internal")
        self.t_zc = p.op("vector", "memset", self.zc[:], 0.0)
        self.t_of = p.op("vector", "memset", self.onesf[:], 1.0)
        self.specs = []
        for c in range(8):
            for su in range(2):
                self.specs.append((g["wupT"], c, su, 2048, c * 2 + su, 0, g["wternU"]))
        for c in range(32):
            self.specs.append((g["wdnT"], c, 0, 1024, 20 + c, 1, g["wternD"]))
        self.s_l1 = [p.dma_sem("tbg1a"), p.dma_sem("tbg1b")]
        self.s_l2 = [p.dma_sem("tbg2a"), p.dma_sem("tbg2b")]
        self.s_o = [p.dma_sem("tbgoa"), p.dma_sem("tbgob")]
        self.i1 = 0
        self.i2 = 0
        self.p1_free = [None, None]
        self.p2_free = [None, None]
        self.wt_free = [None, None]
        self.acc_last = None
        self.sel_last = None
        self.t_thr = [None, None]

    def p1(self, k):
        p = self.p
        for _ in range(k):
            if self.i1 >= len(self.specs):
                return
            i = self.i1
            self.i1 += 1
            wT, c, su, SW, rcol, tc, dst = self.specs[i]
            b = i % 2
            buf = self.wld[b]
            w = [self.p1_free[b]] if self.p1_free[b] else []
            t_l = p.dma("sync" if b == 0 else "gpsimd", buf[:, 0:SW],
                        wT.ap()[c * 128:(c + 1) * 128, su * SW:(su + 1) * SW],
                        sem=self.s_l1[b], waits=w)
            t_a = p.op("scalar", "activation", out=buf[:, 0:SW], in_=buf[:, 0:SW],
                       func=AF.Abs, bias=self.zc[:, 0:1], scale=1.0,
                       waits=[t_l, self.t_zc])
            t_r = p.op("vector", "reduce_sum", out=self.rsum[:, rcol:rcol + 1],
                       in_=buf[:, 0:SW], axis=AX.X, waits=[t_a])
            self.p1_free[b] = t_r
            self.acc_last = t_r

    def emit_threshold(self, C):
        p = self.p
        pb_a = C.ps("t_pb_a", [1, 2], F32)
        s_t = p.dma_sem("tbgt")
        for base, nsums, wd, nch, tc in ((0, 16, DFF, 8, 0), (20, 32, 1024, 32, 1)):
            t_r = p.op("vector", "reduce_sum",
                       out=self.rsum[:, base + nsums:base + nsums + 1],
                       in_=self.rsum[:, base:base + nsums], axis=AX.X, waits=[])
            t_m = p.op("tensor", "matmul", out=pb_a[0:1, tc:tc + 1],
                       lhsT=self.onesf[:],
                       rhs=self.rsum[:, base + nsums:base + nsums + 1],
                       start=True, stop=True, waits=[t_r, self.t_of])
            t_t = p.op("vector", "tensor_scalar", out=self.thr[0:1, tc:tc + 1],
                       in0=pb_a[0:1, tc:tc + 1], scalar1=0.5 / (nch * 128 * wd),
                       scalar2=None, op0=ALU.mult, waits=[t_m])
            t_d1 = p.dma("sync", self.thr_d.ap()[0:1, tc:tc + 1],
                         self.thr[0:1, tc:tc + 1], sem=s_t, waits=[t_t])
            t_d2 = p.dma("sync", self.thr[:, tc:tc + 1],
                         bass.AP(tensor=self.thr_d, offset=tc, ap=[[0, 128], [1, 1]]),
                         sem=s_t, waits=[t_d1])
            self.t_thr[tc] = t_d2

    def p2(self, k):
        p = self.p
        for _ in range(k):
            if self.i2 >= len(self.specs):
                return
            i = self.i2
            self.i2 += 1
            wT, c, su, SW, rcol, tc, dst = self.specs[i]
            b = i % 2
            buf = self.wld[b]
            w = [self.p2_free[b]] if self.p2_free[b] else [self.acc_last]
            t_l = p.dma("sync" if b == 0 else "gpsimd", buf[:, 0:SW],
                        wT.ap()[c * 128:(c + 1) * 128, su * SW:(su + 1) * SW],
                        sem=self.s_l2[b], waits=w)
            t_s = p.op("scalar", "sign", out=self.sgn[:, 0:SW], in_=buf[:, 0:SW],
                       waits=[t_l, self.t_thr[tc]] +
                             ([self.sel_last] if self.sel_last else []))
            t_a = p.op("scalar", "activation", out=buf[:, 0:SW], in_=buf[:, 0:SW],
                       func=AF.Abs, bias=self.zc[:, 0:1], scale=1.0, waits=[t_s])
            wv = [t_a] + ([self.wt_free[b]] if self.wt_free[b] else [])
            t_w = p.op("vector", "scalar_tensor_tensor", out=self.wt[b][:, 0:SW],
                       in0=buf[:, 0:SW], scalar=self.thr[:, tc:tc + 1],
                       in1=self.sgn[:, 0:SW], op0=ALU.is_gt, op1=ALU.mult,
                       waits=wv)
            t_o = p.dma("sync" if b == 0 else "gpsimd",
                        dst.ap()[c * 128:(c + 1) * 128, su * SW:(su + 1) * SW],
                        self.wt[b][:, 0:SW], sem=self.s_o[b], waits=[t_w])
            self.p2_free[b] = t_w
            self.sel_last = t_w
            self.wt_free[b] = t_o


def bcast_ap(dram_handle, parts, free, offset=0, row_len=None):
    return bass.AP(tensor=dram_handle, offset=offset, ap=[[0, parts], [1, free]])


def ln_stats(p, xs, sb, pre, waits):
    t1 = p.op("vector", "bn_stats", out=sb["bnr"][:, 0:6], in_=xs[:, 0:512], waits=waits)
    t2 = p.op("vector", "bn_stats", out=sb["bnr"][:, 6:12], in_=xs[:, 512:1024], waits=[t1])
    t3 = p.op("vector", "bn_aggr", out=sb["agg"][:, 0:2], in_=sb["bnr"][:, 0:12], waits=[t2])
    t4 = p.op("scalar", "activation", out=sb["rs"][:], in_=sb["agg"][:, 1:2], func=AF.Ln,
              bias=sb["cst"][:, 0:1], scale=1.0, waits=[t3] + list(pre))
    t5 = p.op("scalar", "activation", out=sb["rs"][:], in_=sb["rs"][:], func=AF.Exp,
              bias=sb["cst"][:, 1:2], scale=-0.5, waits=[t4])
    t6 = p.op("vector", "scalar_tensor_tensor", out=sb["bi"][:], in0=sb["agg"][:, 0:1],
              scalar=-1.0, in1=sb["rs"][:], op0=ALU.mult, op1=ALU.mult, waits=[t3, t5])
    return t5, t6, t3



def st1_body(p, g, C, T, bg=None):
    nc = p.nc
    NT = T // 128
    GS = min(512, T)
    NG = T // GS
    BT = min(8, NT)

    x = g["xin"].ap()
    wqkvT = g["wqkvT"]
    bq_row = g["bq_row"].ap()
    bq_cols = g["bq_cols"].ap()
    pmT = g["pmT"]
    eye = g["eye"].ap()

    qef = g["qef"].ap()
    kvpo = g["kvpo"].ap()

    xs = [C.sb(f"xs{i}", [128, D], F32) for i in range(2)]
    lnb = C.sb("lnb", [128, D], BF16)
    ln0f = C.sb("ln0f", [128, 8 * T], BF16)
    wqb = C.sb("wqb", [128, 8 * 3 * D], BF16)
    bqr = C.sb("bqr", [1, 3 * D], BF16)
    bqc = C.sb("bqc", [128, 24], F32)
    pm2 = C.sb("pm2", [128, 64], BF16)
    pmf = C.sb("pmf", [128, 64], F32)
    eyes = C.sb("eyes", [128, 128], BF16)
    ones_r = C.sb("ones_r", [1, 128], BF16)
    vtok = C.sb("vtok", [128, NT * H * 65], BF16)
    qfp = C.sb("qfp", [128, T], BF16)
    kfp = C.sb("kfp", [128, T], BF16)
    qep = C.sb("qep", [128, T], BF16)
    kep = C.sb("kep", [128, T], BF16)
    qefp = C.sb("qefp", [128, T], BF16)
    kv_all = C.sb("kv_all", [128, 8 * 65], F32)
    sb = {
        "bnr": C.sb("bnr", [128, 12], F32),
        "agg": C.sb("agg", [128, 2], F32),
        "rs": C.sb("rs", [128, 1], F32),
        "bi": C.sb("bi", [128, 1], F32),
        "cst": C.sb("cst", [128, 4], F32),
    }
    mcol = C.sb("mcol", [128, 4], F32)

    pb_tp = C.ps("pb_tp", [128, 1024], BF16)
    pb_v = [C.ps(f"pb_v{i}", [128, 512], F32) for i in range(2)]
    pb_g = [C.ps(f"pb_g{i}", [128, 512], F32) for i in range(2)]
    pb_f = [C.ps(f"pb_f{i}", [128, 64], F32) for i in range(2)]
    pb_kv = C.ps("pb_kv", [128, 65], F32)

    s_wq = p.dma_sem("wq")
    for kc in range(8):
        p.dma("gpsimd", wqb[:, kc * 3072:(kc + 1) * 3072],
              wqkvT[kc * 128:(kc + 1) * 128, :], sem=s_wq)
    p.dma("gpsimd", bqr[:], bq_row, sem=s_wq)
    t_wq = p.dma_all(s_wq)

    s_ms = p.dma_sem("misc")
    p.dma("sync", bqc[:], bq_cols, sem=s_ms)
    p.dma("sync", eyes[:], eye, sem=s_ms)
    p.dma("sync", pmf[0:64, :], pmT.ap(), sem=s_ms)
    p.dma("sync", pmf[64:128, :], pmT.ap(), sem=s_ms)
    t_ms = p.dma_all(s_ms)

    t_c = p.op("vector", "memset", sb["cst"][:, 0:1], 1e-5)
    t_c2 = p.op("vector", "memset", sb["cst"][:, 1:2], 0.0)
    t_on = p.op("vector", "memset", ones_r[:], 1.0)
    ones_view = bass.AP(tensor=vtok, offset=64,
                        ap=[[NT * H * 65, 128], [65, NT * H], [1, 1]])
    t_ov = p.op("vector", "memset", ones_view, 1.0, waits=[])
    t_pm = p.op("scalar", "activation", out=pm2[:], in_=pmf[:], func=AF.Identity,
                bias=sb["cst"][:, 1:2], scale=float(HD ** -0.25), waits=[t_ms, t_c2])

    x_t = x.rearrange("(n p) d -> n p d", p=128)

    s_x = [p.dma_sem("x0"), p.dma_sem("x1")]
    x_free = [None, None]
    ln_done = [None] * NT
    v_done = [None] * NT
    tp_evac_prev = None
    vband_prev = [None, None]
    for t in range(NT):
        b = t % 2
        w = [x_free[b]] if x_free[b] else []
        t_x = p.dma("sync", xs[b][:], x_t[t], sem=s_x[b], waits=w)
        t_rs, t_bi, t_agg = ln_stats(p, xs[b], sb, [t_c, t_c2], [t_x, t_c, t_c2])
        w_apply = [t_rs, t_bi, t_x]
        if tp_evac_prev is not None:
            w_apply.append(tp_last)
        t_ap = p.op("scalar", "activation", out=lnb[:], in_=xs[b][:], func=AF.Identity,
                    bias=sb["bi"][:, 0:1], scale=sb["rs"][:, 0:1], waits=w_apply)
        x_free[b] = t_ap
        w_tp = [t_ap, t_ms]
        if tp_evac_prev is not None:
            w_tp.append(tp_evac_prev)
        for c in range(8):
            tt = p.op("tensor", "transpose", out=pb_tp[:, c * 128:(c + 1) * 128],
                      in_=lnb[:, c * 128:(c + 1) * 128], identity=eyes[:],
                      waits=(w_tp if c == 0 else []))
        tp_last = tt
        dst = bass.AP(tensor=ln0f, offset=t * 128,
                      ap=[[8 * T, 128], [T, 8], [1, 128]])
        src = bass.AP(tensor=pb_tp, offset=0, ap=[[1024, 128], [128, 8], [1, 128]])
        t_ev = p.op("vector", "tensor_copy", out=dst, in_=src, waits=[tt])
        tp_evac_prev = t_ev
        ln_done[t] = t_ev
        for dc in range(2):
            bank = pb_v[dc]
            wv = [t_ev, t_wq]
            if vband_prev[dc] is not None:
                wv.append(vband_prev[dc])
            for kc in range(8):
                p.op_nosem("tensor", "matmul", out=bank[:],
                           lhsT=ln0f[:, kc * T + t * 128: kc * T + (t + 1) * 128],
                           rhs=wqb[:, kc * 3072 + 2048 + dc * 512: kc * 3072 + 2048 + (dc + 1) * 512],
                           start=(kc == 0), stop=False, waits=(wv if kc == 0 else []))
            t_mm = p.op("tensor", "matmul", out=bank[:], lhsT=ones_r[:],
                        rhs=bqr[0:1, 2048 + dc * 512: 2048 + (dc + 1) * 512],
                        start=False, stop=True, waits=[t_on])
            vdst = bass.AP(tensor=vtok, offset=t * H * 65 + dc * 8 * 65,
                           ap=[[NT * H * 65, 128], [65, 8], [1, 64]])
            vsrc = bass.AP(tensor=pb_v[dc], offset=0,
                           ap=[[512, 128], [64, 8], [1, 64]])
            t_vev = p.op("vector", "tensor_copy", out=vdst, in_=vsrc, waits=[t_mm, t_ov])
            vband_prev[dc] = t_vev
        v_done[t] = vband_prev[1]
        if bg:
            bg.p1(3)

    t_ln_all = p.tok("tensor")
    t_v_all = p.tok("vector")

    s_out = p.dma_sem("out")
    g_prev = [None, None]
    f_prev = [None, None]
    kv_prev = None
    qef_dma_prev = None

    def pair_gemm(hp, which, dst_sb, bias_col, g_prev, t_pairstart):
        base = which * 1024 + hp * 128
        toks = []
        for g in range(NG):
            bank = pb_g[g % 2]
            wv = [t_wq, t_v_all, t_pairstart]
            if g_prev[g % 2] is not None:
                wv.append(g_prev[g % 2])
            for kc in range(8):
                tok = (p.op if kc == 7 else p.op_nosem)(
                    "tensor", "matmul", out=bank[:, 0:GS],
                    lhsT=wqb[:, kc * 3072 + base: kc * 3072 + base + 128],
                    rhs=ln0f[:, kc * T + g * GS: kc * T + (g + 1) * GS],
                    start=(kc == 0), stop=(kc == 7), waits=(wv if kc == 0 else []))
            t_e = p.op("scalar", "activation", out=dst_sb[:, g * GS:(g + 1) * GS],
                       in_=bank[:, 0:GS], func=AF.Identity,
                       bias=bqc[:, bias_col:bias_col + 1], scale=1.0,
                       waits=[tok, t_ms, t_pairstart])
            g_prev[g % 2] = t_e
            toks.append(t_e)
        return toks

    for hp in range(8):
        t_pairstart = p.tok("tensor")
        tq = pair_gemm(hp, 0, qfp, hp, g_prev, t_pairstart)
        tk = pair_gemm(hp, 1, kfp, 8 + hp, g_prev, t_pairstart)
        for which, src_sb, dst_sb, toks in ((0, qfp, qep, tq), (1, kfp, kep, tk)):
            for t in range(NT):
                t_dep = toks[(t * 128) // GS]
                for half in range(2):
                    bank = pb_f[half]
                    wv = [t_dep, t_pm]
                    if f_prev[half] is not None:
                        wv.append(f_prev[half])
                    tm = p.op("tensor", "matmul", out=bank[:],
                              lhsT=src_sb[half * 64:(half + 1) * 64, t * 128:(t + 1) * 128],
                              rhs=pm2[half * 64:(half + 1) * 64, :],
                              start=True, stop=True,
                              tile_position=(half * 64, 0), waits=wv)
                    tr = p.op("vector", "reduce_max", out=mcol[:, 2 * half:2 * half + 1],
                              in_=bank[:], axis=AX.X, waits=[tm])
                    tb = p.op("vector", "tensor_scalar", out=mcol[:, 2 * half + 1:2 * half + 2],
                              in0=mcol[:, 2 * half:2 * half + 1], scalar1=-1.0, scalar2=1e-4,
                              op0=ALU.mult, op1=ALU.add, waits=[tr])
                    te = p.op("scalar", "activation",
                              out=dst_sb[:, t * 128 + half * 64: t * 128 + (half + 1) * 64],
                              in_=bank[:], func=AF.Exp, bias=mcol[:, 2 * half + 1:2 * half + 2],
                              scale=1.0, waits=[tb, t_pairstart])
                    f_prev[half] = te
        k_last = p.tok("scalar")
        for half in range(2):
            for t in range(NT):
                wv = [k_last, t_v_all]
                if kv_prev is not None:
                    wv.append(kv_prev)
                tok = (p.op if (t == NT - 1) else p.op_nosem)(
                    "tensor", "matmul",
                    out=pb_kv[half * 64:(half + 1) * 64, :],
                    lhsT=kep[:, t * 128 + half * 64: t * 128 + (half + 1) * 64],
                    rhs=vtok[:, t * H * 65 + (2 * hp + half) * 65: t * H * 65 + (2 * hp + half + 1) * 65],
                    start=(t == 0), stop=(t == NT - 1),
                    tile_position=(0, half * 64),
                    waits=(wv if t == 0 else []))
        t_kve = p.op("vector", "tensor_copy", out=kv_all[:, hp * 65:(hp + 1) * 65],
                     in_=pb_kv[:], waits=[tok])
        kv_prev = t_kve
        for bt in range(NT // BT):
            w_tp = [p.tok("scalar"), tp_evac_prev]
            if qef_dma_prev is not None:
                w_tp.append(qef_dma_prev)
            for i in range(BT):
                t0 = bt * BT + i
                tt = p.op("tensor", "transpose", out=pb_tp[:, i * 128:(i + 1) * 128],
                          in_=qep[:, t0 * 128:(t0 + 1) * 128], identity=eyes[:],
                          waits=(w_tp if i == 0 else []))
            t_ev = p.op("vector", "tensor_copy",
                        out=qefp[:, bt * BT * 128:(bt + 1) * BT * 128],
                        in_=bass.AP(tensor=pb_tp, offset=0,
                                    ap=[[1024, 128], [1, BT * 128]]),
                        waits=[tt])
            tp_evac_prev = t_ev
        qef_dma_prev = p.dma("sync", qef[hp * 128:(hp + 1) * 128, :], qefp[:], sem=s_out,
                             waits=[tp_evac_prev])

    if bg:
        bg.p1(99)
    p.dma("sync", kvpo, kv_all[:], sem=s_out, waits=[kv_prev])
    p._waits("sync", [p.dma_all(s_out)])


def st2_body(p, g, C, T, bg=None):
    nc = p.nc
    NT = T // 128

    x = g["xin"].ap()
    qefd = g["qef"]
    kvp0 = g["kvsum"].ap()
    wprojT = g["wprojT"]
    bp_row = g["bp_row"].ap()
    eye = g["eye"].ap()
    x1o = g["x1"].ap()

    qefs = C.sb("qefs", [128, 8 * T], BF16)
    kva = C.sb("kva", [128, 8 * 65], F32)
    kvbf = C.sb("kvbf", [128, 8 * 65], BF16)
    wpb = C.sb("wpb", [128, 8 * D], BF16)
    bpr = C.sb("bpr", [1, D], BF16)
    eyes = C.sb("eyes", [128, 128], BF16)
    ones_r = C.sb("ones_r", [1, 128], BF16)
    att = C.sb("att", [128, NT * D], BF16)
    attf = C.sb("attf", [128, 8 * T], BF16)
    xs = [C.sb(f"xs{i}", [128, D], F32) for i in range(2)]
    x1s = C.sb("x1s", [128, D], F32)
    ctmp = C.sb("ctmp", [128, 6], F32)

    pb_o = [C.ps(f"pb_o{i}", [128, 65], F32) for i in range(2)]
    pb_tp = C.ps("pb_tp", [128, 1024], BF16)
    pb_g = [C.ps(f"pb_g{i}", [128, 512], F32) for i in range(2)]

    s_ld = p.dma_sem("ld")
    for hp in range(8):
        p.dma("sync", qefs[:, hp * T:(hp + 1) * T], qefd.ap()[hp * 128:(hp + 1) * 128, :], sem=s_ld)
    p.dma("sync", kva[:], kvp0, sem=s_ld)
    p.dma("sync", eyes[:], eye, sem=s_ld)
    t_ld = p.dma_all(s_ld)
    s_w = p.dma_sem("w")
    for kc in range(8):
        p.dma("gpsimd", wpb[:, kc * D:(kc + 1) * D],
              wprojT.ap()[kc * 128:(kc + 1) * 128, :], sem=s_w)
    p.dma("gpsimd", bpr[:], bp_row, sem=s_w)
    t_w = p.dma_all(s_w)
    t_on = p.op("vector", "memset", ones_r[:], 1.0)
    t_cz = p.op("vector", "memset", ctmp[:, 3:4], 0.0)

    if bg:
        bg.emit_threshold(C)
    t_kvb = p.op("vector", "tensor_copy", out=kvbf[:], in_=kva[:], waits=[t_ld])

    o_prev = [None, None]
    for hp in range(8):
        for t in range(NT):
            toks = []
            for half in range(2):
                wv = [t_ld, t_kvb]
                if o_prev[half] is not None:
                    wv.append(o_prev[half])
                tm = p.op("tensor", "matmul", out=pb_o[half][:],
                          lhsT=qefs[half * 64:(half + 1) * 64, hp * T + t * 128: hp * T + (t + 1) * 128],
                          rhs=kvbf[half * 64:(half + 1) * 64, hp * 65:(hp + 1) * 65],
                          start=True, stop=True, tile_position=(half * 64, 0), waits=wv)
                toks.append(tm)
            t1 = p.op("vector", "tensor_scalar", out=ctmp[:, 0:1], in0=pb_o[0][:, 64:65],
                      scalar1=1e-6, scalar2=None, op0=ALU.add, waits=[toks[0]])
            t2 = p.op("vector", "tensor_scalar", out=ctmp[:, 1:2], in0=pb_o[1][:, 64:65],
                      scalar1=1e-6, scalar2=None, op0=ALU.add, waits=[toks[1]])
            t3 = p.op("vector", "reciprocal", out=ctmp[:, 2:3], in_=ctmp[:, 0:1], waits=[t1, t2])
            t3b = p.op("vector", "reciprocal", out=ctmp[:, 5:6], in_=ctmp[:, 1:2], waits=[t3])
            rc = [t3, t3b]
            for half in range(2):
                h = 2 * hp + half
                te = p.op("scalar", "activation",
                          out=att[:, t * D + h * 64: t * D + (h + 1) * 64],
                          in_=pb_o[half][:, 0:64], func=AF.Identity,
                          bias=ctmp[:, 3:4],
                          scale=ctmp[:, (2 if half == 0 else 5):(3 if half == 0 else 6)],
                          waits=[rc[half], t_cz])
                o_prev[half] = te
        if bg:
            bg.p2(2)
    t_att = p.tok("scalar")

    tp_prev = None
    for t in range(NT):
        for c in range(8):
            wv = [t_att]
            if tp_prev is not None:
                wv.append(tp_prev)
            tt = p.op("tensor", "transpose", out=pb_tp[:, c * 128:(c + 1) * 128],
                      in_=att[:, t * D + c * 128: t * D + (c + 1) * 128],
                      identity=eyes[:], waits=(wv if c == 0 else []))
        dst = bass.AP(tensor=attf, offset=t * 128,
                      ap=[[8 * T, 128], [T, 8], [1, 128]])
        src = bass.AP(tensor=pb_tp, offset=0, ap=[[1024, 128], [128, 8], [1, 128]])
        tp_prev = p.op("vector", "tensor_copy", out=dst, in_=src, waits=[tt])
    t_attf = p.tok("vector")

    s_x = [p.dma_sem("x0"), p.dma_sem("x1b")]
    s_out = p.dma_sem("out")
    x_free = [None, None]
    g_prev = [None, None]
    x1_dma_prev = None
    for t in range(NT):
        b = t % 2
        w = [x_free[b]] if x_free[b] else []
        t_x = p.dma("sync", xs[b][:], x.rearrange("(n p) d -> n p d", p=128)[t], sem=s_x[b], waits=w)
        etoks = []
        for dc in range(2):
            bank = pb_g[dc]
            wv = [t_attf, t_w]
            if g_prev[dc] is not None:
                wv.append(g_prev[dc])
            for kc in range(8):
                p.op_nosem("tensor", "matmul", out=bank[:],
                           lhsT=attf[:, kc * T + t * 128: kc * T + (t + 1) * 128],
                           rhs=wpb[:, kc * D + dc * 512: kc * D + (dc + 1) * 512],
                           start=(kc == 0), stop=False, waits=(wv if kc == 0 else []))
            tm = p.op("tensor", "matmul", out=bank[:], lhsT=ones_r[:],
                      rhs=bpr[0:1, dc * 512:(dc + 1) * 512], start=False, stop=True,
                      waits=[t_on])
            te = p.op("vector", "tensor_add", out=x1s[:, dc * 512:(dc + 1) * 512],
                      in0=pb_g[dc][:], in1=xs[b][:, dc * 512:(dc + 1) * 512],
                      waits=[tm, t_x, x1_dma_prev] if x1_dma_prev else [tm, t_x])
            g_prev[dc] = te
            etoks.append(te)
        x_free[b] = etoks[-1]
        x1_dma_prev = p.dma("sync", x1o.rearrange("(n p) d -> n p d", p=128)[t],
                            x1s[:], sem=s_out, waits=etoks)
        if bg:
            bg.p2(3)
    if bg:
        bg.p2(99)
    p._waits("sync", [p.dma_all(s_out)])


def st3_body(p, g, C, T):
    nc = p.nc
    NT = T // 128

    x = g["x1"].ap()
    wternU = g["wternU"]
    wternD = g["wternD"]
    eye = g["eye"].ap()
    x2o = g["x2"].ap()

    wub = C.sb("wub", [128, 8 * DFF], BF16)
    wdb = C.sb("wdb", [128, 32 * D], BF16)
    eyes = C.sb("eyes", [128, 128], BF16)
    xs = [C.sb(f"xs{i}", [128, D], F32) for i in range(2)]
    lnb = C.sb("lnb", [128, D], BF16)
    ffn_f = C.sb("ffn_f", [128, 8 * 128], BF16)
    hgel = C.sb("hgel", [128, DFF], F32)
    hb = C.sb("hb", [128, DFF], BF16)
    hfb = C.sb("hfb", [128, 32 * 128], BF16)
    x2s = C.sb("x2s", [128, D], F32)
    sb = {
        "bnr": C.sb("bnr", [128, 48], F32),
        "agg": C.sb("agg", [128, 2], F32),
        "rs": C.sb("rs", [128, 1], F32),
        "bi": C.sb("bi", [128, 1], F32),
        "cst": C.sb("cst", [128, 4], F32),
    }
    sc2 = C.sb("sc2", [128, 6], F32)

    pb_tp = C.ps("pb_tp", [128, 1024], BF16)
    pb_u = [C.ps(f"pb_u{i}", [128, 512], F32) for i in range(2)]
    pb_d = [C.ps(f"pb_d{i}", [128, 512], F32) for i in range(2)]

    t_c = p.op("vector", "memset", sb["cst"][:, 0:1], 1e-5)
    t_c2 = p.op("vector", "memset", sb["cst"][:, 1:2], 0.0)
    s_ms = p.dma_sem("misc")
    p.dma("sync", eyes[:], eye, sem=s_ms)
    t_ms = p.dma_all(s_ms)

    s_wl = [p.dma_sem("wla"), p.dma_sem("wlb")]
    t_wub = []
    for dc in range(8):
        for kc in range(8):
            p.dma("sync" if kc % 2 == 0 else "gpsimd",
                  wub[:, kc * DFF + dc * 512: kc * DFF + (dc + 1) * 512],
                  wternU.ap()[kc * 128:(kc + 1) * 128, dc * 512:(dc + 1) * 512],
                  sem=s_wl[kc % 2])
        t_wub.append((p.dma_all(s_wl[0]), p.dma_all(s_wl[1])))
    t_wdb = []
    for dc in range(2):
        for kc in range(32):
            p.dma("sync" if kc % 2 == 0 else "gpsimd",
                  wdb[:, kc * D + dc * 512: kc * D + (dc + 1) * 512],
                  wternD.ap()[kc * 128:(kc + 1) * 128, dc * 512:(dc + 1) * 512],
                  sem=s_wl[kc % 2])
        t_wdb.append((p.dma_all(s_wl[0]), p.dma_all(s_wl[1])))

    s_x = [p.dma_sem("x0"), p.dma_sem("x1b")]
    s_out = p.dma_sem("out")
    x_free = [None, None]
    tp_prev = None
    u_prev = [None, None]
    d_prev = [None, None]
    hg_prev = None
    out_prev = None
    for t in range(NT):
        b = t % 2
        w = [x_free[b]] if x_free[b] else []
        t_x = p.dma("sync", xs[b][:], x.rearrange("(n p) d -> n p d", p=128)[t],
                    sem=s_x[b], waits=w)
        t_rs, t_bi, t_agg = ln_stats(p, xs[b], sb, [t_c, t_c2], [t_x, t_c, t_c2])
        t_q = p.op("scalar", "activation", out=sc2[:, 0:1], in_=sb["agg"][:, 1:2],
                   func=AF.Sqrt, bias=sb["cst"][:, 1:2], scale=1.0, waits=[t_agg, t_rs])
        t_rm = p.op("vector", "tensor_mul", out=sc2[:, 1:2], in0=sc2[:, 0:1],
                    in1=sb["rs"][:], waits=[t_q])
        t_rp = p.op("vector", "tensor_scalar", out=sc2[:, 1:2], in0=sc2[:, 1:2],
                    scalar1=1e-8, scalar2=None, op0=ALU.add, waits=[t_rm])
        t_rc = p.op("vector", "reciprocal", out=sc2[:, 2:3], in_=sc2[:, 1:2], waits=[t_rp])
        t_sc = p.op("vector", "tensor_mul", out=sc2[:, 3:4], in0=sc2[:, 2:3],
                    in1=sb["rs"][:], waits=[t_rc])
        t_sb = p.op("vector", "scalar_tensor_tensor", out=sc2[:, 4:5],
                    in0=sb["agg"][:, 0:1], scalar=-1.0, in1=sc2[:, 3:4],
                    op0=ALU.mult, op1=ALU.mult, waits=[t_sc, t_agg])
        w_ap = [t_sc, t_sb, t_x] + ([tp_prev] if tp_prev else [])
        t_ap = p.op("scalar", "activation", out=lnb[:], in_=xs[b][:], func=AF.Identity,
                    bias=sc2[:, 4:5], scale=sc2[:, 3:4], waits=w_ap)
        x_free[b] = t_ap
        for c in range(8):
            wv = [t_ap, t_ms] + ([tp_prev] if tp_prev else [])
            tt = p.op("tensor", "transpose", out=pb_tp[:, c * 128:(c + 1) * 128],
                      in_=lnb[:, c * 128:(c + 1) * 128], identity=eyes[:],
                      waits=(wv if c == 0 else []))
        t_ev = p.op("vector", "tensor_copy", out=ffn_f[:],
                    in_=bass.AP(tensor=pb_tp, offset=0, ap=[[1024, 128], [1, 1024]]),
                    waits=[tt])
        ge = []
        for dc in range(8):
            bank = pb_u[dc % 2]
            wv = [t_ev, t_wub[dc][0], t_wub[dc][1]] + ([u_prev[dc % 2]] if u_prev[dc % 2] else [])
            for kc in range(8):
                tok = (p.op if kc == 7 else p.op_nosem)(
                    "tensor", "matmul", out=bank[:],
                    lhsT=ffn_f[:, kc * 128:(kc + 1) * 128],
                    rhs=wub[:, kc * DFF + dc * 512: kc * DFF + (dc + 1) * 512],
                    start=(kc == 0), stop=(kc == 7), waits=(wv if kc == 0 else []))
            t_g = p.op("scalar", "activation", out=hgel[:, dc * 512:(dc + 1) * 512],
                       in_=bank[:], func=AF.Gelu, bias=sb["cst"][:, 1:2], scale=1.0,
                       waits=[tok, t_c2] + ([hg_prev] if hg_prev else []))
            u_prev[dc % 2] = t_g
            ge.append(t_g)
        st = []
        for i in range(8):
            st.append(p.op("vector", "bn_stats", out=sb["bnr"][:, 6 * i:6 * i + 6],
                           in_=hgel[:, i * 512:(i + 1) * 512], waits=[ge[i]]))
        t_ag = p.op("vector", "bn_aggr", out=sb["agg"][:, 0:2], in_=sb["bnr"][:, 0:48],
                    waits=st)
        t_m2 = p.op("vector", "scalar_tensor_tensor", out=sc2[:, 5:6],
                    in0=sb["agg"][:, 0:1], scalar=1.0, in1=sb["agg"][:, 0:1],
                    op0=ALU.mult, op1=ALU.mult, waits=[t_ag])
        t_v2 = p.op("vector", "tensor_add", out=sc2[:, 5:6], in0=sc2[:, 5:6],
                    in1=sb["agg"][:, 1:2], waits=[t_m2])
        t_q2 = p.op("scalar", "activation", out=sc2[:, 5:6], in_=sc2[:, 5:6],
                    func=AF.Sqrt, bias=sb["cst"][:, 1:2], scale=1.0, waits=[t_v2])
        t_p2 = p.op("vector", "tensor_scalar", out=sc2[:, 5:6], in0=sc2[:, 5:6],
                    scalar1=1e-8, scalar2=None, op0=ALU.add, waits=[t_q2])
        t_r2 = p.op("vector", "reciprocal", out=sc2[:, 5:6], in_=sc2[:, 5:6], waits=[t_p2])
        t_hb = p.op("vector", "tensor_copy", out=hb[:], in_=hgel[:], waits=ge + [t_ag])
        hg_prev = t_hb
        t_hf = []
        for bt in range(4):
            wv = [t_hb] + ([t_ev2] if bt > 0 else [])
            for i in range(8):
                c = bt * 8 + i
                tt = p.op("tensor", "transpose", out=pb_tp[:, i * 128:(i + 1) * 128],
                          in_=hb[:, c * 128:(c + 1) * 128], identity=eyes[:],
                          waits=(wv if i == 0 else []))
            t_ev2 = p.op("vector", "tensor_copy",
                         out=hfb[:, bt * 1024:(bt + 1) * 1024],
                         in_=bass.AP(tensor=pb_tp, offset=0, ap=[[1024, 128], [1, 1024]]),
                         waits=[tt])
            t_hf.append(t_ev2)
        etoks = []
        for dc in range(2):
            bank = pb_d[dc]
            wv = t_hf + [t_wdb[dc][0], t_wdb[dc][1]] + ([d_prev[dc]] if d_prev[dc] else [])
            for kc in range(32):
                tok = (p.op if kc == 31 else p.op_nosem)(
                    "tensor", "matmul", out=bank[:],
                    lhsT=hfb[:, kc * 128:(kc + 1) * 128],
                    rhs=wdb[:, kc * D + dc * 512: kc * D + (dc + 1) * 512],
                    start=(kc == 0), stop=(kc == 31), waits=(wv if kc == 0 else []))
            te = p.op("vector", "scalar_tensor_tensor",
                      out=x2s[:, dc * 512:(dc + 1) * 512], in0=bank[:],
                      scalar=sc2[:, 5:6], in1=xs[b][:, dc * 512:(dc + 1) * 512],
                      op0=ALU.mult, op1=ALU.add,
                      waits=[tok, t_r2, t_x] + ([out_prev] if out_prev else []))
            d_prev[dc] = te
            etoks.append(te)
        x_free[b] = etoks[-1]
        out_prev = p.dma("sync", x2o.rearrange("(n p) d -> n p d", p=128)[t],
                         x2s[:], sem=s_out, waits=etoks)
    p._waits("sync", [p.dma_all(s_out)])


def st4a_body(p, g, C, T):
    nc = p.nc
    NT = T // 128
    NCH = T // 64
    NP = NCH // 2
    QB = min(4, NP)
    NBB = NP // QB
    TE = T + 1

    x = g["x2"].ap()
    xh = g["xhg"].ap()[0:1, :]
    wmT = g["wmT"]
    alog = g["alog"]
    dtb = g["dtb"]
    eye = g["eye"].ap()
    maskT = g["maskT"].ap()

    gro = g["gro"].ap()
    edfo = g["edf"].ap()
    ln2fo = g["ln2f"].ap()
    lnfx_o = g["lnfx"].ap()
    vbet_o = g["vbet"].ap()
    wkdw_o = g["wkdw"].ap()
    dcd = g["dcd"]
    gd = nc.dram_tensor("gd", [H, NCH], F32, kind="Internal")

    xs = [C.sb(f"xs{i}", [128, D], F32) for i in range(2)]
    lnb = C.sb("lnb", [128, D], BF16)
    ln2fx = C.sb("ln2fx", [128, 8 * TE], BF16)
    wmb = C.sb("wmb", [128, 8 * 1072], BF16)
    vbet = C.sb("vbet_s", [128, NT * D], BF16)
    wkdw = C.sb("wkdw_s", [128, NT * D], BF16)
    gr = C.sb("gr", [128, NT * 48], F32)
    eyes = C.sb("eyes", [128, 128], BF16)
    eyesF = C.sb("eyesF", [128, 128], F32)
    maskS = C.sb("maskS", [128, 64], F32)
    dtbS = C.sb("dtbS", [128, H], F32)
    negea = C.sb("negea", [128, H], F32)
    dect = C.sb("dect", [128, NT * H], F32)
    decF = C.sb("decF", [16, T], F32)
    dcum = C.sb("dcum", [16, T], F32)
    ones16 = C.sb("ones16", [16, 64], F32)
    gF = C.sb("gF", [16, NCH], F32)
    sendF = C.sb("sendF", [16, NCH + 1], F32)
    fmscr = C.sb("fmscr", [16, T], F32)
    dcumt = g["sh_dcumt"]
    dext = g["sh_dext"]
    edft = C.sb("edft", [128, NT * H], F32)
    dwt = C.sb("dwt", [128, NT * H], F32)
    gcols = g["sh_gcols"]
    hstat = C.sb("hstat", [1, 16], F32)
    sb = {
        "bnr": C.sb("bnr", [128, 12], F32),
        "agg": C.sb("agg", [128, 2], F32),
        "rs": C.sb("rs", [128, 1], F32),
        "bi": C.sb("bi", [128, 1], F32),
        "cst": C.sb("cst", [128, 4], F32),
    }
    scr = C.sb("scr", [128, 48], F32)

    pb_tp = C.ps("pb_tp", [128, 1024], BF16)
    pbF = [C.ps(f"pbF{i}", [128, 512], F32) for i in range(7)]
    pb_v = [pbF[0], pbF[1]]
    pb_gt = pbF[5][:, 0:48]
    pb_dt = pbF[5][:, 256:384]

    s_w = p.dma_sem("w")
    for kc in range(8):
        p.dma("gpsimd", wmb[:, kc * 1072:(kc + 1) * 1072],
              wmT.ap()[kc * 128:(kc + 1) * 128, :], sem=s_w)
    t_w = p.dma_all(s_w)
    s_ms = p.dma_sem("misc")
    p.dma("sync", eyes[:], eye, sem=s_ms)
    p.dma("sync", maskS[0:64, :], maskT, sem=s_ms)
    p.dma("sync", maskS[64:128, :], maskT, sem=s_ms)
    p.dma("sync", dtbS[:], bass.AP(tensor=dtb, offset=0, ap=[[0, 128], [1, H]]), sem=s_ms)
    p.dma("sync", negea[:], bass.AP(tensor=alog, offset=0, ap=[[0, 128], [1, H]]), sem=s_ms)
    s_hx = p.dma_sem("hx")
    t_ms = p.dma_all(s_ms)
    t_c = p.op("vector", "memset", sb["cst"][:, 0:1], 1e-5)
    t_c2 = p.op("vector", "memset", sb["cst"][:, 1:2], 0.0)
    t_c3 = p.op("vector", "memset", sb["cst"][:, 2:3], 1.0)
    t_o16 = p.op("vector", "memset", ones16[:], 1.0)
    t_eyf = p.op("vector", "tensor_copy", out=eyesF[:], in_=eyes[:], waits=[t_ms])
    t_snd = p.op("vector", "memset", sendF[:, 0:1], 0.0)
    t_na = p.op("scalar", "activation", out=negea[:], in_=negea[:], func=AF.Exp,
                bias=sb["cst"][:, 1:2], scale=1.0, waits=[t_ms, t_c2])
    t_na = p.op("vector", "tensor_scalar", out=negea[:], in0=negea[:],
                scalar1=-1.0, scalar2=None, op0=ALU.mult, waits=[t_na])

    xhs = C.sb("xhs", [1, D], F32)
    lnh = C.sb("lnh", [1, D], BF16)
    hms = C.sb("hms", [1, 1], F32)
    t_xh0 = p.dma("sync", xhs[:], xh, sem=s_hx)
    t_hm = p.dma("sync", hms[:], g["hmask"].ap(), sem=s_hx)
    t_xh = p.op("scalar", "activation", out=xhs[:], in_=xhs[:], func=AF.Identity,
                bias=sb["cst"][0:1, 1:2], scale=hms[0:1, 0:1],
                waits=[t_xh0, t_hm, t_c2])
    h1 = p.op("vector", "bn_stats", out=hstat[0:1, 0:6], in_=xhs[0:1, 0:512], waits=[t_xh])
    h2 = p.op("vector", "bn_stats", out=hstat[0:1, 6:12], in_=xhs[0:1, 512:1024], waits=[h1])
    h3 = p.op("vector", "bn_aggr", out=hstat[0:1, 12:14], in_=hstat[0:1, 0:12], waits=[h2])
    h4 = p.op("scalar", "activation", out=hstat[0:1, 14:15], in_=hstat[0:1, 13:14],
              func=AF.Ln, bias=sb["cst"][0:1, 0:1], scale=1.0, waits=[h3, t_c])
    h5 = p.op("scalar", "activation", out=hstat[0:1, 14:15], in_=hstat[0:1, 14:15],
              func=AF.Exp, bias=sb["cst"][0:1, 1:2], scale=-0.5, waits=[h4, t_c2])
    h6 = p.op("vector", "scalar_tensor_tensor", out=hstat[0:1, 15:16],
              in0=hstat[0:1, 12:13], scalar=-1.0, in1=hstat[0:1, 14:15],
              op0=ALU.mult, op1=ALU.mult, waits=[h3, h5])
    h7 = p.op("scalar", "activation", out=lnh[:], in_=xhs[:], func=AF.Identity,
              bias=hstat[0:1, 15:16], scale=hstat[0:1, 14:15], waits=[h5, h6])
    for c in range(8):
        tt = p.op("tensor", "transpose", out=pb_tp[:, 2 * c:2 * c + 1],
                  in_=lnh[0:1, c * 128:(c + 1) * 128], identity=eyes[0:1, 0:1],
                  waits=([h7, t_ms] if c == 0 else []))
    hdst = bass.AP(tensor=ln2fx, offset=0, ap=[[8 * TE, 128], [TE, 8], [1, 1]])
    t_hev = p.op("vector", "tensor_copy", out=hdst,
                 in_=bass.AP(tensor=pb_tp, offset=0, ap=[[1024, 128], [2, 8]]),
                 waits=[tt])

    s_x = [p.dma_sem("x0"), p.dma_sem("x1b")]
    x_free = [None, None]
    tp_prev = t_hev
    v_prev = [None, None]
    gt_prev = None
    ln_done = []
    for t in range(NT):
        b = t % 2
        w = [x_free[b]] if x_free[b] else []
        t_x = p.dma("sync", xs[b][:], x.rearrange("(n p) d -> n p d", p=128)[t],
                    sem=s_x[b], waits=w)
        t_rs, t_bi, _ = ln_stats(p, xs[b], sb, [t_c, t_c2], [t_x, t_c, t_c2])
        w_ap = [t_rs, t_bi, t_x, tp_prev]
        t_ap = p.op("scalar", "activation", out=lnb[:], in_=xs[b][:], func=AF.Identity,
                    bias=sb["bi"][:, 0:1], scale=sb["rs"][:, 0:1], waits=w_ap)
        x_free[b] = t_ap
        for c in range(8):
            tt = p.op("tensor", "transpose", out=pb_tp[:, c * 128:(c + 1) * 128],
                      in_=lnb[:, c * 128:(c + 1) * 128], identity=eyes[:],
                      waits=([t_ap, t_ms, tp_prev] if c == 0 else []))
        dst = bass.AP(tensor=ln2fx, offset=t * 128 + 1,
                      ap=[[8 * TE, 128], [TE, 8], [1, 128]])
        src = bass.AP(tensor=pb_tp, offset=0, ap=[[1024, 128], [128, 8], [1, 128]])
        t_ev = p.op("vector", "tensor_copy", out=dst, in_=src, waits=[tt])
        tp_prev = t_ev
        ln_done.append(t_ev)
        wvg = [t_ev, t_w] + ([gt_prev] if gt_prev else [])
        for kc in range(8):
            tokg = (p.op if kc == 7 else p.op_nosem)(
                "tensor", "matmul", out=pb_gt,
                lhsT=ln2fx[:, kc * TE + t * 128 + 1: kc * TE + (t + 1) * 128 + 1],
                rhs=wmb[:, kc * 1072 + 1024: kc * 1072 + 1072],
                start=(kc == 0), stop=(kc == 7), waits=(wvg if kc == 0 else []))
        t_gev = p.op("vector", "tensor_copy", out=gr[:, t * 48:(t + 1) * 48],
                     in_=pb_gt, waits=[tokg])
        gt_prev = t_gev
        t_be = p.op("scalar", "activation", out=scr[:, 0:16], in_=gr[:, t * 48 + 16: t * 48 + 32],
                    func=AF.Sigmoid, bias=sb["cst"][:, 1:2], scale=1.0, waits=[t_gev, t_c2])
        t_z = p.op("vector", "tensor_add", out=scr[:, 16:32],
                   in0=gr[:, t * 48 + 32: t * 48 + 48], in1=dtbS[:], waits=[t_gev, t_ms])
        t_az = p.op("scalar", "activation", out=scr[:, 32:48], in_=scr[:, 16:32],
                    func=AF.Abs, bias=sb["cst"][:, 1:2], scale=1.0, waits=[t_z])
        t_ex = p.op("scalar", "activation", out=scr[:, 32:48], in_=scr[:, 32:48],
                    func=AF.Exp, bias=sb["cst"][:, 1:2], scale=-1.0, waits=[t_az])
        t_ln = p.op("scalar", "activation", out=scr[:, 32:48], in_=scr[:, 32:48],
                    func=AF.Ln, bias=sb["cst"][:, 2:3], scale=1.0, waits=[t_ex, t_c3])
        t_rl = p.op("vector", "tensor_scalar_max", out=scr[:, 16:32], in0=scr[:, 16:32],
                    scalar1=0.0, waits=[t_ln])
        t_sp = p.op("vector", "tensor_add", out=scr[:, 16:32], in0=scr[:, 16:32],
                    in1=scr[:, 32:48], waits=[t_rl])
        t_dc = p.op("vector", "tensor_mul", out=dect[:, t * H:(t + 1) * H],
                    in0=scr[:, 16:32], in1=negea[:], waits=[t_sp, t_na])
        for dc in range(2):
            bank = pb_v[dc]
            wv = [t_ev, t_w] + ([v_prev[dc]] if v_prev[dc] else [])
            for kc in range(8):
                tokv = (p.op if kc == 7 else p.op_nosem)(
                    "tensor", "matmul", out=bank[:],
                    lhsT=ln2fx[:, kc * TE + t * 128 + 1: kc * TE + (t + 1) * 128 + 1],
                    rhs=wmb[:, kc * 1072 + dc * 512: kc * 1072 + (dc + 1) * 512],
                    start=(kc == 0), stop=(kc == 7), waits=(wv if kc == 0 else []))
            bview = bass.AP(tensor=scr, offset=dc * 8,
                            ap=[[48, 128], [1, 8], [0, 64]])
            t_vb = p.op("vector", "tensor_tensor",
                        out=vbet[:, t * D + dc * 512: t * D + (dc + 1) * 512],
                        in0=bank[:], in1=bview, op=ALU.mult, waits=[tokv, t_be])
            v_prev[dc] = t_vb
    t_lnall = p.tok("vector")

    dt_prev = None
    for t in range(NT):
        wv = [p.tok("vector"), t_ms] + ([dt_prev] if dt_prev else [])
        tt = p.op("tensor", "transpose", out=pbF[5][0:16, 256:384],
                  in_=dect[:, t * H:(t + 1) * H], identity=eyesF[:],
                  waits=wv + [t_eyf])
        dt_prev = p.op("vector", "tensor_copy", out=decF[:, t * 128:(t + 1) * 128],
                       in_=pbF[5][0:16, 256:384], waits=[tt])
    sc_prev = []
    for ch in range(NCH):
        tsc = p.op("vector", "tensor_tensor_scan", out=dcum[:, ch * 64:(ch + 1) * 64],
                   data0=ones16[:], data1=decF[:, ch * 64:(ch + 1) * 64],
                   initial=0.0, op0=ALU.mult, op1=ALU.add,
                   waits=[dt_prev, t_o16] if ch == 0 else [])
        sc_prev.append(tsc)
    ends = bass.AP(tensor=dcum, offset=63, ap=[[T, 16], [64, NCH]])
    t_g = p.op("scalar", "activation", out=gF[:], in_=ends, func=AF.Exp,
               bias=sb["cst"][0:16, 1:2], scale=1.0, waits=sc_prev[-1:] + sc_prev[:1])
    t_se = p.op("vector", "tensor_tensor_scan", out=sendF[:, 1:NCH + 1],
                data0=ones16[:, 0:NCH], data1=ends, initial=0.0,
                op0=ALU.mult, op1=ALU.add, waits=[sc_prev[-1], t_snd])
    oview = bass.AP(tensor=sendF, offset=0, ap=[[NCH + 1, 16], [1, NCH], [0, 64]])
    endsb = bass.AP(tensor=dcum, offset=63, ap=[[T, 16], [64, NCH], [0, 64]])

    def fm_to_tok(srcF, dstT, wtok, tt_prev):
        for t in range(NT):
            wv = [wtok, t_ms, t_eyf] + ([tt_prev] if tt_prev else [])
            tt = p.op("tensor", "transpose", out=pbF[5][0:128, 256:272],
                      in_=srcF[:, t * 128:(t + 1) * 128], identity=eyesF[0:16, 0:16],
                      waits=wv)
            tt_prev = p.op("vector", "tensor_copy", out=dstT[:, t * H:(t + 1) * H],
                           in_=pbF[5][0:128, 256:272], waits=[tt])
        return tt_prev, tt

    tt_prev, tlast = fm_to_tok(dcum, dcumt, sc_prev[-1], None)
    t_1 = p.op("vector", "tensor_add", out=fmscr[:], in0=dcum[:], in1=oview,
               waits=[t_se] + sc_prev[-1:])
    t_2 = p.op("scalar", "activation", out=fmscr[:], in_=fmscr[:], func=AF.Exp,
               bias=sb["cst"][0:16, 1:2], scale=1.0, waits=[t_1])
    tt_prev, tlast = fm_to_tok(fmscr, edft, t_2, tt_prev)
    t_3 = p.op("scalar", "activation", out=fmscr[:], in_=dcum[:], func=AF.Exp,
               bias=sb["cst"][0:16, 1:2], scale=1.0, waits=[t_2, (tlast)])
    tt_prev, tlast = fm_to_tok(fmscr, dext, t_3, tt_prev)
    t_4 = p.op("vector", "tensor_sub", out=fmscr[:], in0=dcum[:], in1=endsb,
               waits=[t_1, (tlast)])
    t_5 = p.op("scalar", "activation", out=fmscr[:], in_=fmscr[:], func=AF.Exp,
               bias=sb["cst"][0:16, 1:2], scale=-1.0, waits=[t_4])
    tt_prev, tlast = fm_to_tok(fmscr, dwt, t_5, tt_prev)
    t_decall = tt_prev

    wk_prev = None
    for t in range(NT):
        for c in range(8):
            tt = p.op("tensor", "transpose", out=pb_tp[:, c * 128:(c + 1) * 128],
                      in_=ln2fx[:, c * TE + t * 128: c * TE + (t + 1) * 128],
                      identity=eyes[:],
                      waits=([t_lnall, t_hev, t_ms] + ([wk_prev] if wk_prev else [])
                             if c == 0 else []))
        dwview = bass.AP(tensor=dwt, offset=t * H, ap=[[NT * H, 128], [1, H], [0, 64]])
        wk_prev = p.op("vector", "tensor_tensor", out=wkdw[:, t * D:(t + 1) * D],
                       in0=bass.AP(tensor=pb_tp, offset=0, ap=[[1024, 128], [1, 1024]]),
                       in1=dwview, op=ALU.mult, waits=[tt, t_decall])
    t_wkall = wk_prev

    s_d1 = p.dma_sem("dcumd")
    s_d2 = p.dma_sem("gdd")
    t_d1 = p.dma("sync", dcd.ap(), dcum[:], sem=s_d1, waits=[p.tok("scalar"), p.tok("vector")])
    t_d2 = p.dma("sync", gd.ap(), gF[:], sem=s_d2, waits=[t_g])
    t_gc0 = p.dma("sync", gcols[0:64, :],
                  bass.AP(tensor=gd, offset=0, ap=[[0, 64], [2 * NCH, 8], [1, NCH]]),
                  sem=s_d2, waits=[t_d2])
    t_gc1 = p.dma("sync", gcols[64:128, :],
                  bass.AP(tensor=gd, offset=NCH, ap=[[0, 64], [2 * NCH, 8], [1, NCH]]),
                  sem=s_d2, waits=[t_gc0])
    t_gc = p.dma_all(s_d2)

    s_out = p.dma_sem("out")
    for c in range(8):
        p.dma("sync", lnfx_o[:, c * TE:(c + 1) * TE], ln2fx[:, c * TE:(c + 1) * TE],
              sem=s_out, waits=[t_lnall, t_hev])
        p.dma("sync", ln2fo[c * 128:(c + 1) * 128, :],
              ln2fx[:, c * TE + 1:(c + 1) * TE], sem=s_out, waits=[t_lnall, t_hev])
    for t in range(0, NT, 2):
        p.dma("sync", vbet_o[:, t * D:(t + 2) * D], vbet[:, t * D:(t + 2) * D],
              sem=s_out, waits=[t_lnall])
        p.dma("gpsimd", wkdw_o[:, t * D:(t + 2) * D], wkdw[:, t * D:(t + 2) * D],
              sem=s_out, waits=[t_wkall])
    p.dma("sync", edfo, edft[:], sem=s_out, waits=[t_decall])
    p.dma("sync", gro, gr[:], sem=s_out, waits=[t_lnall])
    p._waits("sync", [p.dma_all(s_out), p.dma_all(s_d1), p.dma_all(s_d2)])
    p._waits("gpsimd", [p.dma_all(s_out)])


def st4b_body(p, g, C, T):
    nc = p.nc
    NT = T // 128
    NCH = T // 64
    NP = NCH // 2
    QB = min(4, NP)
    NBB = NP // QB
    TE = T + 1

    dcd = g["dcd"]
    maskT = g["maskT"].ap()

    oo = g["o_local"].ap()
    sfo = g["sfin"].ap()

    ln2fx = C.sb("ln2fx", [128, 8 * TE], BF16)
    vbet = C.sb("vbet_s", [128, NT * D], BF16)
    wkdw = C.sb("wkdw_s", [128, NT * D], BF16)
    dcumt = g["sh_dcumt"]
    dext = g["sh_dext"]
    gcols = g["sh_gcols"]
    maskS = C.sb("maskS", [128, 64], F32)
    spair = C.sb("spair", [128, 8 * 64], F32)
    spbf = C.sb("spbf", [128, 8 * 64], BF16)
    dcb = [C.sb(f"dcb{i}", [128, T], F32) for i in range(2)]
    e2m = [C.sb(f"e2m{i}", [128, T // 2], F32) for i in range(2)]
    itg = C.sb("itg", [128, NBB * QB * 128], BF16)
    ostg = [C.sb(f"ostg{i}", [128, 512], F32) for i in range(2)]
    ostg2 = [C.sb(f"ostg2{i}", [128, 512], F32) for i in range(2)]
    cst = C.sb("cst", [128, 2], F32)

    pbF = [C.ps(f"pbF{i}", [128, 512], F32) for i in range(7)]
    pb_os = [pbF[0], pbF[1]]
    pb_io = [pbF[2], pbF[3]]
    pb_sc = pbF[4]
    pb_su = [pbF[5][:, 64:128], pbF[6][:, 0:64]]

    lnfx_d = g["lnfx"]
    vbet_d = g["vbet"]
    wkdw_d = g["wkdw"]
    s_w = p.dma_sem("w")
    for c in range(8):
        p.dma("gpsimd", ln2fx[:, c * TE:(c + 1) * TE],
              lnfx_d.ap()[:, c * TE:(c + 1) * TE], sem=s_w)
    for t in range(0, NT, 2):
        p.dma("gpsimd", vbet[:, t * D:(t + 2) * D], vbet_d.ap()[:, t * D:(t + 2) * D], sem=s_w)
        p.dma("gpsimd", wkdw[:, t * D:(t + 2) * D], wkdw_d.ap()[:, t * D:(t + 2) * D], sem=s_w)
    t_w = p.dma_all(s_w)
    s_ms = p.dma_sem("misc")
    p.dma("sync", maskS[0:64, :], maskT, sem=s_ms)
    p.dma("sync", maskS[64:128, :], maskT, sem=s_ms)
    t_ms = p.dma_all(s_ms)
    t_s0 = p.op("vector", "memset", spair[:], 0.0)
    t_s0b = p.op("vector", "memset", spbf[:], 0.0)
    t_c2 = p.op("vector", "memset", cst[:, 1:2], 0.0)

    t_lnall = t_w
    t_decall = None
    t_wkall = t_w
    t_gc = None
    t_hev = t_w

    s_dc = [p.dma_sem("dcb0"), p.dma_sem("dcb1")]
    s_oout = p.dma_sem("oout")
    dcb_free = [None, None]
    e2_free = [None, None]
    itg_free = None
    os_free = [None, None]
    io_free = [None, None]
    su_free = [None, None]
    ost_free = [None, None]
    sc_free = None
    o_dma = []
    for j in range(8):
        tdc = []
        for half in range(2):
            h = 2 * j + half
            w = [dcb_free[half]]
            tdc.append(p.dma("sync", dcb[half][:],
                             bass.AP(tensor=dcd, offset=h * T, ap=[[0, 128], [1, T]]),
                             sem=s_dc[half], waits=[t for t in w if t]))
        e2tok = []
        dcb_last = [None, None]
        for half in range(2):
            h = 2 * j + half
            toks = []
            for par in range(2):
                dccview = bass.AP(tensor=dcb[half], offset=par * 64 * T + par * 64,
                                  ap=[[T, 64], [128, NP], [1, 64]])
                dckview = bass.AP(tensor=dcumt, offset=par * 64 * (NT * H) + h,
                                  ap=[[NT * H, 64], [H, NT], [0, 64]])
                wv = [tdc[half], t_decall] + ([e2_free[half]] if e2_free[half] else [])
                t1 = p.op("vector", "tensor_tensor",
                          out=e2m[half][par * 64:(par + 1) * 64, :],
                          in0=dccview, in1=dckview,
                          op=ALU.subtract, waits=wv)
                dcb_last[half] = t1
                mview = bass.AP(tensor=maskS, offset=par * 64 * 64,
                                ap=[[64, 64], [0, NP], [1, 64]])
                t1b = p.op("gpsimd", "tensor_tensor",
                           out=e2m[half][par * 64:(par + 1) * 64, :],
                           in0=e2m[half][par * 64:(par + 1) * 64, :], in1=mview,
                           op=ALU.add, waits=[t1, t_ms])
                t2 = p.op("scalar", "activation", out=e2m[half][par * 64:(par + 1) * 64, :],
                          in_=e2m[half][par * 64:(par + 1) * 64, :], func=AF.Exp,
                          bias=cst[par * 64:(par + 1) * 64, 1:2], scale=1.0,
                          waits=[t1b, t_c2])
                toks.append(t2)
            e2tok.append(toks)
        dcb_free = dcb_last
        it_toks = []
        sc_tok = sc_free
        for bb in range(NBB):
            for q in range(QB):
                m = bb * QB + q
                for par in range(2):
                    i = 2 * m + par
                    for half in range(2):
                        h = 2 * j + half
                        wv = [t_lnall, t_hev] + ([sc_tok] if sc_tok else [])
                        tm = p.op("tensor", "matmul",
                                  out=pb_sc[par * 64:(par + 1) * 64,
                                            q * 128 + half * 64: q * 128 + (half + 1) * 64],
                                  lhsT=ln2fx[half * 64:(half + 1) * 64,
                                             j * TE + i * 64: j * TE + (i + 1) * 64],
                                  rhs=ln2fx[half * 64:(half + 1) * 64,
                                            j * TE + i * 64 + 1: j * TE + (i + 1) * 64 + 1],
                                  start=True, stop=True,
                                  tile_position=(half * 64, par * 64), waits=wv)
            for half in range(2):
                h = 2 * j + half
                e2view = bass.AP(tensor=e2m[half], offset=bb * QB * 64,
                                 ap=[[T // 2, 128], [64, QB], [1, 64]])
                dst = bass.AP(tensor=itg, offset=bb * QB * 128 + half * 64,
                              ap=[[NBB * QB * 128, 128], [128, QB], [1, 64]])
                srcv = bass.AP(tensor=pb_sc, offset=half * 64,
                               ap=[[512, 128], [128, QB], [1, 64]])
                wv = [tm] + e2tok[half] + ([itg_free] if itg_free else [])
                t_it = p.op("vector", "tensor_tensor", out=dst, in0=srcv, in1=e2view,
                            op=ALU.mult, waits=wv)
                it_toks.append(t_it)
            sc_tok = it_toks[-1]
        sc_free = sc_tok
        e2_free = [it_toks[-2], it_toks[-1]]
        st_prev = [t_s0, t_s0b, t_s0b]
        for i in range(NCH):
            m, par = i // 2, i % 2
            bb, q = m // QB, m % QB
            ba = bb % 2
            for half in range(2):
                wv = [st_prev[-1], t_lnall] + ([os_free[ba]] if (i % (2 * QB) == 0 and os_free[ba]) else [])
                tos = p.op("tensor", "matmul",
                           out=pb_os[ba][par * 64:(par + 1) * 64,
                                         q * 128 + half * 64: q * 128 + (half + 1) * 64],
                           lhsT=ln2fx[half * 64:(half + 1) * 64,
                                      j * TE + i * 64 + 1: j * TE + (i + 1) * 64 + 1],
                           rhs=spbf[half * 64:(half + 1) * 64, j * 64:(j + 1) * 64],
                           start=True, stop=True,
                           tile_position=(half * 64, par * 64), waits=wv)
            for half in range(2):
                wv = it_toks + [t_lnall] + ([io_free[ba]] if (i % (2 * QB) == 0 and io_free[ba]) else [])
                tio = p.op("tensor", "matmul",
                           out=pb_io[ba][par * 64:(par + 1) * 64,
                                         q * 128 + half * 64: q * 128 + (half + 1) * 64],
                           lhsT=bass.AP(tensor=itg,
                                        offset=par * 64 * (NBB * QB * 128) + bb * QB * 128 + q * 128 + half * 64,
                                        ap=[[NBB * QB * 128, 64], [1, 64]]),
                           rhs=vbet[par * 64:(par + 1) * 64,
                                    m * 1024 + (2 * j + half) * 64: m * 1024 + (2 * j + half + 1) * 64],
                           start=True, stop=True,
                           tile_position=(par * 64, par * 64), waits=wv)
            for half in range(2):
                wv = [t_wkall, st_prev[-1]] + ([su_free[par]] if su_free[par] else [])
                tu = p.op("tensor", "matmul",
                          out=pb_su[par][half * 64:(half + 1) * 64, 0:64],
                          lhsT=wkdw[par * 64:(par + 1) * 64,
                                    m * 1024 + (2 * j + half) * 64: m * 1024 + (2 * j + half + 1) * 64],
                          rhs=vbet[par * 64:(par + 1) * 64,
                                   m * 1024 + (2 * j + half) * 64: m * 1024 + (2 * j + half + 1) * 64],
                          start=True, stop=True,
                          tile_position=(par * 64, half * 64), waits=wv)
            tst = p.op("vector", "scalar_tensor_tensor", out=spair[:, j * 64:(j + 1) * 64],
                       in0=spair[:, j * 64:(j + 1) * 64],
                       scalar=gcols[:, j * NCH + i: j * NCH + i + 1],
                       in1=pb_su[par], op0=ALU.mult, op1=ALU.add,
                       waits=[tu, tos, t_gc, st_prev[-1]])
            su_free[par] = tst
            tsh = p.op("vector", "tensor_copy", out=spbf[:, j * 64:(j + 1) * 64],
                       in_=spair[:, j * 64:(j + 1) * 64], waits=[tst])
            st_prev.append(tsh)
            if i % (2 * QB) == 2 * QB - 1:
                stoks = []
                for q2 in range(QB):
                    for half in range(2):
                        hcol = (2 * j + half)
                        t_sc1 = p.op("scalar", "activation",
                                     out=ostg[ba][:, q2 * 128 + half * 64: q2 * 128 + (half + 1) * 64],
                                     in_=pb_os[ba][:, q2 * 128 + half * 64: q2 * 128 + (half + 1) * 64],
                                     func=AF.Identity, bias=cst[:, 1:2],
                                     scale=dext[:, (bb * QB + q2) * H + hcol: (bb * QB + q2) * H + hcol + 1],
                                     waits=[tos, t_decall, t_c2] + ([ost_free[ba]] if ost_free[ba] else []))
                        stoks.append(t_sc1)
                t_add = p.op("vector", "tensor_add", out=ostg2[ba][:, 0:QB * 128],
                             in0=ostg[ba][:, 0:QB * 128],
                             in1=pb_io[ba][:, 0:QB * 128], waits=stoks + [tio])
                os_free[ba] = t_add
                io_free[ba] = t_add
                odst = bass.AP(tensor=oo.tensor, offset=(bb * QB) * 128 * D + (2 * j) * 64,
                               ap=[[D, 128], [128 * D, QB], [64, 2], [1, 64]])
                t_od = p.dma("scalar", odst, ostg2[ba][:, 0:QB * 128], sem=s_oout, waits=[t_add])
                ost_free[ba] = t_od
                o_dma.append(t_od)
        itg_free = p.tok("tensor")

    s_out = p.dma_sem("out")
    p.dma("sync", sfo, spair[:], sem=s_out, waits=[p.tok("vector")])
    p._waits("sync", [p.dma_all(s_out), p.dma_all(s_oout)])
    p._waits("scalar", [p.dma_all(s_oout)])



def st5_body(p, g, C, T):
    nc = p.nc
    NT = T // 128
    NCH = T // 64
    NP = NCH // 2
    QB = min(4, NP)
    NBB = NP // QB

    x = g["x2"].ap()
    od = g["o_local"]
    lnfd = g["ln2f"]
    grd = g["gro"].ap()
    edfd = g["edf"].ap()
    sind = g["sfg"].ap()[0:128, :]
    woutT = g["woutT"]
    eye = g["eye"].ap()
    x3o = g["x3"].ap()

    lnf = C.sb("lnf", [128, 8 * T], BF16)
    olo = C.sb("olo", [128, NT * D], F32)
    gr = C.sb("grs", [128, NT * 48], F32)
    gsig = C.sb("gsig", [128, NT * H], F32)
    edf = C.sb("edfs", [128, NT * H], F32)
    sinf = C.sb("sinf", [128, 8 * 64], F32)
    sinb = C.sb("sinb", [128, 8 * 64], BF16)
    wob = C.sb("wob", [128, 8 * D], BF16)
    eyes = C.sb("eyes", [128, 128], BF16)
    att = C.sb("att", [128, NT * D], BF16)
    attf = C.sb("attf", [128, 8 * T], BF16)
    ostg = [C.sb(f"ostg{i}", [128, 512], F32) for i in range(2)]
    xs = [C.sb(f"xs{i}", [128, D], F32) for i in range(2)]
    ln2s = C.sb("ln2s", [128, D], F32)
    x3s = C.sb("x3s", [128, D], F32)
    sb = {
        "bnr": C.sb("bnr", [128, 12], F32),
        "agg": C.sb("agg", [128, 2], F32),
        "rs": C.sb("rs", [128, 1], F32),
        "bi": C.sb("bi", [128, 1], F32),
        "cst": C.sb("cst", [128, 4], F32),
    }

    pb_c = [C.ps(f"pb_c{i}", [128, 512], F32) for i in range(2)]
    pb_tp = C.ps("pb_tp", [128, 1024], BF16)
    pb_g = [C.ps(f"pb_g{i}", [128, 512], F32) for i in range(2)]

    s_ld = p.dma_sem("ld")
    for c in range(8):
        p.dma("sync", lnf[:, c * T:(c + 1) * T], lnfd.ap()[c * 128:(c + 1) * 128, :], sem=s_ld)
    for t in range(NT):
        p.dma("sync", olo[:, t * D:(t + 1) * D],
              od.ap().rearrange("(n p) d -> n p d", p=128)[t], sem=s_ld)
    hmb = C.sb("hmb", [128, 1], F32)
    p.dma("sync", gr[:], grd, sem=s_ld)
    p.dma("sync", edf[:], edfd, sem=s_ld)
    p.dma("sync", sinf[:], sind, sem=s_ld)
    p.dma("sync", hmb[:], bass.AP(tensor=g["hmask"], offset=0, ap=[[0, 128], [1, 1]]),
          sem=s_ld)
    p.dma("sync", eyes[:], eye, sem=s_ld)
    t_ld = p.dma_all(s_ld)
    s_w = p.dma_sem("w")
    for kc in range(8):
        p.dma("gpsimd", wob[:, kc * D:(kc + 1) * D],
              woutT.ap()[kc * 128:(kc + 1) * 128, :], sem=s_w)
    t_w = p.dma_all(s_w)
    t_c = p.op("vector", "memset", sb["cst"][:, 0:1], 1e-5)
    t_c2 = p.op("vector", "memset", sb["cst"][:, 1:2], 0.0)
    t_gm = p.op("scalar", "activation", out=sinf[:], in_=sinf[:], func=AF.Identity,
                bias=sb["cst"][:, 1:2], scale=hmb[:, 0:1], waits=[t_ld, t_c2])
    t_sb = p.op("vector", "tensor_copy", out=sinb[:], in_=sinf[:], waits=[t_gm])
    gview_in = bass.AP(tensor=gr, offset=0, ap=[[NT * 48, 128], [48, NT], [1, 16]])
    gview_out = bass.AP(tensor=gsig, offset=0, ap=[[NT * H, 128], [16, NT], [1, 16]])
    t_gs = p.op("scalar", "activation", out=gview_out, in_=gview_in, func=AF.Sigmoid,
                bias=sb["cst"][:, 1:2], scale=1.0, waits=[t_ld, t_c2])

    c_free = [None, None]
    st_free = [None, None]
    att_prev = None
    for j in range(8):
        for bb in range(NBB):
            ba = bb % 2
            for q in range(QB):
                m = bb * QB + q
                for par in range(2):
                    i = 2 * m + par
                    for half in range(2):
                        wv = [t_ld, t_sb] + ([c_free[ba]] if (q == 0 and par == 0 and half == 0 and c_free[ba]) else [])
                        tm = p.op("tensor", "matmul",
                                  out=pb_c[ba][par * 64:(par + 1) * 64,
                                               q * 128 + half * 64: q * 128 + (half + 1) * 64],
                                  lhsT=lnf[half * 64:(half + 1) * 64,
                                           j * T + i * 64: j * T + (i + 1) * 64],
                                  rhs=sinb[half * 64:(half + 1) * 64, j * 64:(j + 1) * 64],
                                  start=True, stop=True,
                                  tile_position=(half * 64, par * 64), waits=wv)
            stoks = []
            for q2 in range(QB):
                for half in range(2):
                    hcol = 2 * j + half
                    t_s1 = p.op("scalar", "activation",
                                out=ostg[ba][:, q2 * 128 + half * 64: q2 * 128 + (half + 1) * 64],
                                in_=pb_c[ba][:, q2 * 128 + half * 64: q2 * 128 + (half + 1) * 64],
                                func=AF.Identity, bias=sb["cst"][:, 1:2],
                                scale=edf[:, (bb * QB + q2) * H + hcol: (bb * QB + q2) * H + hcol + 1],
                                waits=[tm, t_ld, t_c2] + ([st_free[ba]] if st_free[ba] else []))
                    stoks.append(t_s1)
            oview = bass.AP(tensor=olo, offset=(bb * QB) * D + (2 * j) * 64,
                            ap=[[NT * D, 128], [D, QB], [64, 2], [1, 64]])
            t_a1 = p.op("vector", "tensor_add", out=ostg[ba][:, 0:QB * 128],
                        in0=ostg[ba][:, 0:QB * 128], in1=oview, waits=stoks + [t_ld])
            gview = bass.AP(tensor=gsig, offset=(bb * QB) * H + 2 * j,
                            ap=[[NT * H, 128], [16, QB], [1, 2], [0, 64]])
            adst = bass.AP(tensor=att, offset=(bb * QB) * D + (2 * j) * 64,
                           ap=[[NT * D, 128], [D, QB], [64, 2], [1, 64]])
            t_a2 = p.op("vector", "tensor_tensor", out=adst, in0=ostg[ba][:, 0:QB * 128],
                        in1=gview, op=ALU.mult, waits=[t_a1, t_gs])
            c_free[ba] = t_a2
            st_free[ba] = t_a2
    t_att = p.tok("vector")

    tp_prev = None
    for t in range(NT):
        for c in range(8):
            wv = [t_att, t_ld] + ([tp_prev] if tp_prev else [])
            tt = p.op("tensor", "transpose", out=pb_tp[:, c * 128:(c + 1) * 128],
                      in_=att[:, t * D + c * 128: t * D + (c + 1) * 128],
                      identity=eyes[:], waits=(wv if c == 0 else []))
        dst = bass.AP(tensor=attf, offset=t * 128, ap=[[8 * T, 128], [T, 8], [1, 128]])
        srcv = bass.AP(tensor=pb_tp, offset=0, ap=[[1024, 128], [128, 8], [1, 128]])
        tp_prev = p.op("vector", "tensor_copy", out=dst, in_=srcv, waits=[tt])
    t_attf = p.tok("vector")

    s_x = [p.dma_sem("x0"), p.dma_sem("x1b")]
    s_out = p.dma_sem("out")
    x_free = [None, None]
    g_prev = [None, None]
    out_prev = None
    for t in range(NT):
        b = t % 2
        w = [x_free[b]] if x_free[b] else []
        t_x = p.dma("sync", xs[b][:], x.rearrange("(n p) d -> n p d", p=128)[t],
                    sem=s_x[b], waits=w)
        t_rs, t_bi, _ = ln_stats(p, xs[b], sb, [t_c, t_c2], [t_x, t_c, t_c2])
        t_ap = p.op("scalar", "activation", out=ln2s[:], in_=xs[b][:], func=AF.Identity,
                    bias=sb["bi"][:, 0:1], scale=sb["rs"][:, 0:1],
                    waits=[t_rs, t_bi, t_x] + ([out_prev] if out_prev else []))
        etoks = []
        for dc in range(2):
            bank = pb_g[dc]
            wv = [t_attf, t_w] + ([g_prev[dc]] if g_prev[dc] else [])
            for kc in range(8):
                tok = (p.op if kc == 7 else p.op_nosem)(
                    "tensor", "matmul", out=bank[:],
                    lhsT=attf[:, kc * T + t * 128: kc * T + (t + 1) * 128],
                    rhs=wob[:, kc * D + dc * 512: kc * D + (dc + 1) * 512],
                    start=(kc == 0), stop=(kc == 7), waits=(wv if kc == 0 else []))
            t_e1 = p.op("vector", "tensor_add", out=x3s[:, dc * 512:(dc + 1) * 512],
                        in0=bank[:], in1=xs[b][:, dc * 512:(dc + 1) * 512],
                        waits=[tok, t_x] + ([out_prev] if out_prev else []))
            t_e2 = p.op("vector", "tensor_add", out=x3s[:, dc * 512:(dc + 1) * 512],
                        in0=x3s[:, dc * 512:(dc + 1) * 512],
                        in1=ln2s[:, dc * 512:(dc + 1) * 512], waits=[t_e1, t_ap])
            g_prev[dc] = t_e2
            etoks.append(t_e2)
        x_free[b] = etoks[-1]
        out_prev = p.dma("sync", x3o.rearrange("(n p) d -> n p d", p=128)[t],
                         x3s[:], sem=s_out, waits=etoks)
    p._waits("sync", [p.dma_all(s_out)])


def st6_body(p, g, C, T):
    nc = p.nc
    NT = T // 128
    TT6 = min(512, T)
    NTT = T // TT6
    NTG = TT6 // 128
    NZC = 128

    x = g["x3"].ap()
    rmJ = g["rmJ"]
    rupJ = g["rupJ"]
    wcT = g["wcT"]
    eye = g["eye"].ap()
    yo = g["y"].ap()

    lnf = C.sb("lnf", [128, 8 * TT6], BF16)
    lnstg = C.sb("lnstg", [128, 1024], BF16)
    zb = C.sb("zb", [128, NZC * TT6], BF16)
    rms = C.sb("rms", [16, 256 * 16], BF16)
    rupq = C.sb("rupq", [16, 64 * 64], BF16)
    m1s = C.sb("m1s", [16, 2048], BF16)
    slab = C.sb("slab", [128, 32 * 128], BF16)
    ln3fd = nc.dram_tensor("ln3fd", [D, T], BF16, kind="Internal")
    m1d = nc.dram_tensor("m1d", [16, 256 * 64], BF16, kind="Internal")
    wcb = [C.sb(f"wcb{i}", [128, 4 * D], BF16) for i in range(2)]
    eyes = C.sb("eyes", [128, 128], BF16)
    xs = [C.sb(f"xs{i}", [128, D], F32) for i in range(2)]
    lnb = C.sb("lnb", [128, D], BF16)
    ys = [C.sb("ys0", [128, D], F32)] * 2
    sb = {
        "bnr": C.sb("bnr", [128, 12], F32),
        "agg": C.sb("agg", [128, 2], F32),
        "rs": C.sb("rs", [128, 1], F32),
        "bi": C.sb("bi", [128, 1], F32),
        "cst": C.sb("cst", [128, 4], F32),
    }

    pb_z = C.ps("pb_z", [128, 2048], F32)
    tp_guard = nc.psum_tensor("g_pb_tp", [128, 1024], BF16)
    pb_tp = tp_guard.__enter__()

    t_c = p.op("vector", "memset", sb["cst"][:, 0:1], 1e-5)
    t_c2 = p.op("vector", "memset", sb["cst"][:, 1:2], 0.0)
    s_ms = p.dma_sem("misc")
    p.dma("sync", eyes[:], eye, sem=s_ms)
    t_ms = p.dma_all(s_ms)
    s_r = p.dma_sem("rails")
    p.dma("gpsimd", rms[:],
          bass.AP(tensor=rmJ, offset=0, ap=[[16, 16], [256, 256], [1, 16]]), sem=s_r)
    t_r = p.dma_all(s_r)
    s_rq = p.dma_sem("railq")
    s_m1 = p.dma_sem("m1out")

    m1_prev = None
    t_md = None
    for q4 in range(4):
        wvq = [m1_prev] if m1_prev else []
        t_rq = p.dma("gpsimd", rupq[:],
                     bass.AP(tensor=rupJ, offset=q4 * 64 * 16 * 64,
                             ap=[[64, 16], [1024, 64], [1, 64]]), sem=s_rq, waits=wvq)
        for grp in range(8):
            for u in range(8):
                hrl = grp * 8 + u
                hr = q4 * 64 + hrl
                wv = [t_r, t_rq] + ([m1_prev] if m1_prev else [])
                tm = p.op("tensor", "matmul", out=pb_z[0:16, u * 64:(u + 1) * 64],
                          lhsT=rms[0:16, hr * 16:(hr + 1) * 16],
                          rhs=rupq[0:16, hrl * 64:(hrl + 1) * 64],
                          start=True, stop=True, waits=(wv if u == 0 else []))
            wv2 = [tm] + ([t_md] if (grp % 4 == 0 and t_md) else [])
            m1_prev = p.op("vector", "tensor_copy",
                           out=m1s[0:16, (grp % 4) * 512:(grp % 4 + 1) * 512],
                           in_=pb_z[0:16, 0:512], waits=wv2)
            if grp % 4 == 3:
                t_md = p.dma("sync", m1d.ap()[0:16, (q4 * 2 + grp // 4) * 2048:
                                              (q4 * 2 + grp // 4 + 1) * 2048],
                             m1s[0:16, :], sem=s_m1, waits=[m1_prev])
    t_z0 = p.op("vector", "memset", slab[:], 0.0)
    s_sl = p.dma_sem("slab")
    for g in range(4):
        for a in range(2):
            dst = bass.AP(tensor=slab, offset=(32 * g + 16 * a) * (32 * 128) + a * 64,
                          ap=[[32 * 128, 16], [128, 32], [1, 64]])
            src = bass.AP(tensor=m1d, offset=(2 * g + a) * 64,
                          ap=[[16384, 16], [4096, 4], [512, 8], [1, 64]])
            p.dma("sync", dst, src, sem=s_sl, waits=[t_md, t_z0])
    t_sl = p.dma_all(s_sl)

    s_x = [p.dma_sem("x0"), p.dma_sem("x1b")]
    x_free = [None, None]
    tp_prev = None
    for t in range(NT):
        b = t % 2
        w = [x_free[b]] if x_free[b] else []
        t_x = p.dma("sync", xs[b][:], x.rearrange("(n p) d -> n p d", p=128)[t],
                    sem=s_x[b], waits=w)
        t_rs, t_bi, _ = ln_stats(p, xs[b], sb, [t_c, t_c2], [t_x, t_c, t_c2])
        t_ap = p.op("scalar", "activation", out=lnb[:], in_=xs[b][:], func=AF.Identity,
                    bias=sb["bi"][:, 0:1], scale=sb["rs"][:, 0:1],
                    waits=[t_rs, t_bi, t_x] + ([tp_prev] if tp_prev else []))
        x_free[b] = t_ap
        for c in range(8):
            tt = p.op("tensor", "transpose", out=pb_tp[:, c * 128:(c + 1) * 128],
                      in_=lnb[:, c * 128:(c + 1) * 128], identity=eyes[:],
                      waits=([t_ap, t_ms] + ([tp_prev] if tp_prev else []) if c == 0 else []))
        t_ev = p.op("vector", "tensor_copy", out=lnstg[:],
                    in_=bass.AP(tensor=pb_tp, offset=0, ap=[[1024, 128], [1, 1024]]),
                    waits=[tt] + ([tp_prev] if tp_prev else []))
        tp_prev = p.dma("sync", bass.AP(tensor=ln3fd, offset=t * 128,
                                        ap=[[T, 128], [128 * T, 8], [1, 128]]),
                        bass.AP(tensor=lnstg, offset=0,
                                ap=[[1024, 128], [128, 8], [1, 128]]),
                        sem=p.dma_sem("lnout"), waits=[t_ev])
    t_lnall = tp_prev
    tp_guard.__exit__(None, None, None)
    pb_c = [C.ps(f"pb_c{i}", [128, 512], F32) for i in range(4)]

    s_wc = [p.dma_sem("wc0"), p.dma_sem("wc1")]
    wc_free = [None, None]
    ge_prev = [None] * 4
    out_prev = None
    y_free = [x_free[0], x_free[1]]
    s_lnin = p.dma_sem("lnin")
    lnf_free = None
    for tt6 in range(NTT):
        wv = [t_lnall] + ([lnf_free] if lnf_free else [])
        t_li = p.dma("sync", lnf[:],
                     bass.AP(tensor=ln3fd, offset=tt6 * TT6,
                             ap=[[T, 128], [128 * T, 8], [1, TT6]]),
                     sem=s_lnin, waits=wv)
        for zg in range(32):
            ztoks = []
            for u in range(4):
                zc = zg * 4 + u
                h, rp = zc // 32, zc % 32
                g = rp % 4
                ch = rp // 4
                wv = [t_sl, t_li] + ([ge_prev[u]] if ge_prev[u] else [])
                tm = p.op("tensor", "matmul",
                          out=pb_z[:, u * 512: u * 512 + TT6],
                          lhsT=bass.AP(tensor=slab,
                                       offset=32 * g * (32 * 128) + (h * 8 + rp // 4) * 128,
                                       ap=[[32 * 128, 32], [1, 128]]),
                          rhs=lnf[32 * g:32 * (g + 1), ch * TT6:(ch + 1) * TT6],
                          start=True, stop=True, tile_position=(32 * g, 0), waits=wv)
                ztoks.append(tm)
            t_ge = p.op("scalar", "activation",
                        out=bass.AP(tensor=zb, offset=(zg * 4) * TT6,
                                    ap=[[NZC * TT6, 128], [TT6, 4], [1, TT6]]),
                        in_=bass.AP(tensor=pb_z, offset=0,
                                    ap=[[2048, 128], [512, 4], [1, TT6]]),
                        func=AF.Gelu,
                        bias=sb["cst"][:, 1:2], scale=1.0, waits=ztoks + [t_c2])
            for u in range(4):
                ge_prev[u] = t_ge
        t_zall = p.tok("scalar")
        lnf_free = t_zall
        banks = [pb_z[:, i * 512:(i + 1) * 512] for i in range(4)] + [b[:] for b in pb_c]
        mm_last = [None] * 8
        for zq in range(32):
            bw = zq % 2
            wv = [wc_free[bw]] if wc_free[bw] else []
            t_wc = p.dma("gpsimd", wcb[bw][:],
                         bass.AP(tensor=wcT, offset=zq * 4 * 128 * D,
                                 ap=[[D, 128], [128 * D, 4], [1, D]]),
                         sem=s_wc[bw], waits=wv)
            for u in range(4):
                zc = zq * 4 + u
                for tg in range(NTG):
                    for dc in range(2):
                        bi_ = tg * 2 + dc
                        wv2 = [t_wc, t_zall]
                        if zc == 0 and out_prev is not None:
                            wv2.append(out_prev)
                        is_grp_last = (u == 3 and tg == NTG - 1 and dc == 1)
                        tok = (p.op if (zc >= NZC - 4 or is_grp_last) else p.op_nosem)(
                            "tensor", "matmul", out=banks[bi_],
                            lhsT=zb[:, zc * TT6 + tg * 128: zc * TT6 + (tg + 1) * 128],
                            rhs=wcb[bw][:, u * D + dc * 512: u * D + (dc + 1) * 512],
                            start=(zc == 0), stop=(zc == NZC - 1),
                            waits=(wv2 if (u == 0 and tg == 0 and dc == 0) else []))
                        if zc == NZC - 1:
                            mm_last[bi_] = tok
                        if is_grp_last:
                            wc_free[bw] = tok
        for tg in range(NTG):
            gt = tt6 * NTG + tg
            b = gt % 2
            w = [y_free[b]] if y_free[b] else []
            t_x = p.dma("sync", xs[b][:], x.rearrange("(n p) d -> n p d", p=128)[gt],
                        sem=s_x[b], waits=w)
            etoks = []
            for dc in range(2):
                te = p.op("vector", "tensor_add", out=ys[b][:, dc * 512:(dc + 1) * 512],
                          in0=banks[tg * 2 + dc], in1=xs[b][:, dc * 512:(dc + 1) * 512],
                          waits=[mm_last[tg * 2 + dc], t_x] + ([out_prev] if out_prev else []))
                etoks.append(te)
            y_free[b] = etoks[-1]
            out_prev = p.dma("sync", yo.rearrange("(n p) d -> n p d", p=128)[gt],
                             ys[b][:], sem=p.dma_sem("out"), waits=etoks)
    p._waits("sync", [p.dma_all(p.dma_sem("out"))])


def build_fused(T):
    nc = bass.Bass("TRN2", target_bir_lowering=False)
    p = Prog(nc)
    NT = T // 128
    NCH = T // 64

    def EI(n, s, d=F32):
        return nc.dram_tensor(n, s, d, kind="ExternalInput")

    def IN(n, s, d=F32):
        return nc.dram_tensor(n, s, d, kind="Internal")

    g = {
        "xin": EI("x", [T, D]),
        "hmask": EI("hmask", [1, 1]),
        "wqkvT": EI("wqkvT", [D, 3 * D]),
        "bq_row": EI("bq_row", [1, 3 * D]),
        "bq_cols": EI("bq_cols", [128, 24]),
        "pmT": EI("pmT", [64, 64]),
        "eye": EI("eye", [128, 128], BF16),
        "wprojT": EI("wprojT", [D, D]),
        "bp_row": EI("bp_row", [1, D]),
        "wupT": EI("wupT", [D, DFF]),
        "wdnT": EI("wdnT", [DFF, D]),
        "wmT": EI("wmT", [D, 1072]),
        "alog": EI("alog", [1, H]),
        "dtb": EI("dtb", [1, H]),
        "maskT": EI("maskT", [64, 64]),
        "woutT": EI("woutT", [D, D]),
        "rmJ": EI("rmJ", [256 * 16, 16]),
        "rupJ": EI("rupJ", [256 * 16, 64]),
        "wcT": EI("wcT", [16 * D, D]),
        "qef": IN("qef", [D, T], BF16),
        "wternU": nc.dram_tensor("wternU", [D, DFF], BF16,
                                 kind="ExternalOutput" if _DBG else "Internal"),
        "wternD": nc.dram_tensor("wternD", [DFF, D], BF16,
                                 kind="ExternalOutput" if _DBG else "Internal"),
        "lnfx": IN("lnfx", [128, 8 * (T + 1)], BF16),
        "vbet": IN("vbet", [128, NT * D], BF16),
        "wkdw": IN("wkdw", [128, NT * D], BF16),
        "kvpo": IN("kvpo", [128, 8 * 65]),
        "kvsum": IN("kvsum", [128, 8 * 65]),
        "x1": IN("x1", [T, D]),
        "x2": IN("x2", [T, D]),
        "xhg": IN("xhg", [2, D]),
        "gro": IN("gro", [128, NT * 48]),
        "edf": IN("edf", [128, NT * H]),
        "ln2f": IN("ln2f", [D, T], BF16),
        "dcd": IN("dcd", [H, T]),
        "o_local": IN("o_local", [T, D]),
        "sfin": IN("sfin", [128, 8 * 64]),
        "sfg": IN("sfg", [256, 8 * 64]),
        "x3": IN("x3", [T, D]),
        "y": nc.dram_tensor("y", [T, D], F32, kind="ExternalOutput"),
    }

    GRP = [[0, 1], [2, 3], [4, 5], [6, 7]]

    C13 = StageCtx(nc, "s13_")
    tern = TernBG(p, C13, g)
    C = StageCtx(nc, "a_"); st1_body(p, g, C, T, bg=tern); C.close()
    p.barrier()
    p.collective("AllReduce", ALU.add, GRP,
                 ins=[g["kvpo"].ap().opt()], outs=[g["kvsum"].ap().opt()])
    p.barrier()
    C = StageCtx(nc, "b_"); st2_body(p, g, C, T, bg=tern); C.close()
    C13.close()
    p.barrier()
    C = StageCtx(nc, "c_"); st3_body(p, g, C, T); C.close()
    p.barrier()
    p.collective("AllGather", ALU.bypass, GRP,
                 ins=[g["x2"].ap()[T - 1:T, :].opt()], outs=[g["xhg"].ap().opt()])
    p.barrier()
    C4 = StageCtx(nc, "s4_")
    g["sh_dcumt"] = C4.sb("dcumt", [128, NT * H])
    g["sh_dext"] = C4.sb("dext", [128, NT * H])
    g["sh_gcols"] = C4.sb("gcols", [128, 8 * NCH])
    C = StageCtx(nc, "d_"); st4a_body(p, g, C, T); C.close()
    p.barrier()
    C = StageCtx(nc, "e_"); st4b_body(p, g, C, T); C.close()
    C4.close()
    p.barrier()
    p.collective("AllGather", ALU.bypass, GRP,
                 ins=[g["sfin"].ap().opt()], outs=[g["sfg"].ap().opt()])
    p.barrier()
    C = StageCtx(nc, "f_"); st5_body(p, g, C, T); C.close()
    p.barrier()
    C = StageCtx(nc, "g_"); st6_body(p, g, C, T); C.close()
    return nc


def _run(nc, in_maps):
    return run_bass_kernel_spmd(nc, in_maps, core_ids=list(range(NCORE))).results


def kernel(L0_attn_x, L0_attn_W_qkv, L0_attn_b_qkv, L0_attn_W_proj, L0_attn_b_proj,
           L0_attn_proj_mat, L0_ffn_w_up, L0_ffn_w_down, L1_attn_W_write,
           L1_attn_W_gate, L1_attn_W_beta, L1_attn_W_alpha, L1_attn_W_out,
           L1_attn_A_log, L1_attn_dt_bias, L1_ffn_rail_maker, L1_ffn_rail_up_proj,
           L1_ffn_w_cproj):
    T = 2048
    xs = [np.ascontiguousarray(s) for s in
          np.asarray(L0_attn_x, dtype=np.float32).reshape(NCORE, T, D)]

    nc = build_fused(T)
    b_qkv = np.asarray(L0_attn_b_qkv)
    wm = np.concatenate([np.asarray(L1_attn_W_write).T, np.asarray(L1_attn_W_gate).T,
                         np.asarray(L1_attn_W_beta).T, np.asarray(L1_attn_W_alpha).T],
                        axis=1)
    base = {
        "wqkvT": np.ascontiguousarray(np.asarray(L0_attn_W_qkv).T),
        "bq_row": np.ascontiguousarray(b_qkv.reshape(1, -1)),
        "bq_cols": np.ascontiguousarray(b_qkv.reshape(24, 128).T),
        "pmT": np.ascontiguousarray(np.asarray(L0_attn_proj_mat).T),
        "eye": np.eye(128, dtype=np.float32).astype(ml_dtypes.bfloat16),
        "wprojT": np.ascontiguousarray(np.asarray(L0_attn_W_proj).T),
        "bp_row": np.ascontiguousarray(np.asarray(L0_attn_b_proj).reshape(1, -1)),
        "wupT": np.ascontiguousarray(np.asarray(L0_ffn_w_up).T),
        "wdnT": np.ascontiguousarray(np.asarray(L0_ffn_w_down).T),
        "wmT": np.ascontiguousarray(wm),
        "alog": np.ascontiguousarray(np.asarray(L1_attn_A_log).reshape(1, -1)),
        "dtb": np.ascontiguousarray(np.asarray(L1_attn_dt_bias).reshape(1, -1)),
        "maskT": np.ascontiguousarray(
            np.where(np.triu(np.ones((64, 64), bool), 1), 0.0, -1e30)
            .astype(np.float32)),
        "woutT": np.ascontiguousarray(np.asarray(L1_attn_W_out).T),
        "rmJ": np.ascontiguousarray(
            np.asarray(L1_ffn_rail_maker).transpose(0, 1, 3, 2).reshape(256 * 16, 16)),
        "rupJ": np.ascontiguousarray(
            np.asarray(L1_ffn_rail_up_proj).transpose(0, 1, 3, 2)
            .reshape(256, 16, 64).reshape(256 * 16, 64)),
        "wcT": np.ascontiguousarray(np.asarray(L1_ffn_w_cproj).T),
    }
    maps = []
    for c in range(NCORE):
        m = dict(base)
        m["x"] = xs[c]
        m["hmask"] = np.array([[1.0 if c % 2 else 0.0]], np.float32)
        maps.append(m)
    r = _run(nc, maps)
    globals()["_last_r"] = r
    y = np.concatenate([np.asarray(r[c]["y"]) for c in range(NCORE)], axis=0)
    return y.reshape(4, 4096, D)


# version 15 (final)
# speedup vs baseline: 1.1900x; 1.0310x over previous
import contextlib
import numpy as np
import ml_dtypes
import concourse.bass as bass
import concourse.mybir as mybir
from concourse.bass_utils import run_bass_kernel_spmd

F32 = mybir.dt.float32
BF16 = mybir.dt.bfloat16
AF = mybir.ActivationFunctionType
ALU = mybir.AluOpType
AX = mybir.AxisListType

ENGINES = ("tensor", "vector", "scalar", "gpsimd", "sync")

D = 1024
H = 16
HD = 64
NF = 64
DFF = 4096
NCORE = 8


class Prog:
    def __init__(self, nc: bass.Bass):
        self.nc = nc
        self.sem = {}
        self.cnt = {}
        for e in ENGINES:
            self.sem[e] = nc.alloc_semaphore(name=f"es_{e}")
            self.cnt[e] = 0
        self._cc_toks = []
        self._ccidx = 0
        self._dma_sems = {}
        self._bar = None
        self._bar_seen = set()

    def eng(self, name):
        return getattr(self.nc, name)

    def barrier(self):
        toks = [(self.sem[e], self.cnt[e]) for e in ENGINES]
        toks += [(rec[0], rec[1]) for rec in self._dma_sems.values()]
        toks += self._cc_toks
        self._bar = [t for t in toks if t[1] > 0]
        self._bar_seen = set()

    def _waits(self, engine, waits):
        if self._bar is not None and engine not in self._bar_seen:
            self._bar_seen.add(engine)
            waits = list(waits) + self._bar
        best = {}
        for t in waits:
            if t is None:
                continue
            s, v = t
            k = id(s)
            if k not in best or best[k][1] < v:
                best[k] = (s, v)
        e = self.eng(engine)
        for (s, v) in best.values():
            e.wait_ge(s, v)

    def op(self, engine, method, *args, waits=(), **kwargs):
        self._waits(engine, waits)
        inst = getattr(self.eng(engine), method)(*args, **kwargs)
        self.cnt[engine] += 1
        inst.then_inc(self.sem[engine], 1)
        return (self.sem[engine], self.cnt[engine])

    def op_nosem(self, engine, method, *args, waits=(), **kwargs):
        self._waits(engine, waits)
        getattr(self.eng(engine), method)(*args, **kwargs)
        return None

    def dma_sem(self, name):
        if name not in self._dma_sems:
            self._dma_sems[name] = [self.nc.alloc_semaphore(name=f"dma_{name}"), 0]
        return name

    def dma(self, engine, out, in_, sem, waits=(), **kwargs):
        self._waits(engine, waits)
        inst = self.eng(engine).dma_start(out=out, in_=in_, **kwargs)
        rec = self._dma_sems[sem]
        rec[1] += 16
        inst.then_inc(rec[0], 16)
        return (rec[0], rec[1])

    def dma_all(self, sem):
        rec = self._dma_sems[sem]
        return (rec[0], rec[1])

    def collective(self, kind, alu, groups, ins, outs, waits=()):
        self._waits("gpsimd", waits)
        inst = self.nc.gpsimd.collective_compute(kind, alu, replica_groups=groups,
                                                 ins=ins, outs=outs)
        sem = self.nc.alloc_semaphore(name=f"cc{self._ccidx}")
        self._ccidx += 1
        inst.then_inc(sem)
        self._cc_toks.append((sem, 1))
        return (sem, 1)

    def tok(self, engine):
        return (self.sem[engine], self.cnt[engine])


class StageCtx:

    def __init__(self, nc, tag):
        self.nc, self.tag = nc, tag
        self.stack = contextlib.ExitStack()

    def sb(self, name, shape, dtype=F32):
        return self.stack.enter_context(
            self.nc.sbuf_tensor(self.tag + name, shape, dtype))

    def ps(self, name, shape, dtype=F32):
        return self.stack.enter_context(
            self.nc.psum_tensor(self.tag + name, shape, dtype))

    def close(self):
        self.stack.close()


def bcast_ap(dram_handle, parts, free, offset=0, row_len=None):
    return bass.AP(tensor=dram_handle, offset=offset, ap=[[0, parts], [1, free]])


def ln_stats(p, xs, sb, pre, waits):
    t1 = p.op("vector", "bn_stats", out=sb["bnr"][:, 0:6], in_=xs[:, 0:512], waits=waits)
    t2 = p.op("vector", "bn_stats", out=sb["bnr"][:, 6:12], in_=xs[:, 512:1024], waits=[t1])
    t3 = p.op("vector", "bn_aggr", out=sb["agg"][:, 0:2], in_=sb["bnr"][:, 0:12], waits=[t2])
    t4 = p.op("scalar", "activation", out=sb["rs"][:], in_=sb["agg"][:, 1:2], func=AF.Ln,
              bias=sb["cst"][:, 0:1], scale=1.0, waits=[t3] + list(pre))
    t5 = p.op("scalar", "activation", out=sb["rs"][:], in_=sb["rs"][:], func=AF.Exp,
              bias=sb["cst"][:, 1:2], scale=-0.5, waits=[t4])
    t6 = p.op("vector", "scalar_tensor_tensor", out=sb["bi"][:], in0=sb["agg"][:, 0:1],
              scalar=-1.0, in1=sb["rs"][:], op0=ALU.mult, op1=ALU.mult, waits=[t3, t5])
    return t5, t6, t3



def st1_body(p, g, C, T):
    nc = p.nc
    NT = T // 128
    GS = min(512, T)
    NG = T // GS
    BT = min(8, NT)

    x = g["xin"].ap()
    wqkvT = g["wqkvT"]
    bq_row = g["bq_row"].ap()
    bq_cols = g["bq_cols"].ap()
    pmT = g["pmT"]
    eye = g["eye"].ap()

    qef = g["qef"].ap()
    kvpo = g["kvpo"].ap()

    xs = [C.sb(f"xs{i}", [128, D], F32) for i in range(2)]
    lnb = C.sb("lnb", [128, D], BF16)
    ln0f = C.sb("ln0f", [128, 8 * T], BF16)
    wqb = C.sb("wqb", [128, 8 * 3 * D], BF16)
    bqr = C.sb("bqr", [1, 3 * D], BF16)
    bqc = C.sb("bqc", [128, 24], F32)
    pm2 = C.sb("pm2", [128, 64], BF16)
    pmf = C.sb("pmf", [128, 64], F32)
    eyes = C.sb("eyes", [128, 128], BF16)
    ones_r = C.sb("ones_r", [1, 128], BF16)
    vtok = C.sb("vtok", [128, NT * H * 65], BF16)
    qfp = C.sb("qfp", [128, T], BF16)
    kfp = C.sb("kfp", [128, T], BF16)
    qep = C.sb("qep", [128, T], BF16)
    kep = C.sb("kep", [128, T], BF16)
    qefp = C.sb("qefp", [128, T], BF16)
    kv_all = C.sb("kv_all", [128, 8 * 65], F32)
    sb = {
        "bnr": C.sb("bnr", [128, 12], F32),
        "agg": C.sb("agg", [128, 2], F32),
        "rs": C.sb("rs", [128, 1], F32),
        "bi": C.sb("bi", [128, 1], F32),
        "cst": C.sb("cst", [128, 4], F32),
    }
    mcol = C.sb("mcol", [128, 4], F32)

    pb_tp = C.ps("pb_tp", [128, 1024], BF16)
    pb_v = [C.ps(f"pb_v{i}", [128, 512], F32) for i in range(2)]
    pb_g = [C.ps(f"pb_g{i}", [128, 512], F32) for i in range(2)]
    pb_f = [C.ps(f"pb_f{i}", [128, 64], F32) for i in range(2)]
    pb_kv = C.ps("pb_kv", [128, 65], F32)

    s_wq = p.dma_sem("wq")
    for kc in range(8):
        p.dma("gpsimd", wqb[:, kc * 3072:(kc + 1) * 3072],
              wqkvT[kc * 128:(kc + 1) * 128, :], sem=s_wq)
    p.dma("gpsimd", bqr[:], bq_row, sem=s_wq)
    t_wq = p.dma_all(s_wq)

    s_ms = p.dma_sem("misc")
    p.dma("sync", bqc[:], bq_cols, sem=s_ms)
    p.dma("sync", eyes[:], eye, sem=s_ms)
    p.dma("sync", pmf[0:64, :], pmT.ap(), sem=s_ms)
    p.dma("sync", pmf[64:128, :], pmT.ap(), sem=s_ms)
    t_ms = p.dma_all(s_ms)

    t_c = p.op("vector", "memset", sb["cst"][:, 0:1], 1e-5)
    t_c2 = p.op("vector", "memset", sb["cst"][:, 1:2], 0.0)
    t_on = p.op("vector", "memset", ones_r[:], 1.0)
    ones_view = bass.AP(tensor=vtok, offset=64,
                        ap=[[NT * H * 65, 128], [65, NT * H], [1, 1]])
    t_ov = p.op("vector", "memset", ones_view, 1.0, waits=[])
    t_pm = p.op("scalar", "activation", out=pm2[:], in_=pmf[:], func=AF.Identity,
                bias=sb["cst"][:, 1:2], scale=float(HD ** -0.25), waits=[t_ms, t_c2])

    x_t = x.rearrange("(n p) d -> n p d", p=128)

    s_x = [p.dma_sem("x0"), p.dma_sem("x1")]
    x_free = [None, None]
    ln_done = [None] * NT
    v_done = [None] * NT
    tp_evac_prev = None
    vband_prev = [None, None]
    for t in range(NT):
        b = t % 2
        w = [x_free[b]] if x_free[b] else []
        t_x = p.dma("sync", xs[b][:], x_t[t], sem=s_x[b], waits=w)
        t_rs, t_bi, t_agg = ln_stats(p, xs[b], sb, [t_c, t_c2], [t_x, t_c, t_c2])
        w_apply = [t_rs, t_bi, t_x]
        if tp_evac_prev is not None:
            w_apply.append(tp_last)
        t_ap = p.op("scalar", "activation", out=lnb[:], in_=xs[b][:], func=AF.Identity,
                    bias=sb["bi"][:, 0:1], scale=sb["rs"][:, 0:1], waits=w_apply)
        x_free[b] = t_ap
        w_tp = [t_ap, t_ms]
        if tp_evac_prev is not None:
            w_tp.append(tp_evac_prev)
        for c in range(8):
            tt = p.op("tensor", "transpose", out=pb_tp[:, c * 128:(c + 1) * 128],
                      in_=lnb[:, c * 128:(c + 1) * 128], identity=eyes[:],
                      waits=(w_tp if c == 0 else []))
        tp_last = tt
        dst = bass.AP(tensor=ln0f, offset=t * 128,
                      ap=[[8 * T, 128], [T, 8], [1, 128]])
        src = bass.AP(tensor=pb_tp, offset=0, ap=[[1024, 128], [128, 8], [1, 128]])
        t_ev = p.op("vector", "tensor_copy", out=dst, in_=src, waits=[tt])
        tp_evac_prev = t_ev
        ln_done[t] = t_ev
        for dc in range(2):
            bank = pb_v[dc]
            wv = [t_ev, t_wq]
            if vband_prev[dc] is not None:
                wv.append(vband_prev[dc])
            for kc in range(8):
                p.op_nosem("tensor", "matmul", out=bank[:],
                           lhsT=ln0f[:, kc * T + t * 128: kc * T + (t + 1) * 128],
                           rhs=wqb[:, kc * 3072 + 2048 + dc * 512: kc * 3072 + 2048 + (dc + 1) * 512],
                           start=(kc == 0), stop=False, waits=(wv if kc == 0 else []))
            t_mm = p.op("tensor", "matmul", out=bank[:], lhsT=ones_r[:],
                        rhs=bqr[0:1, 2048 + dc * 512: 2048 + (dc + 1) * 512],
                        start=False, stop=True, waits=[t_on])
            vdst = bass.AP(tensor=vtok, offset=t * H * 65 + dc * 8 * 65,
                           ap=[[NT * H * 65, 128], [65, 8], [1, 64]])
            vsrc = bass.AP(tensor=pb_v[dc], offset=0,
                           ap=[[512, 128], [64, 8], [1, 64]])
            t_vev = p.op("vector", "tensor_copy", out=vdst, in_=vsrc, waits=[t_mm, t_ov])
            vband_prev[dc] = t_vev
        v_done[t] = vband_prev[1]

    t_ln_all = p.tok("tensor")
    t_v_all = p.tok("vector")

    s_out = p.dma_sem("out")
    g_prev = [None, None]
    f_prev = [None, None]
    kv_prev = None
    qef_dma_prev = None

    def pair_gemm(hp, which, dst_sb, bias_col, g_prev, t_pairstart):
        base = which * 1024 + hp * 128
        toks = []
        for g in range(NG):
            bank = pb_g[g % 2]
            wv = [t_wq, t_v_all, t_pairstart]
            if g_prev[g % 2] is not None:
                wv.append(g_prev[g % 2])
            for kc in range(8):
                tok = (p.op if kc == 7 else p.op_nosem)(
                    "tensor", "matmul", out=bank[:, 0:GS],
                    lhsT=wqb[:, kc * 3072 + base: kc * 3072 + base + 128],
                    rhs=ln0f[:, kc * T + g * GS: kc * T + (g + 1) * GS],
                    start=(kc == 0), stop=(kc == 7), waits=(wv if kc == 0 else []))
            t_e = p.op("scalar", "activation", out=dst_sb[:, g * GS:(g + 1) * GS],
                       in_=bank[:, 0:GS], func=AF.Identity,
                       bias=bqc[:, bias_col:bias_col + 1], scale=1.0,
                       waits=[tok, t_ms, t_pairstart])
            g_prev[g % 2] = t_e
            toks.append(t_e)
        return toks

    for hp in range(8):
        t_pairstart = p.tok("tensor")
        tq = pair_gemm(hp, 0, qfp, hp, g_prev, t_pairstart)
        tk = pair_gemm(hp, 1, kfp, 8 + hp, g_prev, t_pairstart)
        for which, src_sb, dst_sb, toks in ((0, qfp, qep, tq), (1, kfp, kep, tk)):
            for t in range(NT):
                t_dep = toks[(t * 128) // GS]
                for half in range(2):
                    bank = pb_f[half]
                    wv = [t_dep, t_pm]
                    if f_prev[half] is not None:
                        wv.append(f_prev[half])
                    tm = p.op("tensor", "matmul", out=bank[:],
                              lhsT=src_sb[half * 64:(half + 1) * 64, t * 128:(t + 1) * 128],
                              rhs=pm2[half * 64:(half + 1) * 64, :],
                              start=True, stop=True,
                              tile_position=(half * 64, 0), waits=wv)
                    tr = p.op("vector", "reduce_max", out=mcol[:, 2 * half:2 * half + 1],
                              in_=bank[:], axis=AX.X, waits=[tm])
                    tb = p.op("vector", "tensor_scalar", out=mcol[:, 2 * half + 1:2 * half + 2],
                              in0=mcol[:, 2 * half:2 * half + 1], scalar1=-1.0, scalar2=1e-4,
                              op0=ALU.mult, op1=ALU.add, waits=[tr])
                    te = p.op("scalar", "activation",
                              out=dst_sb[:, t * 128 + half * 64: t * 128 + (half + 1) * 64],
                              in_=bank[:], func=AF.Exp, bias=mcol[:, 2 * half + 1:2 * half + 2],
                              scale=1.0, waits=[tb, t_pairstart])
                    f_prev[half] = te
        k_last = p.tok("scalar")
        for half in range(2):
            for t in range(NT):
                wv = [k_last, t_v_all]
                if kv_prev is not None:
                    wv.append(kv_prev)
                tok = (p.op if (t == NT - 1) else p.op_nosem)(
                    "tensor", "matmul",
                    out=pb_kv[half * 64:(half + 1) * 64, :],
                    lhsT=kep[:, t * 128 + half * 64: t * 128 + (half + 1) * 64],
                    rhs=vtok[:, t * H * 65 + (2 * hp + half) * 65: t * H * 65 + (2 * hp + half + 1) * 65],
                    start=(t == 0), stop=(t == NT - 1),
                    tile_position=(0, half * 64),
                    waits=(wv if t == 0 else []))
        t_kve = p.op("vector", "tensor_copy", out=kv_all[:, hp * 65:(hp + 1) * 65],
                     in_=pb_kv[:], waits=[tok])
        kv_prev = t_kve
        for bt in range(NT // BT):
            w_tp = [p.tok("scalar"), tp_evac_prev]
            if qef_dma_prev is not None:
                w_tp.append(qef_dma_prev)
            for i in range(BT):
                t0 = bt * BT + i
                tt = p.op("tensor", "transpose", out=pb_tp[:, i * 128:(i + 1) * 128],
                          in_=qep[:, t0 * 128:(t0 + 1) * 128], identity=eyes[:],
                          waits=(w_tp if i == 0 else []))
            t_ev = p.op("vector", "tensor_copy",
                        out=qefp[:, bt * BT * 128:(bt + 1) * BT * 128],
                        in_=bass.AP(tensor=pb_tp, offset=0,
                                    ap=[[1024, 128], [1, BT * 128]]),
                        waits=[tt])
            tp_evac_prev = t_ev
        qef_dma_prev = p.dma("sync", qef[hp * 128:(hp + 1) * 128, :], qefp[:], sem=s_out,
                             waits=[tp_evac_prev])

    p.dma("sync", kvpo, kv_all[:], sem=s_out, waits=[kv_prev])
    p._waits("sync", [p.dma_all(s_out)])


def st2_body(p, g, C, T):
    nc = p.nc
    NT = T // 128

    x = g["xin"].ap()
    qefd = g["qef"]
    kvp0 = g["kvsum"].ap()
    wprojT = g["wprojT"]
    bp_row = g["bp_row"].ap()
    eye = g["eye"].ap()
    x1o = g["x1"].ap()

    qefs = C.sb("qefs", [128, 8 * T], BF16)
    kva = C.sb("kva", [128, 8 * 65], F32)
    kvbf = C.sb("kvbf", [128, 8 * 65], BF16)
    wpb = C.sb("wpb", [128, 8 * D], BF16)
    bpr = C.sb("bpr", [1, D], BF16)
    eyes = C.sb("eyes", [128, 128], BF16)
    ones_r = C.sb("ones_r", [1, 128], BF16)
    att = C.sb("att", [128, NT * D], BF16)
    attf = C.sb("attf", [128, 8 * T], BF16)
    xs = [C.sb(f"xs{i}", [128, D], F32) for i in range(2)]
    x1s = C.sb("x1s", [128, D], F32)
    ctmp = C.sb("ctmp", [128, 6], F32)

    pb_o = [C.ps(f"pb_o{i}", [128, 65], F32) for i in range(2)]
    pb_tp = C.ps("pb_tp", [128, 1024], BF16)
    pb_g = [C.ps(f"pb_g{i}", [128, 512], F32) for i in range(2)]

    s_ld = p.dma_sem("ld")
    for hp in range(8):
        p.dma("sync", qefs[:, hp * T:(hp + 1) * T], qefd.ap()[hp * 128:(hp + 1) * 128, :], sem=s_ld)
    p.dma("sync", kva[:], kvp0, sem=s_ld)
    p.dma("sync", eyes[:], eye, sem=s_ld)
    t_ld = p.dma_all(s_ld)
    s_w = p.dma_sem("w")
    for kc in range(8):
        p.dma("gpsimd", wpb[:, kc * D:(kc + 1) * D],
              wprojT.ap()[kc * 128:(kc + 1) * 128, :], sem=s_w)
    p.dma("gpsimd", bpr[:], bp_row, sem=s_w)
    t_w = p.dma_all(s_w)
    t_on = p.op("vector", "memset", ones_r[:], 1.0)
    t_cz = p.op("vector", "memset", ctmp[:, 3:4], 0.0)

    t_kvb = p.op("vector", "tensor_copy", out=kvbf[:], in_=kva[:], waits=[t_ld])

    o_prev = [None, None]
    for hp in range(8):
        for t in range(NT):
            toks = []
            for half in range(2):
                wv = [t_ld, t_kvb]
                if o_prev[half] is not None:
                    wv.append(o_prev[half])
                tm = p.op("tensor", "matmul", out=pb_o[half][:],
                          lhsT=qefs[half * 64:(half + 1) * 64, hp * T + t * 128: hp * T + (t + 1) * 128],
                          rhs=kvbf[half * 64:(half + 1) * 64, hp * 65:(hp + 1) * 65],
                          start=True, stop=True, tile_position=(half * 64, 0), waits=wv)
                toks.append(tm)
            t1 = p.op("vector", "tensor_scalar", out=ctmp[:, 0:1], in0=pb_o[0][:, 64:65],
                      scalar1=1e-6, scalar2=None, op0=ALU.add, waits=[toks[0]])
            t2 = p.op("vector", "tensor_scalar", out=ctmp[:, 1:2], in0=pb_o[1][:, 64:65],
                      scalar1=1e-6, scalar2=None, op0=ALU.add, waits=[toks[1]])
            t3 = p.op("vector", "reciprocal", out=ctmp[:, 2:3], in_=ctmp[:, 0:1], waits=[t1, t2])
            t3b = p.op("vector", "reciprocal", out=ctmp[:, 5:6], in_=ctmp[:, 1:2], waits=[t3])
            rc = [t3, t3b]
            for half in range(2):
                h = 2 * hp + half
                te = p.op("scalar", "activation",
                          out=att[:, t * D + h * 64: t * D + (h + 1) * 64],
                          in_=pb_o[half][:, 0:64], func=AF.Identity,
                          bias=ctmp[:, 3:4],
                          scale=ctmp[:, (2 if half == 0 else 5):(3 if half == 0 else 6)],
                          waits=[rc[half], t_cz])
                o_prev[half] = te
    t_att = p.tok("scalar")

    tp_prev = None
    for t in range(NT):
        for c in range(8):
            wv = [t_att]
            if tp_prev is not None:
                wv.append(tp_prev)
            tt = p.op("tensor", "transpose", out=pb_tp[:, c * 128:(c + 1) * 128],
                      in_=att[:, t * D + c * 128: t * D + (c + 1) * 128],
                      identity=eyes[:], waits=(wv if c == 0 else []))
        dst = bass.AP(tensor=attf, offset=t * 128,
                      ap=[[8 * T, 128], [T, 8], [1, 128]])
        src = bass.AP(tensor=pb_tp, offset=0, ap=[[1024, 128], [128, 8], [1, 128]])
        tp_prev = p.op("vector", "tensor_copy", out=dst, in_=src, waits=[tt])
    t_attf = p.tok("vector")

    s_x = [p.dma_sem("x0"), p.dma_sem("x1b")]
    s_out = p.dma_sem("out")
    x_free = [None, None]
    g_prev = [None, None]
    x1_dma_prev = None
    for t in range(NT):
        b = t % 2
        w = [x_free[b]] if x_free[b] else []
        t_x = p.dma("sync", xs[b][:], x.rearrange("(n p) d -> n p d", p=128)[t], sem=s_x[b], waits=w)
        etoks = []
        for dc in range(2):
            bank = pb_g[dc]
            wv = [t_attf, t_w]
            if g_prev[dc] is not None:
                wv.append(g_prev[dc])
            for kc in range(8):
                p.op_nosem("tensor", "matmul", out=bank[:],
                           lhsT=attf[:, kc * T + t * 128: kc * T + (t + 1) * 128],
                           rhs=wpb[:, kc * D + dc * 512: kc * D + (dc + 1) * 512],
                           start=(kc == 0), stop=False, waits=(wv if kc == 0 else []))
            tm = p.op("tensor", "matmul", out=bank[:], lhsT=ones_r[:],
                      rhs=bpr[0:1, dc * 512:(dc + 1) * 512], start=False, stop=True,
                      waits=[t_on])
            te = p.op("vector", "tensor_add", out=x1s[:, dc * 512:(dc + 1) * 512],
                      in0=pb_g[dc][:], in1=xs[b][:, dc * 512:(dc + 1) * 512],
                      waits=[tm, t_x, x1_dma_prev] if x1_dma_prev else [tm, t_x])
            g_prev[dc] = te
            etoks.append(te)
        x_free[b] = etoks[-1]
        x1_dma_prev = p.dma("sync", x1o.rearrange("(n p) d -> n p d", p=128)[t],
                            x1s[:], sem=s_out, waits=etoks)
    p._waits("sync", [p.dma_all(s_out)])


def st3_body(p, g, C, T):
    nc = p.nc
    NT = T // 128

    x = g["x1"].ap()
    wupT = g["wupT"]
    wdnT = g["wdnT"]
    eye = g["eye"].ap()
    x2o = g["x2"].ap()
    thr_d = nc.dram_tensor("thr_d", [1, 2], F32, kind="Internal")

    wub = C.sb("wub", [128, 8 * DFF], BF16)
    wdb = C.sb("wdb", [128, 32 * D], BF16)
    wldA = C.sb("wldA", [128, 2048], F32)
    wldB = C.sb("wldB", [128, 2048], F32)
    wldC = C.sb("wldC", [128, 1024], F32)
    wldD = C.sb("wldD", [128, 1024], F32)
    rsum = C.sb("rsum", [128, 60], F32)
    onesf = C.sb("onesf", [128, 1], F32)
    thr = C.sb("thr", [128, 2], F32)
    eyes = C.sb("eyes", [128, 128], BF16)
    xs = [C.sb(f"xs{i}", [128, D], F32) for i in range(2)]
    lnb = C.sb("lnb", [128, D], BF16)
    ffn_f = C.sb("ffn_f", [128, 8 * 128], BF16)
    hgel = C.sb("hgel", [128, DFF], F32)
    hb = C.sb("hb", [128, DFF], BF16)
    hfb = C.sb("hfb", [128, 32 * 128], BF16)
    x2s = C.sb("x2s", [128, D], F32)
    sb = {
        "bnr": C.sb("bnr", [128, 48], F32),
        "agg": C.sb("agg", [128, 2], F32),
        "rs": C.sb("rs", [128, 1], F32),
        "bi": C.sb("bi", [128, 1], F32),
        "cst": C.sb("cst", [128, 4], F32),
    }
    sc2 = C.sb("sc2", [128, 6], F32)

    pb_a = C.ps("pb_a", [1, 2], F32)
    pb_tp = C.ps("pb_tp", [128, 1024], BF16)
    pb_u = [C.ps(f"pb_u{i}", [128, 512], F32) for i in range(2)]
    pb_d = [C.ps(f"pb_d{i}", [128, 512], F32) for i in range(2)]

    t_c = p.op("vector", "memset", sb["cst"][:, 0:1], 1e-5)
    t_c2 = p.op("vector", "memset", sb["cst"][:, 1:2], 0.0)
    t_of = p.op("vector", "memset", onesf[:], 1.0)
    s_ms = p.dma_sem("misc")
    p.dma("sync", eyes[:], eye, sem=s_ms)
    t_ms = p.dma_all(s_ms)

    sgn_scr = hgel
    sgn2 = C.sb("sgn2", [128, 1024], F32)

    def tern_pass1(wT, nch, wd, thr_col, rs_base):
        s_l = [p.dma_sem(f"tl{thr_col}a"), p.dma_sem(f"tl{thr_col}b")]
        bufs = [wldA, wldB] if thr_col == 0 else [wldC, wldD]
        SW = min(2048, wd)
        nsub = wd // SW
        t0 = [p.tok("scalar"), p.tok("vector")]
        free_tok = [t0, t0]
        acc = []
        for c in range(nch):
            for su in range(nsub):
                i = c * nsub + su
                b = i % 2
                t_l = p.dma("sync" if b == 0 else "gpsimd", bufs[b][:, 0:SW],
                            wT.ap()[c * 128:(c + 1) * 128, su * SW:(su + 1) * SW],
                            sem=s_l[b], waits=free_tok[b])
                t_a = p.op("scalar", "activation", out=bufs[b][:, 0:SW],
                           in_=bufs[b][:, 0:SW], func=AF.Abs, bias=sb["cst"][:, 1:2],
                           scale=1.0, accum_out=rsum[:, rs_base + i:rs_base + i + 1],
                           waits=[t_l, t_c2] + ([acc[-1]] if acc else []))
                free_tok[b] = [t_a]
                acc.append(t_a)
        return s_l, acc

    def tern_pass2(wT, nch, wd, dst, thr_col, s_l, acc, rs_base):
        bufs = [wldA, wldB] if thr_col == 0 else [wldC, wldD]
        SW = min(2048, wd)
        nsub = wd // SW
        nsums = nch * nsub
        t_r = p.op("vector", "reduce_sum", out=rsum[:, rs_base + nsums:rs_base + nsums + 1],
                   in_=rsum[:, rs_base:rs_base + nsums], axis=AX.X, waits=[acc[-1]])
        t_m = p.op("tensor", "matmul", out=pb_a[0:1, thr_col:thr_col + 1],
                   lhsT=onesf[:], rhs=rsum[:, rs_base + nsums:rs_base + nsums + 1],
                   start=True, stop=True, waits=[t_r, t_of])
        t_t = p.op("vector", "tensor_scalar", out=thr[0:1, thr_col:thr_col + 1],
                   in0=pb_a[0:1, thr_col:thr_col + 1],
                   scalar1=0.5 / (nch * 128 * wd), scalar2=None, op0=ALU.mult,
                   waits=[t_m])
        s_t = p.dma_sem(f"thr{thr_col}")
        t_d1 = p.dma("sync", thr_d.ap()[0:1, thr_col:thr_col + 1],
                     thr[0:1, thr_col:thr_col + 1], sem=s_t, waits=[t_t])
        t_d2 = p.dma("sync", thr[:, thr_col:thr_col + 1],
                     bass.AP(tensor=thr_d, offset=thr_col, ap=[[0, 128], [1, 1]]),
                     sem=s_t, waits=[t_d1])
        t1 = [p.tok("scalar"), p.tok("vector")]
        free_tok = [t1, t1]
        prev = None
        for c in range(nch):
            for su in range(nsub):
                i = c * nsub + su
                b = i % 2
                t_l = p.dma("sync" if b == 0 else "gpsimd", bufs[b][:, 0:SW],
                            wT.ap()[c * 128:(c + 1) * 128, su * SW:(su + 1) * SW],
                            sem=s_l[b], waits=free_tok[b])
                scr = sgn_scr if thr_col == 0 else sgn2
                wv = [t_l, t_d2] + ([prev] if prev else [])
                t_s = p.op("scalar", "sign", out=scr[:, 0:SW], in_=bufs[b][:, 0:SW],
                           waits=wv)
                t_a = p.op("scalar", "activation", out=bufs[b][:, 0:SW],
                           in_=bufs[b][:, 0:SW], func=AF.Abs, bias=sb["cst"][:, 1:2],
                           scale=1.0, waits=[t_s])
                t_w = p.op("vector", "scalar_tensor_tensor",
                           out=dst[:, c * wd + su * SW: c * wd + (su + 1) * SW],
                           in0=bufs[b][:, 0:SW], scalar=thr[:, thr_col:thr_col + 1],
                           in1=scr[:, 0:SW], op0=ALU.is_gt, op1=ALU.mult,
                           waits=[t_a])
                free_tok[b] = [t_w]
                prev = t_w
        return prev

    sl_u, acc_u = tern_pass1(wupT, 8, DFF, 0, 0)
    sl_d, acc_d = tern_pass1(wdnT, 32, D, 1, 20)
    t_wu = tern_pass2(wupT, 8, DFF, wub, 0, sl_u, acc_u, 0)
    t_wd = tern_pass2(wdnT, 32, D, wdb, 1, sl_d, acc_d, 20)

    s_x = [p.dma_sem("x0"), p.dma_sem("x1b")]
    s_out = p.dma_sem("out")
    x_free = [None, None]
    tp_prev = None
    u_prev = [None, None]
    d_prev = [None, None]
    hg_prev = None
    out_prev = None
    for t in range(NT):
        b = t % 2
        w = [x_free[b]] if x_free[b] else []
        t_x = p.dma("sync", xs[b][:], x.rearrange("(n p) d -> n p d", p=128)[t],
                    sem=s_x[b], waits=w)
        t_rs, t_bi, t_agg = ln_stats(p, xs[b], sb, [t_c, t_c2], [t_x, t_c, t_c2])
        t_q = p.op("scalar", "activation", out=sc2[:, 0:1], in_=sb["agg"][:, 1:2],
                   func=AF.Sqrt, bias=sb["cst"][:, 1:2], scale=1.0, waits=[t_agg, t_rs])
        t_rm = p.op("vector", "tensor_mul", out=sc2[:, 1:2], in0=sc2[:, 0:1],
                    in1=sb["rs"][:], waits=[t_q])
        t_rp = p.op("vector", "tensor_scalar", out=sc2[:, 1:2], in0=sc2[:, 1:2],
                    scalar1=1e-8, scalar2=None, op0=ALU.add, waits=[t_rm])
        t_rc = p.op("vector", "reciprocal", out=sc2[:, 2:3], in_=sc2[:, 1:2], waits=[t_rp])
        t_sc = p.op("vector", "tensor_mul", out=sc2[:, 3:4], in0=sc2[:, 2:3],
                    in1=sb["rs"][:], waits=[t_rc])
        t_sb = p.op("vector", "scalar_tensor_tensor", out=sc2[:, 4:5],
                    in0=sb["agg"][:, 0:1], scalar=-1.0, in1=sc2[:, 3:4],
                    op0=ALU.mult, op1=ALU.mult, waits=[t_sc, t_agg])
        w_ap = [t_sc, t_sb, t_x] + ([tp_prev] if tp_prev else [])
        t_ap = p.op("scalar", "activation", out=lnb[:], in_=xs[b][:], func=AF.Identity,
                    bias=sc2[:, 4:5], scale=sc2[:, 3:4], waits=w_ap)
        x_free[b] = t_ap
        for c in range(8):
            wv = [t_ap, t_ms] + ([tp_prev] if tp_prev else [])
            tt = p.op("tensor", "transpose", out=pb_tp[:, c * 128:(c + 1) * 128],
                      in_=lnb[:, c * 128:(c + 1) * 128], identity=eyes[:],
                      waits=(wv if c == 0 else []))
        t_ev = p.op("vector", "tensor_copy", out=ffn_f[:],
                    in_=bass.AP(tensor=pb_tp, offset=0, ap=[[1024, 128], [1, 1024]]),
                    waits=[tt])
        ge = []
        for dc in range(8):
            bank = pb_u[dc % 2]
            wv = [t_ev, t_wu] + ([u_prev[dc % 2]] if u_prev[dc % 2] else [])
            for kc in range(8):
                tok = (p.op if kc == 7 else p.op_nosem)(
                    "tensor", "matmul", out=bank[:],
                    lhsT=ffn_f[:, kc * 128:(kc + 1) * 128],
                    rhs=wub[:, kc * DFF + dc * 512: kc * DFF + (dc + 1) * 512],
                    start=(kc == 0), stop=(kc == 7), waits=(wv if kc == 0 else []))
            t_g = p.op("scalar", "activation", out=hgel[:, dc * 512:(dc + 1) * 512],
                       in_=bank[:], func=AF.Gelu, bias=sb["cst"][:, 1:2], scale=1.0,
                       waits=[tok, t_c2] + ([hg_prev] if hg_prev else []))
            u_prev[dc % 2] = t_g
            ge.append(t_g)
        st = []
        for i in range(8):
            st.append(p.op("vector", "bn_stats", out=sb["bnr"][:, 6 * i:6 * i + 6],
                           in_=hgel[:, i * 512:(i + 1) * 512], waits=[ge[i]]))
        t_ag = p.op("vector", "bn_aggr", out=sb["agg"][:, 0:2], in_=sb["bnr"][:, 0:48],
                    waits=st)
        t_m2 = p.op("vector", "scalar_tensor_tensor", out=sc2[:, 5:6],
                    in0=sb["agg"][:, 0:1], scalar=1.0, in1=sb["agg"][:, 0:1],
                    op0=ALU.mult, op1=ALU.mult, waits=[t_ag])
        t_v2 = p.op("vector", "tensor_add", out=sc2[:, 5:6], in0=sc2[:, 5:6],
                    in1=sb["agg"][:, 1:2], waits=[t_m2])
        t_q2 = p.op("scalar", "activation", out=sc2[:, 5:6], in_=sc2[:, 5:6],
                    func=AF.Sqrt, bias=sb["cst"][:, 1:2], scale=1.0, waits=[t_v2])
        t_p2 = p.op("vector", "tensor_scalar", out=sc2[:, 5:6], in0=sc2[:, 5:6],
                    scalar1=1e-8, scalar2=None, op0=ALU.add, waits=[t_q2])
        t_r2 = p.op("vector", "reciprocal", out=sc2[:, 5:6], in_=sc2[:, 5:6], waits=[t_p2])
        t_hb = p.op("vector", "tensor_copy", out=hb[:], in_=hgel[:], waits=ge + [t_ag])
        hg_prev = t_hb
        t_hf = []
        for bt in range(4):
            wv = [t_hb] + ([t_ev2] if bt > 0 else [])
            for i in range(8):
                c = bt * 8 + i
                tt = p.op("tensor", "transpose", out=pb_tp[:, i * 128:(i + 1) * 128],
                          in_=hb[:, c * 128:(c + 1) * 128], identity=eyes[:],
                          waits=(wv if i == 0 else []))
            t_ev2 = p.op("vector", "tensor_copy",
                         out=hfb[:, bt * 1024:(bt + 1) * 1024],
                         in_=bass.AP(tensor=pb_tp, offset=0, ap=[[1024, 128], [1, 1024]]),
                         waits=[tt])
            t_hf.append(t_ev2)
        etoks = []
        for dc in range(2):
            bank = pb_d[dc]
            wv = t_hf + [t_wd] + ([d_prev[dc]] if d_prev[dc] else [])
            for kc in range(32):
                tok = (p.op if kc == 31 else p.op_nosem)(
                    "tensor", "matmul", out=bank[:],
                    lhsT=hfb[:, kc * 128:(kc + 1) * 128],
                    rhs=wdb[:, kc * D + dc * 512: kc * D + (dc + 1) * 512],
                    start=(kc == 0), stop=(kc == 31), waits=(wv if kc == 0 else []))
            te = p.op("vector", "scalar_tensor_tensor",
                      out=x2s[:, dc * 512:(dc + 1) * 512], in0=bank[:],
                      scalar=sc2[:, 5:6], in1=xs[b][:, dc * 512:(dc + 1) * 512],
                      op0=ALU.mult, op1=ALU.add,
                      waits=[tok, t_r2, t_x] + ([out_prev] if out_prev else []))
            d_prev[dc] = te
            etoks.append(te)
        x_free[b] = etoks[-1]
        out_prev = p.dma("sync", x2o.rearrange("(n p) d -> n p d", p=128)[t],
                         x2s[:], sem=s_out, waits=etoks)
    p._waits("sync", [p.dma_all(s_out)])


def st4a_body(p, g, C, T):
    nc = p.nc
    NT = T // 128
    NCH = T // 64
    NP = NCH // 2
    QB = min(4, NP)
    NBB = NP // QB
    TE = T + 1

    x = g["x2"].ap()
    xh = g["xhg"].ap()[0:1, :]
    wmT = g["wmT"]
    alog = g["alog"]
    dtb = g["dtb"]
    eye = g["eye"].ap()
    maskT = g["maskT"].ap()

    gro = g["gro"].ap()
    edfo = g["edf"].ap()
    ln2fo = g["ln2f"].ap()
    lnfx_o = g["lnfx"].ap()
    vbet_o = g["vbet"].ap()
    wkdw_o = g["wkdw"].ap()
    dcd = g["dcd"]
    gd = nc.dram_tensor("gd", [H, NCH], F32, kind="Internal")

    xs = [C.sb(f"xs{i}", [128, D], F32) for i in range(2)]
    lnb = C.sb("lnb", [128, D], BF16)
    ln2fx = C.sb("ln2fx", [128, 8 * TE], BF16)
    wmb = C.sb("wmb", [128, 8 * 1072], BF16)
    vbet = C.sb("vbet_s", [128, NT * D], BF16)
    wkdw = C.sb("wkdw_s", [128, NT * D], BF16)
    gr = C.sb("gr", [128, NT * 48], F32)
    eyes = C.sb("eyes", [128, 128], BF16)
    eyesF = C.sb("eyesF", [128, 128], F32)
    maskS = C.sb("maskS", [128, 64], F32)
    dtbS = C.sb("dtbS", [128, H], F32)
    negea = C.sb("negea", [128, H], F32)
    dect = C.sb("dect", [128, NT * H], F32)
    decF = C.sb("decF", [16, T], F32)
    dcum = C.sb("dcum", [16, T], F32)
    ones16 = C.sb("ones16", [16, 64], F32)
    gF = C.sb("gF", [16, NCH], F32)
    sendF = C.sb("sendF", [16, NCH + 1], F32)
    fmscr = C.sb("fmscr", [16, T], F32)
    dcumt = g["sh_dcumt"]
    dext = g["sh_dext"]
    edft = C.sb("edft", [128, NT * H], F32)
    dwt = C.sb("dwt", [128, NT * H], F32)
    gcols = g["sh_gcols"]
    hstat = C.sb("hstat", [1, 16], F32)
    sb = {
        "bnr": C.sb("bnr", [128, 12], F32),
        "agg": C.sb("agg", [128, 2], F32),
        "rs": C.sb("rs", [128, 1], F32),
        "bi": C.sb("bi", [128, 1], F32),
        "cst": C.sb("cst", [128, 4], F32),
    }
    scr = C.sb("scr", [128, 48], F32)

    pb_tp = C.ps("pb_tp", [128, 1024], BF16)
    pbF = [C.ps(f"pbF{i}", [128, 512], F32) for i in range(7)]
    pb_v = [pbF[0], pbF[1]]
    pb_gt = pbF[5][:, 0:48]
    pb_dt = pbF[5][:, 256:384]

    s_w = p.dma_sem("w")
    for kc in range(8):
        p.dma("gpsimd", wmb[:, kc * 1072:(kc + 1) * 1072],
              wmT.ap()[kc * 128:(kc + 1) * 128, :], sem=s_w)
    t_w = p.dma_all(s_w)
    s_ms = p.dma_sem("misc")
    p.dma("sync", eyes[:], eye, sem=s_ms)
    p.dma("sync", maskS[0:64, :], maskT, sem=s_ms)
    p.dma("sync", maskS[64:128, :], maskT, sem=s_ms)
    p.dma("sync", dtbS[:], bass.AP(tensor=dtb, offset=0, ap=[[0, 128], [1, H]]), sem=s_ms)
    p.dma("sync", negea[:], bass.AP(tensor=alog, offset=0, ap=[[0, 128], [1, H]]), sem=s_ms)
    s_hx = p.dma_sem("hx")
    t_ms = p.dma_all(s_ms)
    t_c = p.op("vector", "memset", sb["cst"][:, 0:1], 1e-5)
    t_c2 = p.op("vector", "memset", sb["cst"][:, 1:2], 0.0)
    t_c3 = p.op("vector", "memset", sb["cst"][:, 2:3], 1.0)
    t_o16 = p.op("vector", "memset", ones16[:], 1.0)
    t_eyf = p.op("vector", "tensor_copy", out=eyesF[:], in_=eyes[:], waits=[t_ms])
    t_snd = p.op("vector", "memset", sendF[:, 0:1], 0.0)
    t_na = p.op("scalar", "activation", out=negea[:], in_=negea[:], func=AF.Exp,
                bias=sb["cst"][:, 1:2], scale=1.0, waits=[t_ms, t_c2])
    t_na = p.op("vector", "tensor_scalar", out=negea[:], in0=negea[:],
                scalar1=-1.0, scalar2=None, op0=ALU.mult, waits=[t_na])

    xhs = C.sb("xhs", [1, D], F32)
    lnh = C.sb("lnh", [1, D], BF16)
    hms = C.sb("hms", [1, 1], F32)
    t_xh0 = p.dma("sync", xhs[:], xh, sem=s_hx)
    t_hm = p.dma("sync", hms[:], g["hmask"].ap(), sem=s_hx)
    t_xh = p.op("scalar", "activation", out=xhs[:], in_=xhs[:], func=AF.Identity,
                bias=sb["cst"][0:1, 1:2], scale=hms[0:1, 0:1],
                waits=[t_xh0, t_hm, t_c2])
    h1 = p.op("vector", "bn_stats", out=hstat[0:1, 0:6], in_=xhs[0:1, 0:512], waits=[t_xh])
    h2 = p.op("vector", "bn_stats", out=hstat[0:1, 6:12], in_=xhs[0:1, 512:1024], waits=[h1])
    h3 = p.op("vector", "bn_aggr", out=hstat[0:1, 12:14], in_=hstat[0:1, 0:12], waits=[h2])
    h4 = p.op("scalar", "activation", out=hstat[0:1, 14:15], in_=hstat[0:1, 13:14],
              func=AF.Ln, bias=sb["cst"][0:1, 0:1], scale=1.0, waits=[h3, t_c])
    h5 = p.op("scalar", "activation", out=hstat[0:1, 14:15], in_=hstat[0:1, 14:15],
              func=AF.Exp, bias=sb["cst"][0:1, 1:2], scale=-0.5, waits=[h4, t_c2])
    h6 = p.op("vector", "scalar_tensor_tensor", out=hstat[0:1, 15:16],
              in0=hstat[0:1, 12:13], scalar=-1.0, in1=hstat[0:1, 14:15],
              op0=ALU.mult, op1=ALU.mult, waits=[h3, h5])
    h7 = p.op("scalar", "activation", out=lnh[:], in_=xhs[:], func=AF.Identity,
              bias=hstat[0:1, 15:16], scale=hstat[0:1, 14:15], waits=[h5, h6])
    for c in range(8):
        tt = p.op("tensor", "transpose", out=pb_tp[:, 2 * c:2 * c + 1],
                  in_=lnh[0:1, c * 128:(c + 1) * 128], identity=eyes[0:1, 0:1],
                  waits=([h7, t_ms] if c == 0 else []))
    hdst = bass.AP(tensor=ln2fx, offset=0, ap=[[8 * TE, 128], [TE, 8], [1, 1]])
    t_hev = p.op("vector", "tensor_copy", out=hdst,
                 in_=bass.AP(tensor=pb_tp, offset=0, ap=[[1024, 128], [2, 8]]),
                 waits=[tt])

    s_x = [p.dma_sem("x0"), p.dma_sem("x1b")]
    x_free = [None, None]
    tp_prev = t_hev
    v_prev = [None, None]
    gt_prev = None
    ln_done = []
    for t in range(NT):
        b = t % 2
        w = [x_free[b]] if x_free[b] else []
        t_x = p.dma("sync", xs[b][:], x.rearrange("(n p) d -> n p d", p=128)[t],
                    sem=s_x[b], waits=w)
        t_rs, t_bi, _ = ln_stats(p, xs[b], sb, [t_c, t_c2], [t_x, t_c, t_c2])
        w_ap = [t_rs, t_bi, t_x, tp_prev]
        t_ap = p.op("scalar", "activation", out=lnb[:], in_=xs[b][:], func=AF.Identity,
                    bias=sb["bi"][:, 0:1], scale=sb["rs"][:, 0:1], waits=w_ap)
        x_free[b] = t_ap
        for c in range(8):
            tt = p.op("tensor", "transpose", out=pb_tp[:, c * 128:(c + 1) * 128],
                      in_=lnb[:, c * 128:(c + 1) * 128], identity=eyes[:],
                      waits=([t_ap, t_ms, tp_prev] if c == 0 else []))
        dst = bass.AP(tensor=ln2fx, offset=t * 128 + 1,
                      ap=[[8 * TE, 128], [TE, 8], [1, 128]])
        src = bass.AP(tensor=pb_tp, offset=0, ap=[[1024, 128], [128, 8], [1, 128]])
        t_ev = p.op("vector", "tensor_copy", out=dst, in_=src, waits=[tt])
        tp_prev = t_ev
        ln_done.append(t_ev)
        wvg = [t_ev, t_w] + ([gt_prev] if gt_prev else [])
        for kc in range(8):
            tokg = (p.op if kc == 7 else p.op_nosem)(
                "tensor", "matmul", out=pb_gt,
                lhsT=ln2fx[:, kc * TE + t * 128 + 1: kc * TE + (t + 1) * 128 + 1],
                rhs=wmb[:, kc * 1072 + 1024: kc * 1072 + 1072],
                start=(kc == 0), stop=(kc == 7), waits=(wvg if kc == 0 else []))
        t_gev = p.op("vector", "tensor_copy", out=gr[:, t * 48:(t + 1) * 48],
                     in_=pb_gt, waits=[tokg])
        gt_prev = t_gev
        t_be = p.op("scalar", "activation", out=scr[:, 0:16], in_=gr[:, t * 48 + 16: t * 48 + 32],
                    func=AF.Sigmoid, bias=sb["cst"][:, 1:2], scale=1.0, waits=[t_gev, t_c2])
        t_z = p.op("vector", "tensor_add", out=scr[:, 16:32],
                   in0=gr[:, t * 48 + 32: t * 48 + 48], in1=dtbS[:], waits=[t_gev, t_ms])
        t_az = p.op("scalar", "activation", out=scr[:, 32:48], in_=scr[:, 16:32],
                    func=AF.Abs, bias=sb["cst"][:, 1:2], scale=1.0, waits=[t_z])
        t_ex = p.op("scalar", "activation", out=scr[:, 32:48], in_=scr[:, 32:48],
                    func=AF.Exp, bias=sb["cst"][:, 1:2], scale=-1.0, waits=[t_az])
        t_ln = p.op("scalar", "activation", out=scr[:, 32:48], in_=scr[:, 32:48],
                    func=AF.Ln, bias=sb["cst"][:, 2:3], scale=1.0, waits=[t_ex, t_c3])
        t_rl = p.op("vector", "tensor_scalar_max", out=scr[:, 16:32], in0=scr[:, 16:32],
                    scalar1=0.0, waits=[t_ln])
        t_sp = p.op("vector", "tensor_add", out=scr[:, 16:32], in0=scr[:, 16:32],
                    in1=scr[:, 32:48], waits=[t_rl])
        t_dc = p.op("vector", "tensor_mul", out=dect[:, t * H:(t + 1) * H],
                    in0=scr[:, 16:32], in1=negea[:], waits=[t_sp, t_na])
        for dc in range(2):
            bank = pb_v[dc]
            wv = [t_ev, t_w] + ([v_prev[dc]] if v_prev[dc] else [])
            for kc in range(8):
                tokv = (p.op if kc == 7 else p.op_nosem)(
                    "tensor", "matmul", out=bank[:],
                    lhsT=ln2fx[:, kc * TE + t * 128 + 1: kc * TE + (t + 1) * 128 + 1],
                    rhs=wmb[:, kc * 1072 + dc * 512: kc * 1072 + (dc + 1) * 512],
                    start=(kc == 0), stop=(kc == 7), waits=(wv if kc == 0 else []))
            bview = bass.AP(tensor=scr, offset=dc * 8,
                            ap=[[48, 128], [1, 8], [0, 64]])
            t_vb = p.op("vector", "tensor_tensor",
                        out=vbet[:, t * D + dc * 512: t * D + (dc + 1) * 512],
                        in0=bank[:], in1=bview, op=ALU.mult, waits=[tokv, t_be])
            v_prev[dc] = t_vb
    t_lnall = p.tok("vector")

    dt_prev = None
    for t in range(NT):
        wv = [p.tok("vector"), t_ms] + ([dt_prev] if dt_prev else [])
        tt = p.op("tensor", "transpose", out=pbF[5][0:16, 256:384],
                  in_=dect[:, t * H:(t + 1) * H], identity=eyesF[:],
                  waits=wv + [t_eyf])
        dt_prev = p.op("vector", "tensor_copy", out=decF[:, t * 128:(t + 1) * 128],
                       in_=pbF[5][0:16, 256:384], waits=[tt])
    sc_prev = []
    for ch in range(NCH):
        tsc = p.op("vector", "tensor_tensor_scan", out=dcum[:, ch * 64:(ch + 1) * 64],
                   data0=ones16[:], data1=decF[:, ch * 64:(ch + 1) * 64],
                   initial=0.0, op0=ALU.mult, op1=ALU.add,
                   waits=[dt_prev, t_o16] if ch == 0 else [])
        sc_prev.append(tsc)
    ends = bass.AP(tensor=dcum, offset=63, ap=[[T, 16], [64, NCH]])
    t_g = p.op("scalar", "activation", out=gF[:], in_=ends, func=AF.Exp,
               bias=sb["cst"][0:16, 1:2], scale=1.0, waits=sc_prev[-1:] + sc_prev[:1])
    t_se = p.op("vector", "tensor_tensor_scan", out=sendF[:, 1:NCH + 1],
                data0=ones16[:, 0:NCH], data1=ends, initial=0.0,
                op0=ALU.mult, op1=ALU.add, waits=[sc_prev[-1], t_snd])
    oview = bass.AP(tensor=sendF, offset=0, ap=[[NCH + 1, 16], [1, NCH], [0, 64]])
    endsb = bass.AP(tensor=dcum, offset=63, ap=[[T, 16], [64, NCH], [0, 64]])

    def fm_to_tok(srcF, dstT, wtok, tt_prev):
        for t in range(NT):
            wv = [wtok, t_ms, t_eyf] + ([tt_prev] if tt_prev else [])
            tt = p.op("tensor", "transpose", out=pbF[5][0:128, 256:272],
                      in_=srcF[:, t * 128:(t + 1) * 128], identity=eyesF[0:16, 0:16],
                      waits=wv)
            tt_prev = p.op("vector", "tensor_copy", out=dstT[:, t * H:(t + 1) * H],
                           in_=pbF[5][0:128, 256:272], waits=[tt])
        return tt_prev, tt

    tt_prev, tlast = fm_to_tok(dcum, dcumt, sc_prev[-1], None)
    t_1 = p.op("vector", "tensor_add", out=fmscr[:], in0=dcum[:], in1=oview,
               waits=[t_se] + sc_prev[-1:])
    t_2 = p.op("scalar", "activation", out=fmscr[:], in_=fmscr[:], func=AF.Exp,
               bias=sb["cst"][0:16, 1:2], scale=1.0, waits=[t_1])
    tt_prev, tlast = fm_to_tok(fmscr, edft, t_2, tt_prev)
    t_3 = p.op("scalar", "activation", out=fmscr[:], in_=dcum[:], func=AF.Exp,
               bias=sb["cst"][0:16, 1:2], scale=1.0, waits=[t_2, (tlast)])
    tt_prev, tlast = fm_to_tok(fmscr, dext, t_3, tt_prev)
    t_4 = p.op("vector", "tensor_sub", out=fmscr[:], in0=dcum[:], in1=endsb,
               waits=[t_1, (tlast)])
    t_5 = p.op("scalar", "activation", out=fmscr[:], in_=fmscr[:], func=AF.Exp,
               bias=sb["cst"][0:16, 1:2], scale=-1.0, waits=[t_4])
    tt_prev, tlast = fm_to_tok(fmscr, dwt, t_5, tt_prev)
    t_decall = tt_prev

    wk_prev = None
    for t in range(NT):
        for c in range(8):
            tt = p.op("tensor", "transpose", out=pb_tp[:, c * 128:(c + 1) * 128],
                      in_=ln2fx[:, c * TE + t * 128: c * TE + (t + 1) * 128],
                      identity=eyes[:],
                      waits=([t_lnall, t_hev, t_ms] + ([wk_prev] if wk_prev else [])
                             if c == 0 else []))
        dwview = bass.AP(tensor=dwt, offset=t * H, ap=[[NT * H, 128], [1, H], [0, 64]])
        wk_prev = p.op("vector", "tensor_tensor", out=wkdw[:, t * D:(t + 1) * D],
                       in0=bass.AP(tensor=pb_tp, offset=0, ap=[[1024, 128], [1, 1024]]),
                       in1=dwview, op=ALU.mult, waits=[tt, t_decall])
    t_wkall = wk_prev

    s_d1 = p.dma_sem("dcumd")
    s_d2 = p.dma_sem("gdd")
    t_d1 = p.dma("sync", dcd.ap(), dcum[:], sem=s_d1, waits=[p.tok("scalar"), p.tok("vector")])
    t_d2 = p.dma("sync", gd.ap(), gF[:], sem=s_d2, waits=[t_g])
    t_gc0 = p.dma("sync", gcols[0:64, :],
                  bass.AP(tensor=gd, offset=0, ap=[[0, 64], [2 * NCH, 8], [1, NCH]]),
                  sem=s_d2, waits=[t_d2])
    t_gc1 = p.dma("sync", gcols[64:128, :],
                  bass.AP(tensor=gd, offset=NCH, ap=[[0, 64], [2 * NCH, 8], [1, NCH]]),
                  sem=s_d2, waits=[t_gc0])
    t_gc = p.dma_all(s_d2)

    s_out = p.dma_sem("out")
    for c in range(8):
        p.dma("sync", lnfx_o[:, c * TE:(c + 1) * TE], ln2fx[:, c * TE:(c + 1) * TE],
              sem=s_out, waits=[t_lnall, t_hev])
        p.dma("sync", ln2fo[c * 128:(c + 1) * 128, :],
              ln2fx[:, c * TE + 1:(c + 1) * TE], sem=s_out, waits=[t_lnall, t_hev])
    for t in range(0, NT, 2):
        p.dma("sync", vbet_o[:, t * D:(t + 2) * D], vbet[:, t * D:(t + 2) * D],
              sem=s_out, waits=[t_lnall])
        p.dma("gpsimd", wkdw_o[:, t * D:(t + 2) * D], wkdw[:, t * D:(t + 2) * D],
              sem=s_out, waits=[t_wkall])
    p.dma("sync", edfo, edft[:], sem=s_out, waits=[t_decall])
    p.dma("sync", gro, gr[:], sem=s_out, waits=[t_lnall])
    p._waits("sync", [p.dma_all(s_out), p.dma_all(s_d1), p.dma_all(s_d2)])
    p._waits("gpsimd", [p.dma_all(s_out)])


def st4b_body(p, g, C, T):
    nc = p.nc
    NT = T // 128
    NCH = T // 64
    NP = NCH // 2
    QB = min(4, NP)
    NBB = NP // QB
    TE = T + 1

    dcd = g["dcd"]
    maskT = g["maskT"].ap()

    oo = g["o_local"].ap()
    sfo = g["sfin"].ap()

    ln2fx = C.sb("ln2fx", [128, 8 * TE], BF16)
    vbet = C.sb("vbet_s", [128, NT * D], BF16)
    wkdw = C.sb("wkdw_s", [128, NT * D], BF16)
    dcumt = g["sh_dcumt"]
    dext = g["sh_dext"]
    gcols = g["sh_gcols"]
    maskS = C.sb("maskS", [128, 64], F32)
    spair = C.sb("spair", [128, 8 * 64], F32)
    spbf = C.sb("spbf", [128, 8 * 64], BF16)
    dcb = [C.sb(f"dcb{i}", [128, T], F32) for i in range(2)]
    e2m = [C.sb(f"e2m{i}", [128, T // 2], F32) for i in range(2)]
    itg = C.sb("itg", [128, NBB * QB * 128], BF16)
    ostg = [C.sb(f"ostg{i}", [128, 512], F32) for i in range(2)]
    ostg2 = [C.sb(f"ostg2{i}", [128, 512], F32) for i in range(2)]
    cst = C.sb("cst", [128, 2], F32)

    pbF = [C.ps(f"pbF{i}", [128, 512], F32) for i in range(7)]
    pb_os = [pbF[0], pbF[1]]
    pb_io = [pbF[2], pbF[3]]
    pb_sc = pbF[4]
    pb_su = [pbF[5][:, 64:128], pbF[6][:, 0:64]]

    lnfx_d = g["lnfx"]
    vbet_d = g["vbet"]
    wkdw_d = g["wkdw"]
    s_w = p.dma_sem("w")
    s_wa = p.dma_sem("w2a")
    s_wc = p.dma_sem("w2c")
    for c in range(8):
        p.dma("sync", ln2fx[:, c * TE:(c + 1) * TE],
              lnfx_d.ap()[:, c * TE:(c + 1) * TE], sem=s_wa)
    for t in range(0, NT, 2):
        p.dma("gpsimd", vbet[:, t * D:(t + 2) * D], vbet_d.ap()[:, t * D:(t + 2) * D], sem=s_w)
        p.dma("scalar", wkdw[:, t * D:(t + 2) * D], wkdw_d.ap()[:, t * D:(t + 2) * D], sem=s_wc)
    t_wj = [p.dma_all(s_wa), p.dma_all(s_w), p.dma_all(s_wc)]
    s_ms = p.dma_sem("misc")
    p.dma("sync", maskS[0:64, :], maskT, sem=s_ms)
    p.dma("sync", maskS[64:128, :], maskT, sem=s_ms)
    t_ms = p.dma_all(s_ms)
    t_s0 = p.op("vector", "memset", spair[:], 0.0)
    t_s0b = p.op("vector", "memset", spbf[:], 0.0)
    t_c2 = p.op("vector", "memset", cst[:, 1:2], 0.0)
    t_w = p.op("vector", "memset", cst[:, 0:1], 0.0, waits=t_wj)

    t_lnall = t_w
    t_decall = None
    t_wkall = t_w
    t_gc = None
    t_hev = t_w

    s_dc = [p.dma_sem("dcb0"), p.dma_sem("dcb1")]
    s_oout = p.dma_sem("oout")
    dcb_free = [None, None]
    e2_free = [None, None]
    itg_free = None
    os_free = [None, None]
    io_free = [None, None]
    su_free = [None, None]
    ost_free = [None, None]
    sc_free = None
    o_dma = []
    for j in range(8):
        tdc = []
        for half in range(2):
            h = 2 * j + half
            w = [dcb_free[half]]
            tdc.append(p.dma("sync", dcb[half][:],
                             bass.AP(tensor=dcd, offset=h * T, ap=[[0, 128], [1, T]]),
                             sem=s_dc[half], waits=[t for t in w if t]))
        e2tok = []
        dcb_last = [None, None]
        for half in range(2):
            h = 2 * j + half
            toks = []
            for par in range(2):
                dccview = bass.AP(tensor=dcb[half], offset=par * 64 * T + par * 64,
                                  ap=[[T, 64], [128, NP], [1, 64]])
                dckview = bass.AP(tensor=dcumt, offset=par * 64 * (NT * H) + h,
                                  ap=[[NT * H, 64], [H, NT], [0, 64]])
                wv = [tdc[half], t_decall] + ([e2_free[half]] if e2_free[half] else [])
                t1 = p.op("vector", "tensor_tensor",
                          out=e2m[half][par * 64:(par + 1) * 64, :],
                          in0=dccview, in1=dckview,
                          op=ALU.subtract, waits=wv)
                dcb_last[half] = t1
                mview = bass.AP(tensor=maskS, offset=par * 64 * 64,
                                ap=[[64, 64], [0, NP], [1, 64]])
                t1b = p.op("gpsimd", "tensor_tensor",
                           out=e2m[half][par * 64:(par + 1) * 64, :],
                           in0=e2m[half][par * 64:(par + 1) * 64, :], in1=mview,
                           op=ALU.add, waits=[t1, t_ms])
                t2 = p.op("scalar", "activation", out=e2m[half][par * 64:(par + 1) * 64, :],
                          in_=e2m[half][par * 64:(par + 1) * 64, :], func=AF.Exp,
                          bias=cst[par * 64:(par + 1) * 64, 1:2], scale=1.0,
                          waits=[t1b, t_c2])
                toks.append(t2)
            e2tok.append(toks)
        dcb_free = dcb_last
        it_toks = []
        sc_tok = sc_free
        for bb in range(NBB):
            for q in range(QB):
                m = bb * QB + q
                for par in range(2):
                    i = 2 * m + par
                    for half in range(2):
                        h = 2 * j + half
                        wv = [t_lnall, t_hev] + ([sc_tok] if sc_tok else [])
                        tm = p.op("tensor", "matmul",
                                  out=pb_sc[par * 64:(par + 1) * 64,
                                            q * 128 + half * 64: q * 128 + (half + 1) * 64],
                                  lhsT=ln2fx[half * 64:(half + 1) * 64,
                                             j * TE + i * 64: j * TE + (i + 1) * 64],
                                  rhs=ln2fx[half * 64:(half + 1) * 64,
                                            j * TE + i * 64 + 1: j * TE + (i + 1) * 64 + 1],
                                  start=True, stop=True,
                                  tile_position=(half * 64, par * 64), waits=wv)
            for half in range(2):
                h = 2 * j + half
                e2view = bass.AP(tensor=e2m[half], offset=bb * QB * 64,
                                 ap=[[T // 2, 128], [64, QB], [1, 64]])
                dst = bass.AP(tensor=itg, offset=bb * QB * 128 + half * 64,
                              ap=[[NBB * QB * 128, 128], [128, QB], [1, 64]])
                srcv = bass.AP(tensor=pb_sc, offset=half * 64,
                               ap=[[512, 128], [128, QB], [1, 64]])
                wv = [tm] + e2tok[half] + ([itg_free] if itg_free else [])
                t_it = p.op("vector", "tensor_tensor", out=dst, in0=srcv, in1=e2view,
                            op=ALU.mult, waits=wv)
                it_toks.append(t_it)
            sc_tok = it_toks[-1]
        sc_free = sc_tok
        e2_free = [it_toks[-2], it_toks[-1]]
        st_prev = [t_s0, t_s0b, t_s0b]
        for i in range(NCH):
            m, par = i // 2, i % 2
            bb, q = m // QB, m % QB
            ba = bb % 2
            for half in range(2):
                wv = [st_prev[-1], t_lnall] + ([os_free[ba]] if (i % (2 * QB) == 0 and os_free[ba]) else [])
                tos = p.op("tensor", "matmul",
                           out=pb_os[ba][par * 64:(par + 1) * 64,
                                         q * 128 + half * 64: q * 128 + (half + 1) * 64],
                           lhsT=ln2fx[half * 64:(half + 1) * 64,
                                      j * TE + i * 64 + 1: j * TE + (i + 1) * 64 + 1],
                           rhs=spbf[half * 64:(half + 1) * 64, j * 64:(j + 1) * 64],
                           start=True, stop=True,
                           tile_position=(half * 64, par * 64), waits=wv)
            for half in range(2):
                wv = it_toks + [t_lnall] + ([io_free[ba]] if (i % (2 * QB) == 0 and io_free[ba]) else [])
                tio = p.op("tensor", "matmul",
                           out=pb_io[ba][par * 64:(par + 1) * 64,
                                         q * 128 + half * 64: q * 128 + (half + 1) * 64],
                           lhsT=bass.AP(tensor=itg,
                                        offset=par * 64 * (NBB * QB * 128) + bb * QB * 128 + q * 128 + half * 64,
                                        ap=[[NBB * QB * 128, 64], [1, 64]]),
                           rhs=vbet[par * 64:(par + 1) * 64,
                                    m * 1024 + (2 * j + half) * 64: m * 1024 + (2 * j + half + 1) * 64],
                           start=True, stop=True,
                           tile_position=(par * 64, par * 64), waits=wv)
            for half in range(2):
                wv = [t_wkall, st_prev[-1]] + ([su_free[par]] if su_free[par] else [])
                tu = p.op("tensor", "matmul",
                          out=pb_su[par][half * 64:(half + 1) * 64, 0:64],
                          lhsT=wkdw[par * 64:(par + 1) * 64,
                                    m * 1024 + (2 * j + half) * 64: m * 1024 + (2 * j + half + 1) * 64],
                          rhs=vbet[par * 64:(par + 1) * 64,
                                   m * 1024 + (2 * j + half) * 64: m * 1024 + (2 * j + half + 1) * 64],
                          start=True, stop=True,
                          tile_position=(par * 64, half * 64), waits=wv)
            tst = p.op("vector", "scalar_tensor_tensor", out=spair[:, j * 64:(j + 1) * 64],
                       in0=spair[:, j * 64:(j + 1) * 64],
                       scalar=gcols[:, j * NCH + i: j * NCH + i + 1],
                       in1=pb_su[par], op0=ALU.mult, op1=ALU.add,
                       waits=[tu, tos, t_gc, st_prev[-1]])
            su_free[par] = tst
            tsh = p.op("vector", "tensor_copy", out=spbf[:, j * 64:(j + 1) * 64],
                       in_=spair[:, j * 64:(j + 1) * 64], waits=[tst])
            st_prev.append(tsh)
            if i % (2 * QB) == 2 * QB - 1:
                stoks = []
                for q2 in range(QB):
                    for half in range(2):
                        hcol = (2 * j + half)
                        t_sc1 = p.op("scalar", "activation",
                                     out=ostg[ba][:, q2 * 128 + half * 64: q2 * 128 + (half + 1) * 64],
                                     in_=pb_os[ba][:, q2 * 128 + half * 64: q2 * 128 + (half + 1) * 64],
                                     func=AF.Identity, bias=cst[:, 1:2],
                                     scale=dext[:, (bb * QB + q2) * H + hcol: (bb * QB + q2) * H + hcol + 1],
                                     waits=[tos, t_decall, t_c2] + ([ost_free[ba]] if ost_free[ba] else []))
                        stoks.append(t_sc1)
                t_add = p.op("vector", "tensor_add", out=ostg2[ba][:, 0:QB * 128],
                             in0=ostg[ba][:, 0:QB * 128],
                             in1=pb_io[ba][:, 0:QB * 128], waits=stoks + [tio])
                os_free[ba] = t_add
                io_free[ba] = t_add
                odst = bass.AP(tensor=oo.tensor, offset=(bb * QB) * 128 * D + (2 * j) * 64,
                               ap=[[D, 128], [128 * D, QB], [64, 2], [1, 64]])
                t_od = p.dma("scalar", odst, ostg2[ba][:, 0:QB * 128], sem=s_oout, waits=[t_add])
                ost_free[ba] = t_od
                o_dma.append(t_od)
        itg_free = p.tok("tensor")

    s_out = p.dma_sem("out")
    p.dma("sync", sfo, spair[:], sem=s_out, waits=[p.tok("vector")])
    p._waits("sync", [p.dma_all(s_out), p.dma_all(s_oout)])
    p._waits("scalar", [p.dma_all(s_oout)])



def st5_body(p, g, C, T):
    nc = p.nc
    NT = T // 128
    NCH = T // 64
    NP = NCH // 2
    QB = min(4, NP)
    NBB = NP // QB

    x = g["x2"].ap()
    od = g["o_local"]
    lnfd = g["ln2f"]
    grd = g["gro"].ap()
    edfd = g["edf"].ap()
    sind = g["sfg"].ap()[0:128, :]
    woutT = g["woutT"]
    eye = g["eye"].ap()
    x3o = g["x3"].ap()

    lnf = C.sb("lnf", [128, 8 * T], BF16)
    olo = C.sb("olo", [128, NT * D], F32)
    gr = C.sb("grs", [128, NT * 48], F32)
    gsig = C.sb("gsig", [128, NT * H], F32)
    edf = C.sb("edfs", [128, NT * H], F32)
    sinf = C.sb("sinf", [128, 8 * 64], F32)
    sinb = C.sb("sinb", [128, 8 * 64], BF16)
    wob = C.sb("wob", [128, 8 * D], BF16)
    eyes = C.sb("eyes", [128, 128], BF16)
    att = C.sb("att", [128, NT * D], BF16)
    attf = C.sb("attf", [128, 8 * T], BF16)
    ostg = [C.sb(f"ostg{i}", [128, 512], F32) for i in range(2)]
    xs = [C.sb(f"xs{i}", [128, D], F32) for i in range(2)]
    ln2s = C.sb("ln2s", [128, D], F32)
    x3s = C.sb("x3s", [128, D], F32)
    sb = {
        "bnr": C.sb("bnr", [128, 12], F32),
        "agg": C.sb("agg", [128, 2], F32),
        "rs": C.sb("rs", [128, 1], F32),
        "bi": C.sb("bi", [128, 1], F32),
        "cst": C.sb("cst", [128, 4], F32),
    }

    pb_c = [C.ps(f"pb_c{i}", [128, 512], F32) for i in range(2)]
    pb_tp = C.ps("pb_tp", [128, 1024], BF16)
    pb_g = [C.ps(f"pb_g{i}", [128, 512], F32) for i in range(2)]

    s_ld = p.dma_sem("ld")
    for c in range(8):
        p.dma("sync", lnf[:, c * T:(c + 1) * T], lnfd.ap()[c * 128:(c + 1) * 128, :], sem=s_ld)
    for t in range(NT):
        p.dma("sync", olo[:, t * D:(t + 1) * D],
              od.ap().rearrange("(n p) d -> n p d", p=128)[t], sem=s_ld)
    hmb = C.sb("hmb", [128, 1], F32)
    p.dma("sync", gr[:], grd, sem=s_ld)
    p.dma("sync", edf[:], edfd, sem=s_ld)
    p.dma("sync", sinf[:], sind, sem=s_ld)
    p.dma("sync", hmb[:], bass.AP(tensor=g["hmask"], offset=0, ap=[[0, 128], [1, 1]]),
          sem=s_ld)
    p.dma("sync", eyes[:], eye, sem=s_ld)
    t_ld = p.dma_all(s_ld)
    s_w = p.dma_sem("w")
    for kc in range(8):
        p.dma("gpsimd", wob[:, kc * D:(kc + 1) * D],
              woutT.ap()[kc * 128:(kc + 1) * 128, :], sem=s_w)
    t_w = p.dma_all(s_w)
    t_c = p.op("vector", "memset", sb["cst"][:, 0:1], 1e-5)
    t_c2 = p.op("vector", "memset", sb["cst"][:, 1:2], 0.0)
    t_gm = p.op("scalar", "activation", out=sinf[:], in_=sinf[:], func=AF.Identity,
                bias=sb["cst"][:, 1:2], scale=hmb[:, 0:1], waits=[t_ld, t_c2])
    t_sb = p.op("vector", "tensor_copy", out=sinb[:], in_=sinf[:], waits=[t_gm])
    gview_in = bass.AP(tensor=gr, offset=0, ap=[[NT * 48, 128], [48, NT], [1, 16]])
    gview_out = bass.AP(tensor=gsig, offset=0, ap=[[NT * H, 128], [16, NT], [1, 16]])
    t_gs = p.op("scalar", "activation", out=gview_out, in_=gview_in, func=AF.Sigmoid,
                bias=sb["cst"][:, 1:2], scale=1.0, waits=[t_ld, t_c2])

    c_free = [None, None]
    st_free = [None, None]
    att_prev = None
    for j in range(8):
        for bb in range(NBB):
            ba = bb % 2
            for q in range(QB):
                m = bb * QB + q
                for par in range(2):
                    i = 2 * m + par
                    for half in range(2):
                        wv = [t_ld, t_sb] + ([c_free[ba]] if (q == 0 and par == 0 and half == 0 and c_free[ba]) else [])
                        tm = p.op("tensor", "matmul",
                                  out=pb_c[ba][par * 64:(par + 1) * 64,
                                               q * 128 + half * 64: q * 128 + (half + 1) * 64],
                                  lhsT=lnf[half * 64:(half + 1) * 64,
                                           j * T + i * 64: j * T + (i + 1) * 64],
                                  rhs=sinb[half * 64:(half + 1) * 64, j * 64:(j + 1) * 64],
                                  start=True, stop=True,
                                  tile_position=(half * 64, par * 64), waits=wv)
            stoks = []
            for q2 in range(QB):
                for half in range(2):
                    hcol = 2 * j + half
                    t_s1 = p.op("scalar", "activation",
                                out=ostg[ba][:, q2 * 128 + half * 64: q2 * 128 + (half + 1) * 64],
                                in_=pb_c[ba][:, q2 * 128 + half * 64: q2 * 128 + (half + 1) * 64],
                                func=AF.Identity, bias=sb["cst"][:, 1:2],
                                scale=edf[:, (bb * QB + q2) * H + hcol: (bb * QB + q2) * H + hcol + 1],
                                waits=[tm, t_ld, t_c2] + ([st_free[ba]] if st_free[ba] else []))
                    stoks.append(t_s1)
            oview = bass.AP(tensor=olo, offset=(bb * QB) * D + (2 * j) * 64,
                            ap=[[NT * D, 128], [D, QB], [64, 2], [1, 64]])
            t_a1 = p.op("vector", "tensor_add", out=ostg[ba][:, 0:QB * 128],
                        in0=ostg[ba][:, 0:QB * 128], in1=oview, waits=stoks + [t_ld])
            gview = bass.AP(tensor=gsig, offset=(bb * QB) * H + 2 * j,
                            ap=[[NT * H, 128], [16, QB], [1, 2], [0, 64]])
            adst = bass.AP(tensor=att, offset=(bb * QB) * D + (2 * j) * 64,
                           ap=[[NT * D, 128], [D, QB], [64, 2], [1, 64]])
            t_a2 = p.op("vector", "tensor_tensor", out=adst, in0=ostg[ba][:, 0:QB * 128],
                        in1=gview, op=ALU.mult, waits=[t_a1, t_gs])
            c_free[ba] = t_a2
            st_free[ba] = t_a2
    t_att = p.tok("vector")

    tp_prev = None
    for t in range(NT):
        for c in range(8):
            wv = [t_att, t_ld] + ([tp_prev] if tp_prev else [])
            tt = p.op("tensor", "transpose", out=pb_tp[:, c * 128:(c + 1) * 128],
                      in_=att[:, t * D + c * 128: t * D + (c + 1) * 128],
                      identity=eyes[:], waits=(wv if c == 0 else []))
        dst = bass.AP(tensor=attf, offset=t * 128, ap=[[8 * T, 128], [T, 8], [1, 128]])
        srcv = bass.AP(tensor=pb_tp, offset=0, ap=[[1024, 128], [128, 8], [1, 128]])
        tp_prev = p.op("vector", "tensor_copy", out=dst, in_=srcv, waits=[tt])
    t_attf = p.tok("vector")

    s_x = [p.dma_sem("x0"), p.dma_sem("x1b")]
    s_out = p.dma_sem("out")
    x_free = [None, None]
    g_prev = [None, None]
    out_prev = None
    for t in range(NT):
        b = t % 2
        w = [x_free[b]] if x_free[b] else []
        t_x = p.dma("sync", xs[b][:], x.rearrange("(n p) d -> n p d", p=128)[t],
                    sem=s_x[b], waits=w)
        t_rs, t_bi, _ = ln_stats(p, xs[b], sb, [t_c, t_c2], [t_x, t_c, t_c2])
        t_ap = p.op("scalar", "activation", out=ln2s[:], in_=xs[b][:], func=AF.Identity,
                    bias=sb["bi"][:, 0:1], scale=sb["rs"][:, 0:1],
                    waits=[t_rs, t_bi, t_x] + ([out_prev] if out_prev else []))
        etoks = []
        for dc in range(2):
            bank = pb_g[dc]
            wv = [t_attf, t_w] + ([g_prev[dc]] if g_prev[dc] else [])
            for kc in range(8):
                tok = (p.op if kc == 7 else p.op_nosem)(
                    "tensor", "matmul", out=bank[:],
                    lhsT=attf[:, kc * T + t * 128: kc * T + (t + 1) * 128],
                    rhs=wob[:, kc * D + dc * 512: kc * D + (dc + 1) * 512],
                    start=(kc == 0), stop=(kc == 7), waits=(wv if kc == 0 else []))
            t_e1 = p.op("vector", "tensor_add", out=x3s[:, dc * 512:(dc + 1) * 512],
                        in0=bank[:], in1=xs[b][:, dc * 512:(dc + 1) * 512],
                        waits=[tok, t_x] + ([out_prev] if out_prev else []))
            t_e2 = p.op("vector", "tensor_add", out=x3s[:, dc * 512:(dc + 1) * 512],
                        in0=x3s[:, dc * 512:(dc + 1) * 512],
                        in1=ln2s[:, dc * 512:(dc + 1) * 512], waits=[t_e1, t_ap])
            g_prev[dc] = t_e2
            etoks.append(t_e2)
        x_free[b] = etoks[-1]
        out_prev = p.dma("sync", x3o.rearrange("(n p) d -> n p d", p=128)[t],
                         x3s[:], sem=s_out, waits=etoks)
    p._waits("sync", [p.dma_all(s_out)])


def st6_body(p, g, C, T):
    nc = p.nc
    NT = T // 128
    TT6 = min(512, T)
    NTT = T // TT6
    NTG = TT6 // 128
    NZC = 128

    x = g["x3"].ap()
    rmJ = g["rmJ"]
    rupJ = g["rupJ"]
    wcT = g["wcT"]
    eye = g["eye"].ap()
    yo = g["y"].ap()

    lnf = C.sb("lnf", [128, 8 * TT6], BF16)
    lnstg = C.sb("lnstg", [128, 1024], BF16)
    zb = C.sb("zb", [128, NZC * TT6], BF16)
    rms = C.sb("rms", [16, 256 * 16], BF16)
    rupq = C.sb("rupq", [16, 64 * 64], BF16)
    m1s = C.sb("m1s", [16, 2048], BF16)
    slab = C.sb("slab", [128, 32 * 128], BF16)
    ln3fd = nc.dram_tensor("ln3fd", [D, T], BF16, kind="Internal")
    m1d = nc.dram_tensor("m1d", [16, 256 * 64], BF16, kind="Internal")
    wcb = [C.sb(f"wcb{i}", [128, 4 * D], BF16) for i in range(2)]
    eyes = C.sb("eyes", [128, 128], BF16)
    xs = [C.sb(f"xs{i}", [128, D], F32) for i in range(2)]
    lnb = C.sb("lnb", [128, D], BF16)
    ys = [C.sb("ys0", [128, D], F32)] * 2
    sb = {
        "bnr": C.sb("bnr", [128, 12], F32),
        "agg": C.sb("agg", [128, 2], F32),
        "rs": C.sb("rs", [128, 1], F32),
        "bi": C.sb("bi", [128, 1], F32),
        "cst": C.sb("cst", [128, 4], F32),
    }

    pb_z = C.ps("pb_z", [128, 2048], F32)
    tp_guard = nc.psum_tensor("g_pb_tp", [128, 1024], BF16)
    pb_tp = tp_guard.__enter__()

    t_c = p.op("vector", "memset", sb["cst"][:, 0:1], 1e-5)
    t_c2 = p.op("vector", "memset", sb["cst"][:, 1:2], 0.0)
    s_ms = p.dma_sem("misc")
    p.dma("sync", eyes[:], eye, sem=s_ms)
    t_ms = p.dma_all(s_ms)
    s_r = p.dma_sem("rails")
    p.dma("gpsimd", rms[:],
          bass.AP(tensor=rmJ, offset=0, ap=[[16, 16], [256, 256], [1, 16]]), sem=s_r)
    t_r = p.dma_all(s_r)
    s_rq = p.dma_sem("railq")
    s_m1 = p.dma_sem("m1out")

    m1_prev = None
    t_md = None
    for q4 in range(4):
        wvq = [m1_prev] if m1_prev else []
        t_rq = p.dma("gpsimd", rupq[:],
                     bass.AP(tensor=rupJ, offset=q4 * 64 * 16 * 64,
                             ap=[[64, 16], [1024, 64], [1, 64]]), sem=s_rq, waits=wvq)
        for grp in range(8):
            for u in range(8):
                hrl = grp * 8 + u
                hr = q4 * 64 + hrl
                wv = [t_r, t_rq] + ([m1_prev] if m1_prev else [])
                tm = p.op("tensor", "matmul", out=pb_z[0:16, u * 64:(u + 1) * 64],
                          lhsT=rms[0:16, hr * 16:(hr + 1) * 16],
                          rhs=rupq[0:16, hrl * 64:(hrl + 1) * 64],
                          start=True, stop=True, waits=(wv if u == 0 else []))
            wv2 = [tm] + ([t_md] if (grp % 4 == 0 and t_md) else [])
            m1_prev = p.op("vector", "tensor_copy",
                           out=m1s[0:16, (grp % 4) * 512:(grp % 4 + 1) * 512],
                           in_=pb_z[0:16, 0:512], waits=wv2)
            if grp % 4 == 3:
                t_md = p.dma("sync", m1d.ap()[0:16, (q4 * 2 + grp // 4) * 2048:
                                              (q4 * 2 + grp // 4 + 1) * 2048],
                             m1s[0:16, :], sem=s_m1, waits=[m1_prev])
    t_z0 = p.op("vector", "memset", slab[:], 0.0)
    s_sl = p.dma_sem("slab")
    for g in range(4):
        for a in range(2):
            dst = bass.AP(tensor=slab, offset=(32 * g + 16 * a) * (32 * 128) + a * 64,
                          ap=[[32 * 128, 16], [128, 32], [1, 64]])
            src = bass.AP(tensor=m1d, offset=(2 * g + a) * 64,
                          ap=[[16384, 16], [4096, 4], [512, 8], [1, 64]])
            p.dma("sync", dst, src, sem=s_sl, waits=[t_md, t_z0])
    t_sl = p.dma_all(s_sl)

    s_x = [p.dma_sem("x0"), p.dma_sem("x1b")]
    x_free = [None, None]
    tp_prev = None
    for t in range(NT):
        b = t % 2
        w = [x_free[b]] if x_free[b] else []
        t_x = p.dma("sync", xs[b][:], x.rearrange("(n p) d -> n p d", p=128)[t],
                    sem=s_x[b], waits=w)
        t_rs, t_bi, _ = ln_stats(p, xs[b], sb, [t_c, t_c2], [t_x, t_c, t_c2])
        t_ap = p.op("scalar", "activation", out=lnb[:], in_=xs[b][:], func=AF.Identity,
                    bias=sb["bi"][:, 0:1], scale=sb["rs"][:, 0:1],
                    waits=[t_rs, t_bi, t_x] + ([tp_prev] if tp_prev else []))
        x_free[b] = t_ap
        for c in range(8):
            tt = p.op("tensor", "transpose", out=pb_tp[:, c * 128:(c + 1) * 128],
                      in_=lnb[:, c * 128:(c + 1) * 128], identity=eyes[:],
                      waits=([t_ap, t_ms] + ([tp_prev] if tp_prev else []) if c == 0 else []))
        t_ev = p.op("vector", "tensor_copy", out=lnstg[:],
                    in_=bass.AP(tensor=pb_tp, offset=0, ap=[[1024, 128], [1, 1024]]),
                    waits=[tt] + ([tp_prev] if tp_prev else []))
        tp_prev = p.dma("sync", bass.AP(tensor=ln3fd, offset=t * 128,
                                        ap=[[T, 128], [128 * T, 8], [1, 128]]),
                        bass.AP(tensor=lnstg, offset=0,
                                ap=[[1024, 128], [128, 8], [1, 128]]),
                        sem=p.dma_sem("lnout"), waits=[t_ev])
    t_lnall = tp_prev
    tp_guard.__exit__(None, None, None)
    pb_c = [C.ps(f"pb_c{i}", [128, 512], F32) for i in range(4)]

    s_wc = [p.dma_sem("wc0"), p.dma_sem("wc1")]
    wc_free = [None, None]
    ge_prev = [None] * 4
    out_prev = None
    y_free = [x_free[0], x_free[1]]
    s_lnin = p.dma_sem("lnin")
    lnf_free = None
    for tt6 in range(NTT):
        wv = [t_lnall] + ([lnf_free] if lnf_free else [])
        t_li = p.dma("sync", lnf[:],
                     bass.AP(tensor=ln3fd, offset=tt6 * TT6,
                             ap=[[T, 128], [128 * T, 8], [1, TT6]]),
                     sem=s_lnin, waits=wv)
        for zg in range(32):
            ztoks = []
            for u in range(4):
                zc = zg * 4 + u
                h, rp = zc // 32, zc % 32
                g = rp % 4
                ch = rp // 4
                wv = [t_sl, t_li] + ([ge_prev[u]] if ge_prev[u] else [])
                tm = p.op("tensor", "matmul",
                          out=pb_z[:, u * 512: u * 512 + TT6],
                          lhsT=bass.AP(tensor=slab,
                                       offset=32 * g * (32 * 128) + (h * 8 + rp // 4) * 128,
                                       ap=[[32 * 128, 32], [1, 128]]),
                          rhs=lnf[32 * g:32 * (g + 1), ch * TT6:(ch + 1) * TT6],
                          start=True, stop=True, tile_position=(32 * g, 0), waits=wv)
                ztoks.append(tm)
            t_ge = p.op("scalar", "activation",
                        out=bass.AP(tensor=zb, offset=(zg * 4) * TT6,
                                    ap=[[NZC * TT6, 128], [TT6, 4], [1, TT6]]),
                        in_=bass.AP(tensor=pb_z, offset=0,
                                    ap=[[2048, 128], [512, 4], [1, TT6]]),
                        func=AF.Gelu,
                        bias=sb["cst"][:, 1:2], scale=1.0, waits=ztoks + [t_c2])
            for u in range(4):
                ge_prev[u] = t_ge
        t_zall = p.tok("scalar")
        lnf_free = t_zall
        banks = [pb_z[:, i * 512:(i + 1) * 512] for i in range(4)] + [b[:] for b in pb_c]
        mm_last = [None] * 8
        for zq in range(32):
            bw = zq % 2
            wv = [wc_free[bw]] if wc_free[bw] else []
            t_wc = p.dma("gpsimd", wcb[bw][:],
                         bass.AP(tensor=wcT, offset=zq * 4 * 128 * D,
                                 ap=[[D, 128], [128 * D, 4], [1, D]]),
                         sem=s_wc[bw], waits=wv)
            for u in range(4):
                zc = zq * 4 + u
                for tg in range(NTG):
                    for dc in range(2):
                        bi_ = tg * 2 + dc
                        wv2 = [t_wc, t_zall]
                        if zc == 0 and out_prev is not None:
                            wv2.append(out_prev)
                        is_grp_last = (u == 3 and tg == NTG - 1 and dc == 1)
                        tok = (p.op if (zc >= NZC - 4 or is_grp_last) else p.op_nosem)(
                            "tensor", "matmul", out=banks[bi_],
                            lhsT=zb[:, zc * TT6 + tg * 128: zc * TT6 + (tg + 1) * 128],
                            rhs=wcb[bw][:, u * D + dc * 512: u * D + (dc + 1) * 512],
                            start=(zc == 0), stop=(zc == NZC - 1),
                            waits=(wv2 if (u == 0 and tg == 0 and dc == 0) else []))
                        if zc == NZC - 1:
                            mm_last[bi_] = tok
                        if is_grp_last:
                            wc_free[bw] = tok
        for tg in range(NTG):
            gt = tt6 * NTG + tg
            b = gt % 2
            w = [y_free[b]] if y_free[b] else []
            t_x = p.dma("sync", xs[b][:], x.rearrange("(n p) d -> n p d", p=128)[gt],
                        sem=s_x[b], waits=w)
            etoks = []
            for dc in range(2):
                te = p.op("vector", "tensor_add", out=ys[b][:, dc * 512:(dc + 1) * 512],
                          in0=banks[tg * 2 + dc], in1=xs[b][:, dc * 512:(dc + 1) * 512],
                          waits=[mm_last[tg * 2 + dc], t_x] + ([out_prev] if out_prev else []))
                etoks.append(te)
            y_free[b] = etoks[-1]
            out_prev = p.dma("sync", yo.rearrange("(n p) d -> n p d", p=128)[gt],
                             ys[b][:], sem=p.dma_sem("out"), waits=etoks)
    p._waits("sync", [p.dma_all(p.dma_sem("out"))])


def build_fused(T):
    nc = bass.Bass("TRN2", target_bir_lowering=False)
    p = Prog(nc)
    NT = T // 128
    NCH = T // 64

    def EI(n, s, d=F32):
        return nc.dram_tensor(n, s, d, kind="ExternalInput")

    def IN(n, s, d=F32):
        return nc.dram_tensor(n, s, d, kind="Internal")

    g = {
        "xin": EI("x", [T, D]),
        "hmask": EI("hmask", [1, 1]),
        "wqkvT": EI("wqkvT", [D, 3 * D]),
        "bq_row": EI("bq_row", [1, 3 * D]),
        "bq_cols": EI("bq_cols", [128, 24]),
        "pmT": EI("pmT", [64, 64]),
        "eye": EI("eye", [128, 128], BF16),
        "wprojT": EI("wprojT", [D, D]),
        "bp_row": EI("bp_row", [1, D]),
        "wupT": EI("wupT", [D, DFF]),
        "wdnT": EI("wdnT", [DFF, D]),
        "wmT": EI("wmT", [D, 1072]),
        "alog": EI("alog", [1, H]),
        "dtb": EI("dtb", [1, H]),
        "maskT": EI("maskT", [64, 64]),
        "woutT": EI("woutT", [D, D]),
        "rmJ": EI("rmJ", [256 * 16, 16]),
        "rupJ": EI("rupJ", [256 * 16, 64]),
        "wcT": EI("wcT", [16 * D, D]),
        "qef": IN("qef", [D, T], BF16),
        "lnfx": IN("lnfx", [128, 8 * (T + 1)], BF16),
        "vbet": IN("vbet", [128, NT * D], BF16),
        "wkdw": IN("wkdw", [128, NT * D], BF16),
        "kvpo": IN("kvpo", [128, 8 * 65]),
        "kvsum": IN("kvsum", [128, 8 * 65]),
        "x1": IN("x1", [T, D]),
        "x2": IN("x2", [T, D]),
        "xhg": IN("xhg", [2, D]),
        "gro": IN("gro", [128, NT * 48]),
        "edf": IN("edf", [128, NT * H]),
        "ln2f": IN("ln2f", [D, T], BF16),
        "dcd": IN("dcd", [H, T]),
        "o_local": IN("o_local", [T, D]),
        "sfin": IN("sfin", [128, 8 * 64]),
        "sfg": IN("sfg", [256, 8 * 64]),
        "x3": IN("x3", [T, D]),
        "y": nc.dram_tensor("y", [T, D], F32, kind="ExternalOutput"),
    }

    GRP = [[0, 1], [2, 3], [4, 5], [6, 7]]

    C = StageCtx(nc, "a_"); st1_body(p, g, C, T); C.close()
    p.barrier()
    p.collective("AllReduce", ALU.add, GRP,
                 ins=[g["kvpo"].ap().opt()], outs=[g["kvsum"].ap().opt()])
    p.barrier()
    C = StageCtx(nc, "b_"); st2_body(p, g, C, T); C.close()
    p.barrier()
    C = StageCtx(nc, "c_"); st3_body(p, g, C, T); C.close()
    p.barrier()
    p.collective("AllGather", ALU.bypass, GRP,
                 ins=[g["x2"].ap()[T - 1:T, :].opt()], outs=[g["xhg"].ap().opt()])
    p.barrier()
    C4 = StageCtx(nc, "s4_")
    g["sh_dcumt"] = C4.sb("dcumt", [128, NT * H])
    g["sh_dext"] = C4.sb("dext", [128, NT * H])
    g["sh_gcols"] = C4.sb("gcols", [128, 8 * NCH])
    C = StageCtx(nc, "d_"); st4a_body(p, g, C, T); C.close()
    p.barrier()
    C = StageCtx(nc, "e_"); st4b_body(p, g, C, T); C.close()
    C4.close()
    p.barrier()
    p.collective("AllGather", ALU.bypass, GRP,
                 ins=[g["sfin"].ap().opt()], outs=[g["sfg"].ap().opt()])
    p.barrier()
    C = StageCtx(nc, "f_"); st5_body(p, g, C, T); C.close()
    p.barrier()
    C = StageCtx(nc, "g_"); st6_body(p, g, C, T); C.close()
    return nc


def _run(nc, in_maps):
    return run_bass_kernel_spmd(nc, in_maps, core_ids=list(range(NCORE))).results


def kernel(L0_attn_x, L0_attn_W_qkv, L0_attn_b_qkv, L0_attn_W_proj, L0_attn_b_proj,
           L0_attn_proj_mat, L0_ffn_w_up, L0_ffn_w_down, L1_attn_W_write,
           L1_attn_W_gate, L1_attn_W_beta, L1_attn_W_alpha, L1_attn_W_out,
           L1_attn_A_log, L1_attn_dt_bias, L1_ffn_rail_maker, L1_ffn_rail_up_proj,
           L1_ffn_w_cproj):
    T = 2048
    xs = [np.ascontiguousarray(s) for s in
          np.asarray(L0_attn_x, dtype=np.float32).reshape(NCORE, T, D)]

    nc = build_fused(T)
    b_qkv = np.asarray(L0_attn_b_qkv)
    wm = np.concatenate([np.asarray(L1_attn_W_write).T, np.asarray(L1_attn_W_gate).T,
                         np.asarray(L1_attn_W_beta).T, np.asarray(L1_attn_W_alpha).T],
                        axis=1)
    base = {
        "wqkvT": np.ascontiguousarray(np.asarray(L0_attn_W_qkv).T),
        "bq_row": np.ascontiguousarray(b_qkv.reshape(1, -1)),
        "bq_cols": np.ascontiguousarray(b_qkv.reshape(24, 128).T),
        "pmT": np.ascontiguousarray(np.asarray(L0_attn_proj_mat).T),
        "eye": np.eye(128, dtype=np.float32).astype(ml_dtypes.bfloat16),
        "wprojT": np.ascontiguousarray(np.asarray(L0_attn_W_proj).T),
        "bp_row": np.ascontiguousarray(np.asarray(L0_attn_b_proj).reshape(1, -1)),
        "wupT": np.ascontiguousarray(np.asarray(L0_ffn_w_up).T),
        "wdnT": np.ascontiguousarray(np.asarray(L0_ffn_w_down).T),
        "wmT": np.ascontiguousarray(wm),
        "alog": np.ascontiguousarray(np.asarray(L1_attn_A_log).reshape(1, -1)),
        "dtb": np.ascontiguousarray(np.asarray(L1_attn_dt_bias).reshape(1, -1)),
        "maskT": np.ascontiguousarray(
            np.where(np.triu(np.ones((64, 64), bool), 1), 0.0, -1e30)
            .astype(np.float32)),
        "woutT": np.ascontiguousarray(np.asarray(L1_attn_W_out).T),
        "rmJ": np.ascontiguousarray(
            np.asarray(L1_ffn_rail_maker).transpose(0, 1, 3, 2).reshape(256 * 16, 16)),
        "rupJ": np.ascontiguousarray(
            np.asarray(L1_ffn_rail_up_proj).transpose(0, 1, 3, 2)
            .reshape(256, 16, 64).reshape(256 * 16, 64)),
        "wcT": np.ascontiguousarray(np.asarray(L1_ffn_w_cproj).T),
    }
    maps = []
    for c in range(NCORE):
        m = dict(base)
        m["x"] = xs[c]
        m["hmask"] = np.array([[1.0 if c % 2 else 0.0]], np.float32)
        maps.append(m)
    r = _run(nc, maps)
    y = np.concatenate([np.asarray(r[c]["y"]) for c in range(NCORE)], axis=0)
    return y.reshape(4, 4096, D)
